# version 07
# baseline (speedup 1.0000x reference)
import numpy as np
from contextlib import ExitStack

import concourse.bass as bass
import concourse.mybir as mybir
from concourse.bass_utils import run_bass_kernel_spmd

F32 = mybir.dt.float32
BF16 = mybir.dt.bfloat16
I32 = mybir.dt.int32
AF = mybir.ActivationFunctionType
ALU = mybir.AluOpType
AX = mybir.AxisListType

TWO_PI = float(2 * np.pi)
NEG_BIG = -1.0e30
LN8 = float(np.log(8.0))

STREAMS = ("pe", "act", "dve", "gp", "sp")


class Buf:
    __slots__ = ("name", "w", "readers")

    def __init__(self, name):
        self.name = name
        self.w = None
        self.readers = {}

    def __repr__(self):
        return f"Buf({self.name})"


NDMASEM = 12


class Sched:
    def __init__(self, nc, es):
        self.nc = nc
        self.streams = {s: [] for s in STREAMS}
        self.sems = {}
        for s in ("pe", "act", "dve", "gp", "sp"):
            self.sems[s] = es.enter_context(nc.semaphore(f"sem_{s}"))
        for q in ("gpd", "spd"):
            for j in range(NDMASEM):
                self.sems[f"{q}{j}"] = es.enter_context(nc.semaphore(f"sem_{q}{j}"))
        self.counts = {s: 0 for s in self.sems}
        self.dma_rot = {"gp": 0, "sp": 0}
        self.waited = {s: {} for s in STREAMS}
        self.nwaits = 0
        self.ninstr = 0

    def _collect_waits(self, stream, reads, writes):
        want = {}

        def add(tok):
            if tok is None:
                return
            s, v = tok
            if want.get(s, -1) < v:
                want[s] = v

        for b in reads:
            add(b.w)
        for b in writes:
            add(b.w)
            for s, v in b.readers.items():
                if want.get(s, -1) < v:
                    want[s] = v
        waits = []
        for s, v in sorted(want.items()):
            if s == "pe" and stream == "pe":
                continue
            if self.waited[stream].get(s, 0) >= v:
                continue
            waits.append((s, v))
            self.waited[stream][s] = v
        return waits

    def emit(self, stream, fn, reads=(), writes=(), dma=False):
        return self.emit_group(stream, [fn], reads=reads, writes=writes, dma=dma)

    def emit_group(self, stream, fns, reads=(), writes=(), dma=False):
        waits = self._collect_waits(stream, reads, writes)
        if dma:
            q = {"gp": "gpd", "sp": "spd"}[stream]
            j = self.dma_rot[stream]
            self.dma_rot[stream] = (j + 1) % NDMASEM
            semname = f"{q}{j}"
            prev = self.counts[semname]
            if prev > 0 and self.waited[stream].get(semname, 0) < prev:
                waits.append((semname, prev))
                self.waited[stream][semname] = prev
            inc = 16
        else:
            semname = stream
            inc = 1
        sem = self.sems[semname]
        self.counts[semname] += inc
        tok = (semname, self.counts[semname])
        semobjs = self.sems
        self.ninstr += len(fns) + len(waits)

        def closure(eng, waits=waits, fns=fns, sem=sem, inc=inc):
            for s, v in waits:
                eng.wait_ge(semobjs[s], v)
            for f in fns[:-1]:
                f(eng)
            ins = fns[-1](eng)
            ins.then_inc(sem, inc)
            return ins

        self.streams[stream].append(closure)
        self.nwaits += len(waits)
        for b in writes:
            b.w = tok
            b.readers = {}
        for b in reads:
            if b.readers.get(semname, 0) < tok[1]:
                b.readers[semname] = tok[1]
        return tok

    def link(self, dsts, srcs):
        merged = {}
        for b in srcs:
            if b.w is not None:
                s, v = b.w
                if merged.get(s, 0) < v:
                    merged[s] = v
            for s, v in b.readers.items():
                if merged.get(s, 0) < v:
                    merged[s] = v
        for b in dsts:
            for s, v in merged.items():
                if b.readers.get(s, 0) < v:
                    b.readers[s] = v

    def wait_all_done(self):
        finals = [(s, self.counts[s]) for s in self.counts if self.counts[s] > 0]
        semobjs = self.sems

        def closure(eng):
            for s, v in finals:
                eng.wait_ge(semobjs[s], v)
        self.streams["sp"].append(closure)

    def flush(self, block):
        st = self.streams
        block.tensor(lambda e: [f(e) for f in st["pe"]])
        block.scalar(lambda e: [f(e) for f in st["act"]])
        block.vector(lambda e: [f(e) for f in st["dve"]])
        block.gpsimd(lambda e: [f(e) for f in st["gp"]])
        block.sync(lambda e: [f(e) for f in st["sp"]])


class Cfg:
    def __init__(self, T=2048, depth=6, n_seq=2, phases="faf"):
        self.T = T
        self.depth = depth
        self.n_seq = n_seq
        self.phases = phases
        self.d = 1024
        self.ncnk = 8
        self.nheads = 16
        self.nhp = 8
        self.TT = 512
        self.ntt = T // self.TT
        self.nsub = T // 128


def build(nc, cfg: Cfg):
    es = ExitStack()
    S = Sched(nc, es)
    T, TT, ntt = cfg.T, cfg.TT, cfg.ntt
    d, ncnk, nhp, nheads = cfg.d, cfg.ncnk, cfg.nhp, cfg.nheads
    DEP = 6
    NSEQ = cfg.n_seq

    dx = nc.dram_tensor("x", [NSEQ, ncnk, 128, T], F32, kind="ExternalInput").ap()
    dout = nc.dram_tensor("out", [NSEQ, ncnk, 128, T], F32, kind="ExternalOutput").ap()
    dwqkv = nc.dram_tensor("wqkv", [DEP, d, 3 * d], F32, kind="ExternalInput").ap()
    dwo = nc.dram_tensor("wo", [DEP, d, d], F32, kind="ExternalInput").ap()
    dw1 = nc.dram_tensor("w1p", [DEP, d, 5632], F32, kind="ExternalInput").ap()
    dw2 = nc.dram_tensor("w2p", [DEP, 2816, d], F32, kind="ExternalInput").ap()
    dgh = nc.dram_tensor("g_h", [128, DEP, ncnk], F32, kind="ExternalInput").ap()
    dbh = nc.dram_tensor("b_h", [128, DEP, ncnk], F32, kind="ExternalInput").ap()
    db1a = nc.dram_tensor("b1a_h", [128, DEP, 22], F32, kind="ExternalInput").ap()
    db1g = nc.dram_tensor("b1g_h", [128, DEP, 22], F32, kind="ExternalInput").ap()
    db2 = nc.dram_tensor("b2_h", [128, DEP, ncnk], F32, kind="ExternalInput").ap()
    dqsm = nc.dram_tensor("qsm", [128, DEP], F32, kind="ExternalInput").ap()
    dqss = nc.dram_tensor("qss", [128, DEP], F32, kind="ExternalInput").ap()
    dksm = nc.dram_tensor("ksm", [128, DEP], F32, kind="ExternalInput").ap()
    dkss = nc.dram_tensor("kss", [128, DEP], F32, kind="ExternalInput").ap()
    dwq2 = nc.dram_tensor("wq2", [d, d], F32, kind="ExternalInput").ap()
    dwk2 = nc.dram_tensor("wk2", [d, d], F32, kind="ExternalInput").ap()
    dwv2 = nc.dram_tensor("wv2", [d, d], F32, kind="ExternalInput").ap()
    dwpj = nc.dram_tensor("wpj", [d, d], F32, kind="ExternalInput").ap()
    dbpj = nc.dram_tensor("bpj_h", [128, ncnk], F32, kind="ExternalInput").ap()
    dx1 = nc.dram_tensor("x1s", [NSEQ, ncnk, 128, T], F32, kind="Internal").ap()
    dx2 = nc.dram_tensor("x2s", [NSEQ, ncnk, 128, T], F32, kind="Internal").ap()
    b_dx = Buf("dx"); b_dx1 = Buf("dx1"); b_dx2 = Buf("dx2"); b_dout = Buf("dout")
    DBUF = {id(dx): b_dx, id(dx1): b_dx1, id(dx2): b_dx2, id(dout): b_dout}

    sbuf = lambda name, shape, dt=F32: es.enter_context(nc.sbuf_tensor(name, shape, dt))

    t_gh = sbuf("t_gh", [128, DEP, ncnk]); b_gh = Buf("gh")
    t_bh = sbuf("t_bh", [128, DEP, ncnk]); b_bh = Buf("bh")
    t_b1a = sbuf("t_b1a", [128, DEP, 22]); b_b1a = Buf("b1a")
    t_b1g = sbuf("t_b1g", [128, DEP, 22]); b_b1g = Buf("b1g")
    t_b2 = sbuf("t_b2", [128, DEP, ncnk]); b_b2 = Buf("b2")
    t_qsm = sbuf("t_qsm", [128, DEP]); b_qsm = Buf("qsm")
    t_qss = sbuf("t_qss", [128, DEP]); _ = None
    t_ksm = sbuf("t_ksm", [128, DEP])
    t_kss = sbuf("t_kss", [128, DEP])
    t_bpj = sbuf("t_bpj", [128, ncnk]); b_bpj = Buf("bpj")
    for tens, dram, bf in ((t_gh, dgh, b_gh), (t_bh, dbh, b_bh), (t_b1a, db1a, b_b1a),
                           (t_b1g, db1g, b_b1g), (t_b2, db2, b_b2), (t_qsm, dqsm, b_qsm),
                           (t_qss, dqss, b_qsm), (t_ksm, dksm, b_qsm), (t_kss, dkss, b_qsm),
                           (t_bpj, dbpj, b_bpj)):
        S.emit("sp", (lambda tens=tens, dram=dram: (lambda e: e.dma_start(out=tens[:], in_=dram[:])))(),
               writes=[bf], dma=True)

    t_pidx = sbuf("t_pidx", [128, 1])
    t_cst = sbuf("t_cst", [128, 2])
    b_cst = Buf("cst")
    nc.const_aps.register(F32, 1e-5, t_cst[:, 0:1])
    nc.const_aps.register(F32, LN8, t_cst[:, 1:2])
    t_I128 = sbuf("t_I128", [128, 128], BF16)
    t_Psig = sbuf("t_Psig", [128, 128], BF16)
    t_mTup = sbuf("t_mTup", [128, 2, 128], BF16)
    t_ones1 = sbuf("t_ones1", [1, 128], BF16)
    t_onesc = sbuf("t_onesc", [128, 1], BF16)
    t_Bhd = sbuf("t_Bhd", [16, nhp, 128], BF16)
    t_Ohp = sbuf("t_Ohp", [128, nhp, 16], BF16)
    t_sm1 = sbuf("t_sm1", [128, 1]); t_sm2 = sbuf("t_sm2", [128, 1])
    t_sm3 = sbuf("t_sm3", [128, 1]); t_sm4 = sbuf("t_sm4", [128, 1])
    t_den = sbuf("t_den", [128, 16])
    b_den0 = Buf("den0"); b_den1 = Buf("den1"); b_den = b_den0
    t_mx = sbuf("t_mx", [128, 4])
    b_mx0 = Buf("mx0"); b_mx1 = Buf("mx1"); b_mx = b_mx0
    t_bias = sbuf("t_bias", [128, 1]); b_bias = Buf("bias")
    t_inv = sbuf("t_inv", [16, 2 * TT], BF16); b_inv = Buf("inv")
    b_const = Buf("const")
    b_tab = Buf("tab")

    AR1 = sbuf("AR1", [128, 16392])
    AR2 = sbuf("AR2", [128, 16896])
    AR3 = sbuf("AR3", [128, 13824])
    AR4 = sbuf("AR4", [128, 4096])

    def view(arena, off, n, shape, dt=F32):
        ap = arena[:][:, off:off + n]
        if dt == BF16:
            ap = ap.bitcast(BF16)
        elif dt == I32:
            ap = ap.bitcast(I32) if hasattr(ap, "bitcast") else ap
        if len(shape) == 2:
            if shape[0] != 128:
                ap = ap[0:shape[0], :]
        elif len(shape) == 3:
            ap = ap.rearrange("p (a b) -> p a b", a=shape[1], b=shape[2])
            if shape[0] != 128:
                ap = ap[0:shape[0], :, :]
        return ap

    y = view(AR1, 0, ncnk * (T + 1), [128, ncnk, T + 1]); b_y = Buf("y")
    o_qf = view(AR1, 0, 4 * T, [128, nhp, T], BF16)
    o_kf = view(AR1, 4 * T, 4 * T, [128, nhp, T], BF16)
    b_QF = Buf("R_qf"); b_KF = Buf("R_kf")
    YBUFS = [b_y, b_QF, b_KF]

    wq = view(AR2, 0, 4096, [128, ncnk, d], BF16); b_wq = Buf("R_wq")
    wk = view(AR2, 4096, 4096, [128, ncnk, d], BF16); b_wk = Buf("R_wk")
    wv = view(AR2, 8192, 4096, [128, ncnk, d], BF16); b_wv = Buf("R_wv")
    wo = view(AR2, 12288, 4096, [128, ncnk, d], BF16); b_wo = Buf("R_wo")
    w1v = view(AR2, 0, 11264, [128, ncnk, 2816], BF16)
    w2v = view(AR2, 11264, 5632, [128, 11, 1024], BF16)
    W1BUFS = [b_wq, b_wk, b_wv]
    W2BUFS = [b_wv, b_wo]

    xs = view(AR3, 0, 2048, [128, ncnk, TT], BF16); b_XS = Buf("R_xs")
    qp = view(AR3, 0, 2048, [128, ncnk, TT], BF16)
    hv_lo = view(AR3, 0, 2048, [128, ncnk, TT], BF16)
    qb = view(AR3, 2048, 2048, [128, ncnk, TT], BF16); b_QB = Buf("R_qb")
    v1 = view(AR3, 2048, 512, [128, TT])
    stv = view(AR3, 2560, 1024, [1, 2, TT])
    hv_hi = view(AR3, 2048, 768, [128, 3, TT], BF16)
    gel = view(AR3, 2816, 256, [128, TT], BF16)
    kr = view(AR3, 4096, 3072, [128, ncnk, 768], BF16); b_KR = Buf("R_kr")
    xf = view(AR3, 4096, 1536, [128, 3, TT])
    vr = view(AR3, 7168, 3072, [128, 6, d], BF16); b_VR = Buf("R_vr")
    ao = view(AR3, 10240, 2048, [128, ncnk, TT], BF16); b_AO = Buf("R_ao")
    xnf = view(AR3, 4096, ncnk * T // 2, [128, ncnk, T], BF16)
    ptA = view(AR3, 12288, 192, [128, 384], BF16); b_ptA = Buf("R_ptA")
    ptB = view(AR3, 12480, 192, [128, 384], BF16); b_ptB = Buf("R_ptB")
    ptTA = view(AR3, 12672, 192, [128, 3, 128], BF16); b_ptTA = Buf("R_ptTA")
    ptTB = view(AR3, 12864, 192, [128, 3, 128], BF16); b_ptTB = Buf("R_ptTB")
    tt1 = view(AR3, 13056, 256, [128, TT], BF16); b_T1 = Buf("R_t1")
    tu2 = view(AR3, 13312, 256, [128, TT], BF16); b_U2 = Buf("R_u2")
    sq = view(AR3, 13568, 256, [128, TT], BF16); b_SQ = Buf("R_sq")
    stb = view(AR3, 3584, 512, [1, 2, TT], BF16); b_STB = b_QB
    o_xnf = view(AR3, 0, ncnk * T // 2, [128, ncnk, T], BF16)
    o_vfa = view(AR3, 8192, 5632, [128, 11, d], BF16)
    O_XNF = [b_XS, b_QB, b_KR]
    O_XNF2 = [b_XS, b_QB, b_KR, b_VR]
    O_VFA = [b_VR, b_AO, b_ptA, b_ptB, b_ptTA, b_ptTB, b_T1, b_U2, b_SQ]
    o_xf = view(AR3, 8192, 1536, [128, 3, TT])
    o_v1 = view(AR3, 11776, 512, [128, TT])
    o_stv = view(AR3, 10240, 1024, [1, 2, TT])
    o_stb = view(AR3, 12288, 512, [1, 2, TT], BF16)
    o_sq = view(AR3, 13568, 256, [128, TT], BF16)
    o_ptT = view(AR3, 2048, 256, [128, 4, 128], BF16)

    tb_csS = view(AR4, 0, 1024, [128, 2048], BF16)
    tb_snS = view(AR4, 1024, 1024, [128, 2048], BF16)
    tb_csI = view(AR4, 2048, 1024, [128, 2048], BF16)
    tb_snI = view(AR4, 3072, 1024, [128, 2048], BF16)
    b_A4 = Buf("R_a4")
    o_vfb = view(AR4, 0, 2560, [128, 5, d], BF16)
    o_aot = view(AR4, 2560, 512, [128, ncnk, 128], BF16)
    o_ptf = view(AR4, 3072, T // 2, [128, T], BF16)
    b_VFB = b_A4; b_AOT = b_A4; b_PTF = b_A4

    def build_constants():
        g = "gp"
        t_colid = view(AR3, 512, 128, [128, 128])
        S.emit(g, lambda e: e.memset(t_cst[:, 0:1], 1e-5), writes=[b_const])
        S.emit(g, lambda e: e.memset(t_cst[:, 1:2], LN8), writes=[b_const])
        S.emit(g, lambda e: e.iota(t_pidx[:], pattern=[[0, 1]], base=0,
                                   channel_multiplier=1, allow_small_or_imprecise_dtypes=True),
               writes=[b_const])
        S.emit(g, lambda e: e.iota(t_colid[:], pattern=[[1, 128]], base=0,
                                   channel_multiplier=0, allow_small_or_imprecise_dtypes=True),
               writes=[b_const])
        S.emit("dve", lambda e: e.tensor_single_scalar(t_I128[:], t_colid[:], t_pidx[:, 0:1], ALU.is_equal),
               reads=[b_const], writes=[b_const])
        S.emit("dve", lambda e: e.tensor_single_scalar(t_sm1[:], t_pidx[:], 63.5, ALU.is_gt),
               reads=[b_const], writes=[b_const])
        S.emit("dve", lambda e: e.scalar_tensor_tensor(t_sm2[:], t_sm1[:], -64.0, t_pidx[:], ALU.mult, ALU.add),
               reads=[b_const], writes=[b_const])
        S.emit("dve", lambda e: e.tensor_single_scalar(t_sm3[:], t_sm2[:], 31.5, ALU.is_gt),
               reads=[b_const], writes=[b_const])
        S.emit("dve", lambda e: e.scalar_tensor_tensor(t_sm4[:], t_sm3[:], -64.0, t_pidx[:], ALU.mult, ALU.add),
               reads=[b_const], writes=[b_const])
        S.emit("dve", lambda e: e.tensor_scalar_add(t_sm4[:], t_sm4[:], 32.0),
               reads=[b_const], writes=[b_const])
        S.emit("dve", lambda e: e.tensor_single_scalar(t_Psig[:], t_colid[:], t_sm4[:, 0:1], ALU.is_equal),
               reads=[b_const], writes=[b_const])
        scr = view(AR3, 0, 128, [128, 128])
        S.emit("dve", lambda e: e.tensor_single_scalar(scr[:], t_colid[:], t_pidx[:, 0:1], ALU.is_lt),
               reads=[b_const], writes=[b_XS])
        S.emit("dve", lambda e: e.tensor_scalar_mul(t_mTup[:, 0, :], scr[:], NEG_BIG),
               reads=[b_XS], writes=[b_const])
        S.emit("dve", lambda e: e.tensor_single_scalar(scr[:], t_colid[:], t_pidx[:, 0:1], ALU.is_gt),
               reads=[b_const], writes=[b_XS])
        S.emit("dve", lambda e: e.tensor_scalar_mul(t_mTup[:, 1, :], scr[:], NEG_BIG),
               reads=[b_XS], writes=[b_const])
        S.emit(g, lambda e: e.memset(t_ones1[:], 1.0), writes=[b_const])
        S.emit(g, lambda e: e.memset(t_onesc[:], 1.0), writes=[b_const])
        scr2 = view(AR3, 128, 128, [128, 128])
        S.emit("dve", lambda e: e.tensor_single_scalar(scr[:], t_colid[:], 63.5, ALU.is_gt),
               reads=[b_const], writes=[b_XS])
        for hp in range(nhp):
            S.emit("dve", (lambda hp=hp: (lambda e: e.tensor_scalar_add(scr2[:], scr[:], float(2 * hp))))(),
                   reads=[b_XS], writes=[b_XS])
            S.emit("dve", (lambda hp=hp: (lambda e: e.tensor_single_scalar(
                t_Bhd[:, hp, :], scr2[0:16, :], t_pidx[0:16, 0:1], ALU.is_equal)))(),
                   reads=[b_XS, b_const], writes=[b_const])
        scr3 = view(AR3, 256, 16, [128, 16])
        S.emit(g, lambda e: e.iota(scr3[:], pattern=[[1, 16]], base=0,
                                   channel_multiplier=0, allow_small_or_imprecise_dtypes=True),
               writes=[b_XS])
        S.emit("dve", lambda e: e.tensor_single_scalar(
            scr2[:, 0:16], scr3[:], t_sm1[:, 0:1], ALU.subtract),
               reads=[b_XS, b_const], writes=[b_XS])
        for hp in range(nhp):
            S.emit("dve", (lambda hp=hp: (lambda e: e.tensor_single_scalar(
                t_Ohp[:, hp, :], scr2[:, 0:16], float(2 * hp), ALU.is_equal)))(),
                   reads=[b_XS], writes=[b_const])
        S.emit("dve", lambda e: e.tensor_scalar_mul(t_sm3[:], t_sm3[:], 2.0), reads=[b_const], writes=[b_const])
        S.emit("dve", lambda e: e.tensor_scalar_add(t_sm3[:], t_sm3[:], -1.0), reads=[b_const], writes=[b_const])
        S.emit("dve", lambda e: e.tensor_single_scalar(t_sm1[:], t_sm2[:], 31.5, ALU.is_gt),
               reads=[b_const], writes=[b_const])
        S.emit("dve", lambda e: e.scalar_tensor_tensor(t_sm4[:], t_sm1[:], -32.0, t_sm2[:], ALU.mult, ALU.add),
               reads=[b_const], writes=[b_const])
        S.emit("act", lambda e: e.activation(t_sm1[:], t_sm4[:], AF.Exp, scale=float(-np.log(10000.0) / 32.0)),
               reads=[b_const], writes=[b_const])
        S.emit("dve", lambda e: e.tensor_scalar_mul(t_sm2[:], t_sm4[:], 2.0 / 89.6), reads=[b_const], writes=[b_const])
        S.emit("dve", lambda e: e.tensor_scalar_add(t_sm2[:], t_sm2[:], 25.6 / 89.6), reads=[b_const], writes=[b_const])
        S.emit("act", lambda e: e.activation(t_sm2[:], t_sm2[:], AF.Ln), reads=[b_const], writes=[b_const])

    def build_tables():
        s0 = view(AR3, 0, 2048, [128, 2048])
        s1 = view(AR3, 2048, 2048, [128, 2048])
        si32 = view(AR3, 4096, 2048, [128, 2048], I32)
        sifl = view(AR3, 4096, 2048, [128, 2048])
        scl = view(AR3, 6144, 2048, [128, 2048])
        SCR = [b_XS, b_QB, b_KR, b_VR]
        g = "gp"
        S.emit(g, lambda e: e.iota(s0, pattern=[[1, 2048]], base=0,
                                   channel_multiplier=0, allow_small_or_imprecise_dtypes=True),
               writes=SCR)
        S.emit("dve", lambda e: e.tensor_single_scalar(s0, s0, t_sm1[:, 0:1], ALU.mult),
               reads=SCR + [b_const], writes=SCR)

        def sincos(dst, shift):
            S.emit("dve", lambda e: e.tensor_scalar_mul(s1, s0, 1.0 / TWO_PI), reads=SCR, writes=SCR)
            if shift:
                S.emit("dve", lambda e: e.tensor_scalar_add(s1, s1, 0.25), reads=SCR, writes=SCR)
            S.emit("dve", lambda e: e.tensor_copy(si32, s1), reads=SCR, writes=SCR)
            S.emit("dve", lambda e: e.tensor_copy(sifl, si32), reads=SCR, writes=SCR)
            S.emit("dve", lambda e: e.tensor_sub(s1, s1, sifl), reads=SCR, writes=SCR)
            S.emit("dve", lambda e: e.tensor_single_scalar(sifl, s1, 0.5, ALU.is_gt), reads=SCR, writes=SCR)
            S.emit("dve", lambda e: e.tensor_sub(s1, s1, sifl), reads=SCR, writes=SCR)
            S.emit("act", (lambda dst=dst: (lambda e: e.activation(dst, s1, AF.Sin, scale=TWO_PI)))(),
                   reads=SCR, writes=[b_A4])

        sincos(tb_snS, False)
        sincos(tb_csS, True)
        S.emit(g, lambda e: e.iota(sifl, pattern=[[1, 2048]], base=-1024,
                                   channel_multiplier=0, allow_small_or_imprecise_dtypes=True),
               writes=SCR)
        S.emit("dve", lambda e: e.tensor_scalar_mul(s0, sifl, 1.0 / 128.0), reads=SCR, writes=SCR)
        S.emit("dve", lambda e: e.tensor_single_scalar(s0, s0, t_sm2[:, 0:1], ALU.mult),
               reads=SCR + [b_const], writes=SCR)
        S.emit("act", lambda e: e.activation(scl, s0, AF.Exp), reads=SCR, writes=SCR)
        S.emit("act", lambda e: e.activation(s1, s0, AF.Exp, scale=-1.0), reads=SCR, writes=SCR)
        S.emit("dve", lambda e: e.tensor_mul(tb_snI, tb_snS, s1), reads=SCR + [b_A4], writes=[b_A4])
        S.emit("dve", lambda e: e.tensor_single_scalar(tb_snI, tb_snI, t_sm3[:, 0:1], ALU.mult),
               reads=[b_A4, b_const], writes=[b_A4])
        S.emit("dve", lambda e: e.tensor_mul(tb_csI, tb_csS, s1), reads=SCR + [b_A4], writes=[b_A4])
        S.emit("dve", lambda e: e.tensor_mul(tb_snS, tb_snS, scl), reads=SCR + [b_A4], writes=[b_A4])
        S.emit("dve", lambda e: e.tensor_single_scalar(tb_snS, tb_snS, t_sm3[:, 0:1], ALU.mult),
               reads=[b_A4, b_const], writes=[b_A4])
        S.emit("dve", lambda e: e.tensor_mul(tb_csS, tb_csS, scl), reads=SCR + [b_A4], writes=[b_A4])

    build_constants()

    def ln_stats(xap, RD, sqv, b_sqv, stvv, b_stvv, stbv, b_stbv,
                 p_s, bp_s, p_r, bp_r, p_m, bp_m):
        fns = []
        for c in range(ncnk):
            fns.append((lambda c=c: (lambda e: e.matmul(p_s[0:1, 0:TT], t_onesc[:], xap[:, c, :],
                                                        start=(c == 0), stop=(c == ncnk - 1))))())
        S.emit_group("pe", fns, reads=RD + [b_const], writes=[bp_s])
        for c in range(ncnk):
            S.emit("act", (lambda c=c: (lambda e: e.activation(sqv[:], xap[:, c, :], AF.Square)))(),
                   reads=RD, writes=[b_sqv])
            S.emit("pe", (lambda c=c: (lambda e: e.matmul(p_s[32:33, 0:TT], t_onesc[:], sqv[:],
                                                          start=(c == 0), stop=(c == ncnk - 1))))(),
                   reads=[b_sqv, b_const], writes=[bp_s])
        S.emit("dve", lambda e: e.tensor_scalar_mul(stvv[:, 0, :], p_s[0:1, 0:TT], -1.0 / d),
               reads=[bp_s], writes=[b_stvv])
        S.emit("dve", lambda e: e.tensor_mul(stvv[:, 1, :], stvv[:, 0, :], stvv[:, 0, :]),
               reads=[b_stvv], writes=[b_stvv])
        S.emit("dve", lambda e: e.scalar_tensor_tensor(stvv[:, 1, :], p_s[32:33, 0:TT], 1.0 / d,
                                                       stvv[:, 1, :], ALU.mult, ALU.subtract),
               reads=[bp_s, b_stvv], writes=[b_stvv])
        S.emit("act", lambda e: e.activation(stvv[:, 1, :], stvv[:, 1, :], AF.Ln, bias=1e-5),
               reads=[b_stvv, b_const], writes=[b_stvv])
        S.emit("act", lambda e: e.activation(stbv[:, 0, :], stvv[:, 1, :], AF.Exp, scale=-0.5),
               reads=[b_stvv], writes=[b_stbv])
        S.emit("dve", lambda e: e.tensor_mul(stbv[:, 1, :], stvv[:, 0, :], stbv[:, 0, :]),
               reads=[b_stvv, b_stbv], writes=[b_stbv])
        S.emit("pe", lambda e: e.matmul(p_r[:, 0:TT], t_ones1[:], stbv[:, 0, :], start=True, stop=True),
               reads=[b_stbv, b_const], writes=[bp_r])
        S.emit("pe", lambda e: e.matmul(p_m[:, 0:TT], t_ones1[:], stbv[:, 1, :], start=True, stop=True),
               reads=[b_stbv, b_const], writes=[bp_m])

    def phase_F(seq, xin_dram, xout_dram):
        b_xin = DBUF[id(xin_dram)]
        b_xout = DBUF[id(xout_dram)]
        pes = ExitStack()
        build._uid = getattr(build, "_uid", 0) + 1
        uid = f"u{build._uid}"
        psum = lambda name, shape, dt=F32: pes.enter_context(nc.psum_tensor(name + uid, shape, dt))
        p0 = psum("p0", [128, 512]); bp0 = Buf("p0")
        p1 = psum("p1", [128, 512]); bp1 = Buf("p1")
        p2 = psum("p2", [128, 512]); bp2 = Buf("p2")
        p3 = psum("p3", [128, 512]); bp3 = Buf("p3")
        pS2A = psum("pS2A", [128, 1024]); bpS2A = Buf("pS2A")
        pS2B = psum("pS2B", [128, 1024]); bpS2B = Buf("pS2B")
        p2bf = p2[:].bitcast(BF16)
        p3bf = p3[:].bitcast(BF16)
        pT = p2bf; bpT = Buf("pT") if False else None
        rstdv = pS2A[:][:, 0:512]
        murstdv = pS2B[:][:, 0:512]

        build_tables()
        S.emit("gp", lambda e: e.memset(y[:, :, 0:1], 0.0), writes=YBUFS)

        def ln_apply_to(dst_writer, src_dram):
            for tau in range(ntt):
                t0 = tau * TT
                for cg in range(3):
                    n = min(3, ncnk - cg * 3)
                    S.emit("sp", (lambda t0=t0, cg=cg, n=n: (lambda e: e.dma_start(
                        out=xf[:, 0:n, :],
                        in_=src_dram[seq, cg * 3:cg * 3 + n, :, t0:t0 + TT].rearrange("c p n -> p c n"))))(),
                           reads=[b_xin], writes=[b_KR], dma=True)
                    for j in range(n):
                        c = cg * 3 + j
                        S.emit("dve", (lambda c=c, j=j: (lambda e: e.tensor_copy(xs[:, c, :], xf[:, j, :])))(),
                               reads=[b_KR], writes=[b_XS])
                ln_stats(xs, [b_XS], sq, b_SQ, stv, b_QB, stb, b_STB, p3, bp3, rstdv, bpS2A, murstdv, bpS2B)
                for cg in range(3):
                    n = min(3, ncnk - cg * 3)
                    S.emit("sp", (lambda t0=t0, cg=cg, n=n: (lambda e: e.dma_start(
                        out=xf[:, 0:n, :],
                        in_=src_dram[seq, cg * 3:cg * 3 + n, :, t0:t0 + TT].rearrange("c p n -> p c n"))))(),
                           reads=[b_xin], writes=[b_KR], dma=True)
                    for j in range(n):
                        c = cg * 3 + j
                        S.emit("dve", (lambda c=c, j=j: (lambda e: e.tensor_mul(v1[:], xf[:, j, :], rstdv[:, 0:TT])))(),
                               reads=[b_KR, bpS2A], writes=[b_QB])
                        dst_writer(c, t0)

        def ywr(c, t0):
            S.emit("dve", (lambda c=c, t0=t0: (lambda e: e.tensor_add(
                y[:, c, 1 + t0:1 + t0 + TT], v1[:], murstdv[:, 0:TT])))(),
                   reads=[b_QB, bpS2B], writes=YBUFS)
        ln_apply_to(ywr, xin_dram)

        P = dict(p0=p0, bp0=bp0, p1=p1, bp1=bp1, p2=p2, bp2=bp2, p3=p3, bp3=bp3,
                 pS2A=pS2A, bpS2A=bpS2A, pS2B=pS2B, bpS2B=bpS2B,
                 p2bf=p2bf, p3bf=p3bf, rstdv=rstdv, murstdv=murstdv)
        for li in range(cfg.depth):
            attn_sublayer(seq, li, P)
            ffn_sublayer(seq, li, P)

        for tau in range(ntt):
            t0 = tau * TT
            for cg in range(3):
                n = min(3, ncnk - cg * 3)
                S.emit("sp", (lambda t0=t0, cg=cg, n=n: (lambda e: e.dma_start(
                    out=xf[:, 0:n, :],
                    in_=xin_dram[seq, cg * 3:cg * 3 + n, :, t0:t0 + TT].rearrange("c p n -> p c n"))))(),
                       reads=[b_xin], writes=[b_KR], dma=True)
                for j in range(n):
                    c = cg * 3 + j
                    S.emit("dve", (lambda c=c, j=j, t0=t0: (lambda e: e.scalar_tensor_tensor(
                        xf[:, j, :], y[:, c, 1 + t0:1 + t0 + TT], 0.5, xf[:, j, :], ALU.mult, ALU.add)))(),
                           reads=YBUFS, writes=[b_KR])
                S.emit("sp", (lambda t0=t0, cg=cg, n=n: (lambda e: e.dma_start(
                    out=xout_dram[seq, cg * 3:cg * 3 + n, :, t0:t0 + TT].rearrange("c p n -> p c n"),
                    in_=xf[:, 0:n, :])))(),
                       reads=[b_KR], writes=[b_xout], dma=True)
        pes.close()

    def attn_sublayer(seq, li, P):
        p0, bp0, p1, bp1, p2, bp2, p3, bp3 = (
            P["p0"], P["bp0"], P["p1"], P["bp1"], P["p2"], P["bp2"], P["p3"], P["bp3"])
        p2bf = P["p2bf"]
        p3bf = P["p3bf"]
        pS2A, bpS2A, pS2B, bpS2B = P["pS2A"], P["bpS2A"], P["pS2B"], P["bpS2B"]
        for tens, bf, base in ((wq, b_wq, 0), (wk, b_wk, d), (wv, b_wv, 2 * d)):
            S.emit("gp", (lambda tens=tens, base=base: (lambda e: e.dma_start(
                out=tens[:, :, :], in_=dwqkv[li, :, base:base + d].rearrange("(c p) n -> p c n", p=128))))(),
                   writes=[bf], dma=True)
        S.emit("gp", lambda e: e.dma_start(out=wo[:, :, :],
                                           in_=dwo[li].rearrange("(c p) n -> p c n", p=128)),
               writes=[b_wo], dma=True)

        bqp = [Buf(f"qp{li}_{hp}") for hp in range(nhp)]
        for tau in range(ntt):
            t0 = tau * TT
            if tau > 0:
                S.emit("gp", lambda e: e.tensor_copy(kr[:, :, 0:256], kr[:, :, 512:768]),
                       reads=[b_KR], writes=[b_KR])
            S.emit("dve", (lambda t0=t0: (lambda e: e.tensor_copy(xs[:, 0:4, :], y[:, 0:4, 1 + t0:1 + t0 + TT])))(),
                   reads=YBUFS, writes=[b_XS] + bqp)
            S.emit("dve", (lambda t0=t0: (lambda e: e.tensor_copy(xs[:, 4:8, :], y[:, 4:8, t0:t0 + TT])))(),
                   reads=YBUFS, writes=[b_XS])
            for g in range(4):
                slot = (4 * tau + g) % 6
                for nh in range(2):
                    fns = []
                    for c in range(ncnk):
                        fns.append((lambda c=c, g=g, nh=nh: (lambda e: e.matmul(
                            p2[:, 0:512], xs[:, c, g * 128:(g + 1) * 128], wv[:, c, nh * 512:(nh + 1) * 512],
                            start=(c == 0), stop=(c == ncnk - 1))))())
                    S.emit_group("pe", fns, reads=[b_XS, b_wv], writes=[bp2])
                    S.emit("act", (lambda slot=slot, nh=nh: (lambda e: e.activation(
                        vr[:, slot, nh * 512:(nh + 1) * 512], p2[:, 0:512], AF.Copy)))(),
                           reads=[bp2], writes=[b_VR])
            for side in (1, 0):
                wt, bwt = (wq, b_wq) if side == 0 else (wk, b_wk)
                smap = t_qsm[:, li:li + 1] if side == 0 else t_ksm[:, li:li + 1]
                ssap = t_qss[:, li:li + 1] if side == 0 else t_kss[:, li:li + 1]
                cst = tb_csS if side == 0 else tb_csI
                snt = tb_snS if side == 0 else tb_snI
                for hp in range(nhp):
                    pb_, bpb_ = (p0, bp0) if hp % 2 == 0 else (p1, bp1)
                    fns = []
                    for c in range(ncnk):
                        fns.append((lambda c=c, hp=hp, wt=wt, pb_=pb_: (lambda e: e.matmul(
                            pb_[:, 0:TT], wt[:, c, hp * 128:(hp + 1) * 128], xs[:, c, :],
                            start=(c == 0), stop=(c == ncnk - 1))))())
                    S.emit_group("pe", fns, reads=[b_XS, bwt], writes=[bpb_])
                    S.emit("act", (lambda hp=hp, pb_=pb_: (lambda e: e.activation(qb[:, hp, :], pb_[:, 0:TT], AF.Copy)))(),
                           reads=[bpb_], writes=[b_QB])
                    S.emit("act", (lambda hp=hp: (lambda e: e.activation(sq[:], qb[:, hp, :], AF.Square)))(),
                           reads=[b_QB], writes=[b_SQ])
                    S.emit("pe", (lambda hp=hp: (lambda e: e.matmul(
                        pS2A[0:16, 0:TT], t_Ohp[:, hp, :], sq[:],
                        start=(hp == 0), stop=(hp == nhp - 1))))(),
                           reads=[b_SQ, b_const], writes=[bpS2A])
                S.emit("dve", lambda e: e.tensor_scalar_max(pS2A[0:16, 0:TT], pS2A[0:16, 0:TT], 1e-18),
                       reads=[bpS2A], writes=[bpS2A])
                S.emit("act", lambda e: e.activation(p2[0:16, 0:TT], pS2A[0:16, 0:TT], AF.Ln),
                       reads=[bpS2A], writes=[bp2])
                if side == 0:
                    S.emit("act", lambda e: e.activation(t_inv[:, 0:TT], p2[0:16, 0:TT], AF.Exp, scale=-0.5, bias=LN8),
                           reads=[bp2, b_const], writes=[b_inv])
                else:
                    S.emit("act", lambda e: e.activation(t_inv[:, TT:2 * TT], p2[0:16, 0:TT], AF.Exp, scale=-0.5),
                           reads=[bp2], writes=[b_inv])
                colb = 0 if side == 0 else TT
                for hp in range(nhp):
                    pg_ = pS2B[:][:, 0:512] if hp % 2 == 0 else pS2B[:][:, 512:1024]
                    bpg_ = bpS2B
                    pv_, bpv_ = (p2, bp2) if hp % 2 == 0 else (p3, bp3)
                    S.emit("pe", (lambda hp=hp, pg_=pg_: (lambda e: e.matmul(pg_[:, 0:TT], t_Psig[:], qb[:, hp, :],
                                                                             start=True, stop=True)))(),
                           reads=[b_QB, b_const], writes=[bpg_])
                    S.emit("dve", (lambda hp=hp, smap=smap, cst=cst, t0=t0: (lambda e: e.scalar_tensor_tensor(
                        tt1[:], qb[:, hp, :], smap, cst[:, t0:t0 + TT], ALU.mult, ALU.mult)))(),
                           reads=[b_QB, b_A4, b_qsm], writes=[b_T1])
                    S.emit("dve", (lambda ssap=ssap, snt=snt, t0=t0, pg_=pg_: (lambda e: e.scalar_tensor_tensor(
                        tu2[:], pg_[:, 0:TT], ssap, snt[:, t0:t0 + TT], ALU.mult, ALU.mult)))(),
                           reads=[bpg_, b_A4, b_qsm], writes=[b_U2])
                    S.emit("dve", lambda e: e.tensor_add(tt1[:], tt1[:], tu2[:]),
                           reads=[b_T1, b_U2], writes=[b_T1])
                    S.emit("pe", (lambda hp=hp, colb=colb, pv_=pv_: (lambda e: e.matmul(
                        pv_[:, 0:TT], t_Bhd[:, hp, :], t_inv[:, colb:colb + TT], start=True, stop=True)))(),
                           reads=[b_inv, b_const], writes=[bpv_])
                    if side == 0:
                        S.emit("dve", (lambda hp=hp, pv_=pv_: (lambda e: e.tensor_mul(qp[:, hp, :], tt1[:], pv_[:, 0:TT])))(),
                               reads=[b_T1, bpv_], writes=[b_XS, bqp[hp]])
                    else:
                        S.emit("dve", (lambda hp=hp, pv_=pv_: (lambda e: e.tensor_mul(kr[:, hp, 256:768], tt1[:], pv_[:, 0:TT])))(),
                               reads=[b_T1, bpv_], writes=[b_KR])
            geo = []
            for gl in range(4):
                sub = 4 * tau + gl
                nchunk = min(sub + 1, 3)
                geo.append((gl, sub, nchunk, 128 * nchunk, (gl + 2 - (nchunk - 1)) * 128))

            def pctx(j):
                gl, sub, nchunk, W, ring0 = geo[j // nhp]
                hp = j % nhp
                m = j % 2
                pSx, bpSx = (pS2A, bpS2A) if m == 0 else (pS2B, bpS2B)
                return dict(gl=gl, sub=sub, nchunk=nchunk, W=W, ring0=ring0, hp=hp,
                            m=m, pSx=pSx, bpSx=bpSx,
                            mxs=t_mx[:, 2 * m:2 * m + 2],
                            b_mxs=b_mx0 if m == 0 else b_mx1,
                            b_dens=b_den0 if m == 0 else b_den1)

            def ps0(j):
                c = pctx(j)
                pSx, bpSx = c["pSx"], c["bpSx"]
                hp, gl, ring0, W, nchunk = c["hp"], c["gl"], c["ring0"], c["W"], c["nchunk"]
                fns = []
                for par in (0, 1):
                    off = par * 512
                    pb = par * 64
                    fns.append((lambda hp=hp, pb=pb, gl=gl, ring0=ring0, W=W, off=off, pSx=pSx:
                                (lambda e: e.matmul(
                                    pSx[:, off:off + W], qp[pb:pb + 64, hp, gl * 128:(gl + 1) * 128],
                                    kr[pb:pb + 64, hp, ring0:ring0 + W], start=True, stop=False)))())
                    if nchunk == 3:
                        fns.append((lambda off=off, pSx=pSx: (lambda e: e.matmul(
                            pSx[:, off:off + 128], t_I128[:], t_mTup[:, 0, :], start=False, stop=False)))())
                        fns.append((lambda off=off, pSx=pSx: (lambda e: e.matmul(
                            pSx[:, off + 256:off + 384], t_I128[:], t_mTup[:, 1, :], start=False, stop=True)))())
                    else:
                        fns.append((lambda off=off, W=W, pSx=pSx: (lambda e: e.matmul(
                            pSx[:, off + W - 128:off + W], t_I128[:], t_mTup[:, 1, :], start=False, stop=True)))())
                S.emit_group("pe", fns, reads=[bqp[hp], b_KR, b_const], writes=[bpSx])

            def ps1(j):
                c = pctx(j)
                pSx, W = c["pSx"], c["W"]
                pairview = pSx[:][:, 0:1024].rearrange("p (a b) -> p a b", a=2, b=512)[:, :, 0:W]
                S.emit("dve", (lambda pv=pairview, mxs=c["mxs"]: (lambda e: e.reduce_max(
                    mxs, pv, axis=AX.X, negate=True)))(),
                       reads=[c["bpSx"]], writes=[c["b_mxs"]])

            def ps23(j):
                c = pctx(j)
                pSx, bpSx, W, hp = c["pSx"], c["bpSx"], c["W"], c["hp"]
                for par in (0, 1):
                    h = 2 * hp + par
                    pt_, bpt_ = (ptA, b_ptA) if par == 0 else (ptB, b_ptB)
                    S.emit("act", (lambda h=h, W=W, off=par * 512, pSx=pSx, pt_=pt_,
                                   mx_=c["mxs"][:, par:par + 1]: (lambda e: e.activation(
                        pt_[:, 0:W], pSx[:, off:off + W], AF.Exp, bias=mx_,
                        accum_out=t_den[:, h:h + 1])))(),
                           reads=[bpSx, c["b_mxs"]], writes=[bpt_, c["b_dens"]])
                S.emit("dve", (lambda hp=hp: (lambda e: e.reciprocal(
                    t_den[:, 2 * hp:2 * hp + 2], t_den[:, 2 * hp:2 * hp + 2])))(),
                       reads=[c["b_dens"]], writes=[c["b_dens"]])
                for par in (0, 1):
                    h = 2 * hp + par
                    pt_, bpt_ = (ptA, b_ptA) if par == 0 else (ptB, b_ptB)
                    S.emit("dve", (lambda h=h, W=W, pt_=pt_: (lambda e: e.tensor_single_scalar(
                        pt_[:, 0:W], pt_[:, 0:W], t_den[:, h:h + 1], ALU.mult)))(),
                           reads=[bpt_, c["b_dens"]], writes=[bpt_])

            def ps47(j):
                c = pctx(j)
                W, nchunk, sub, hp, gl = c["W"], c["nchunk"], c["sub"], c["hp"], c["gl"]
                for par in (0, 1):
                    pt_, bpt_ = (ptA, b_ptA) if par == 0 else (ptB, b_ptB)
                    pTb_, bpTb_ = (p2bf, bp2) if par == 0 else (p3bf, bp3)
                    fns = []
                    for cc in range(nchunk):
                        fns.append((lambda cc=cc, pt_=pt_, pTb_=pTb_: (lambda e: e.transpose(
                            pTb_[:, cc * 128:(cc + 1) * 128], pt_[:, cc * 128:(cc + 1) * 128], t_I128[:])))())
                    S.emit_group("pe", fns, reads=[bpt_, b_const], writes=[bpTb_])
                    ptT_, bptT_ = (ptTA, b_ptTA) if par == 0 else (ptTB, b_ptTB)
                    eng = "act" if par == 0 else "dve"
                    if eng == "act":
                        S.emit("act", (lambda W=W, ptT_=ptT_, pTb_=pTb_: (lambda e: e.activation(
                            ptT_[:, :, :].rearrange("p a b -> p (a b)")[:, 0:W], pTb_[:, 0:W], AF.Copy)))(),
                               reads=[bpTb_], writes=[bptT_])
                    else:
                        S.emit("dve", (lambda W=W, ptT_=ptT_, pTb_=pTb_: (lambda e: e.tensor_copy(
                            ptT_[:, :, :].rearrange("p a b -> p (a b)")[:, 0:W], pTb_[:, 0:W])))(),
                               reads=[bpTb_], writes=[bptT_])
                for par in (0, 1):
                    h = 2 * hp + par
                    ptT_, bptT_ = (ptTA, b_ptTA) if par == 0 else (ptTB, b_ptTB)
                    fns = []
                    for cc in range(nchunk):
                        gslot = (sub - (nchunk - 1) + cc) % 6
                        fns.append((lambda h=h, cc=cc, gslot=gslot, par=par, nchunk=nchunk, ptT_=ptT_:
                                    (lambda e: e.matmul(
                                        p0[par * 64:par * 64 + 64, 0:128], vr[:, gslot, h * 64:(h + 1) * 64],
                                        ptT_[:, cc, :], start=(cc == 0), stop=(cc == nchunk - 1),
                                        tile_position=(0, par * 64))))())
                    S.emit_group("pe", fns, reads=[b_VR, bptT_], writes=[bp0])
                S.emit("act", (lambda hp=hp, gl=gl: (lambda e: e.activation(
                    ao[:, hp, gl * 128:(gl + 1) * 128], p0[:, 0:128], AF.Copy)))(),
                       reads=[bp0], writes=[b_AO])

            NJ = 4 * nhp
            for j in range(NJ + 2):
                if j < NJ:
                    ps0(j)
                if 0 <= j - 2 < NJ:
                    ps47(j - 2)
                if 0 <= j - 1 < NJ:
                    ps23(j - 1)
                if j < NJ:
                    ps1(j)
            for m in range(ncnk):
                fns = []
                for c in range(ncnk):
                    fns.append((lambda m=m, c=c: (lambda e: e.matmul(
                        p1[:, 0:TT], wo[:, c, m * 128:(m + 1) * 128], ao[:, c, :],
                        start=(c == 0), stop=(c == ncnk - 1))))())
                S.emit_group("pe", fns, reads=[b_AO, b_wo], writes=[bp1])
                S.emit("dve", (lambda m=m, t0=t0: (lambda e: e.tensor_add(
                    y[:, m, 1 + t0:1 + t0 + TT], p1[:, 0:TT], y[:, m, 1 + t0:1 + t0 + TT])))(),
                       reads=[bp1] + YBUFS, writes=YBUFS)

    def ffn_sublayer(seq, li, P):
        p0, bp0, p1, bp1, p2, bp2, p3, bp3 = (
            P["p0"], P["bp0"], P["p1"], P["bp1"], P["p2"], P["bp2"], P["p3"], P["bp3"])
        rstdv, murstdv = P["rstdv"], P["murstdv"]
        bpS2A, bpS2B = P["bpS2A"], P["bpS2B"]
        g_ap = lambda c: t_gh[:, li, c:c + 1]
        bias_ap = lambda c: t_bh[:, li, c:c + 1]
        b2_ap = lambda c: t_b2[:, li, c:c + 1]
        XNB = [b_KR, b_VR, b_AO]

        def hv(kk):
            return hv_lo[:, kk, :] if kk < 8 else hv_hi[:, kk - 8, :]
        HVB = [b_XS, b_QB]

        for tau in range(ntt):
            t0 = tau * TT
            S.emit("dve", (lambda t0=t0: (lambda e: e.tensor_copy(xs[:, 0:4, :], y[:, 0:4, 1 + t0:1 + t0 + TT])))(),
                   reads=YBUFS, writes=[b_XS])
            S.emit("dve", (lambda t0=t0: (lambda e: e.tensor_copy(xs[:, 4:8, :], y[:, 4:8, t0:t0 + TT])))(),
                   reads=YBUFS, writes=[b_XS])
            ln_stats(xs, [b_XS], sq, b_SQ, stv, b_QB, stb, b_STB, p3, bp3, rstdv, bpS2A, murstdv, bpS2B)
            for c in range(ncnk):
                off = 1 + t0 if c < 4 else t0
                S.emit("dve", (lambda c=c, off=off: (lambda e: e.scalar_tensor_tensor(
                    v1[:], y[:, c, off:off + TT], g_ap(c), rstdv[:, 0:TT], ALU.mult, ALU.mult)))(),
                       reads=YBUFS + [bpS2A, b_gh], writes=[b_QB])
                S.emit("dve", (lambda c=c: (lambda e: e.scalar_tensor_tensor(
                    v1[:], murstdv[:, 0:TT], g_ap(c), v1[:], ALU.mult, ALU.add)))(),
                       reads=[bpS2B, b_QB, b_gh], writes=[b_QB])
                S.emit("act", (lambda c=c, t0=t0: (lambda e: e.activation(
                    xnf[:, c, t0:t0 + TT], v1[:], AF.Identity, bias=bias_ap(c))))(),
                       reads=[b_QB, b_bh], writes=XNB)
        for half in range(2):
            S.emit("gp", (lambda half=half: (lambda e: e.dma_start(
                out=w1v[:, :, :], in_=dw1[li, :, half * 2816:(half + 1) * 2816].rearrange("(c p) n -> p c n", p=128))))(),
                   writes=W1BUFS, dma=True)
            S.emit("gp", (lambda half=half: (lambda e: e.dma_start(
                out=w2v[:, :, :], in_=dw2[li, half * 1408:(half + 1) * 1408, :].rearrange("(c p) n -> p c n", p=128))))(),
                   writes=W2BUFS, dma=True)
            for tau in range(ntt):
                t0 = tau * TT
                for mm in range(11):
                    fns = []
                    for c in range(ncnk):
                        fns.append((lambda mm=mm, c=c, t0=t0: (lambda e: e.matmul(
                            p1[:, 0:TT], w1v[:, c, 1408 + mm * 128:1408 + (mm + 1) * 128],
                            xnf[:, c, t0:t0 + TT], start=(c == 0), stop=(c == ncnk - 1))))())
                    S.emit_group("pe", fns, reads=XNB + W1BUFS, writes=[bp1])
                    S.emit("act", (lambda mm=mm, half=half: (lambda e: e.activation(
                        gel[:, :], p1[:, 0:TT], AF.Gelu, bias=t_b1g[:, li, half * 11 + mm:half * 11 + mm + 1])))(),
                           reads=[bp1, b_b1g], writes=[b_QB])
                    fns = []
                    for c in range(ncnk):
                        fns.append((lambda mm=mm, c=c, t0=t0: (lambda e: e.matmul(
                            p2[:, 0:TT], w1v[:, c, mm * 128:(mm + 1) * 128],
                            xnf[:, c, t0:t0 + TT], start=(c == 0), stop=(c == ncnk - 1))))())
                    S.emit_group("pe", fns, reads=XNB + W1BUFS, writes=[bp2])
                    S.emit("dve", (lambda mm=mm, half=half: (lambda e: e.scalar_tensor_tensor(
                        hv(mm), p2[:, 0:TT], t_b1a[:, li, half * 11 + mm:half * 11 + mm + 1],
                        gel[:, :], ALU.add, ALU.mult)))(),
                           reads=[bp2, b_QB, b_b1a], writes=HVB)
                for m in range(ncnk):
                    fns = []
                    for kk in range(11):
                        fns.append((lambda m=m, kk=kk: (lambda e: e.matmul(
                            p0[:, 0:TT], w2v[:, kk, m * 128:(m + 1) * 128], hv(kk),
                            start=(kk == 0), stop=(kk == 10))))())
                    S.emit_group("pe", fns, reads=HVB + W2BUFS, writes=[bp0])
                    if half == 0:
                        S.emit("dve", (lambda m=m, t0=t0: (lambda e: e.tensor_add(
                            y[:, m, 1 + t0:1 + t0 + TT], p0[:, 0:TT], y[:, m, 1 + t0:1 + t0 + TT])))(),
                               reads=[bp0] + YBUFS, writes=YBUFS)
                    else:
                        S.emit("dve", (lambda m=m, t0=t0: (lambda e: e.scalar_tensor_tensor(
                            y[:, m, 1 + t0:1 + t0 + TT], p0[:, 0:TT], b2_ap(m),
                            y[:, m, 1 + t0:1 + t0 + TT], ALU.add, ALU.add)))(),
                               reads=[bp0, b_b2] + YBUFS, writes=YBUFS)

    def phase_outer(seq):
        pes = ExitStack()
        psum = lambda name, shape, dt=F32: pes.enter_context(nc.psum_tensor(name + f"o{seq}", shape, dt))
        pS = psum("pS", [128, 2048]); bpS = Buf("pS")
        pA = psum("pA", [128, 512]); bpA = Buf("pA")
        pB = psum("pB", [128, 512]); bpB = Buf("pB")
        pC = psum("pC", [128, 512]); bpC = Buf("pC")
        pT = psum("pTo", [128, 1024], BF16); bpT = Buf("pTo")

        for tens, bf, dram in ((wq, b_wq, dwq2), (wk, b_wk, dwk2), (wv, b_wv, dwv2)):
            S.emit("gp", (lambda tens=tens, dram=dram: (lambda e: e.dma_start(
                out=tens[:, :, :], in_=dram.rearrange("(c p) n -> p c n", p=128))))(),
                   writes=[bf], dma=True)

        def vslot(s):
            return (o_vfa[:, s, :], [b_VR, b_AO, b_ptA, b_ptB, b_ptTA, b_ptTB, b_T1, b_U2, b_SQ]) if s < 11 \
                else (o_vfb[:, s - 11, :], [b_A4])

        OSCR = [b_VR, b_AO]

        for tau in range(ntt):
            t0 = tau * TT
            for cg in range(3):
                n = min(3, ncnk - cg * 3)
                S.emit("sp", (lambda t0=t0, cg=cg, n=n: (lambda e: e.dma_start(
                    out=o_xf[:, 0:n, :],
                    in_=dx1[seq, cg * 3:cg * 3 + n, :, t0:t0 + TT].rearrange("c p n -> p c n"))))(),
                       reads=[b_dx1], writes=OSCR, dma=True)
                for j in range(n):
                    c = cg * 3 + j
                    S.emit("dve", (lambda c=c, j=j, t0=t0: (lambda e: e.tensor_copy(
                        o_xnf[:, c, t0:t0 + TT], o_xf[:, j, :])))(),
                           reads=OSCR, writes=O_XNF2)
            ln_stats(o_xnf[:, :, t0:t0 + TT], O_XNF2, o_sq, b_SQ, o_stv, b_AO, o_stb, b_ptA,
                     pA, bpA, pB, bpB, pC, bpC)
            for cg in range(3):
                n = min(3, ncnk - cg * 3)
                S.emit("sp", (lambda t0=t0, cg=cg, n=n: (lambda e: e.dma_start(
                    out=o_xf[:, 0:n, :],
                    in_=dx1[seq, cg * 3:cg * 3 + n, :, t0:t0 + TT].rearrange("c p n -> p c n"))))(),
                       reads=[b_dx1], writes=OSCR, dma=True)
                for j in range(n):
                    c = cg * 3 + j
                    S.emit("dve", (lambda c=c, j=j: (lambda e: e.tensor_mul(o_v1[:], o_xf[:, j, :], pB[:, 0:TT])))(),
                           reads=OSCR + [bpB], writes=OSCR)
                    S.emit("dve", (lambda c=c, t0=t0: (lambda e: e.tensor_add(
                        o_xnf[:, c, t0:t0 + TT], o_v1[:], pC[:, 0:TT])))(),
                           reads=OSCR + [bpC], writes=O_XNF2)
        for tau in range(ntt):
            t0 = tau * TT
            for side in range(2):
                wt, bwt = (wq, b_wq) if side == 0 else (wk, b_wk)
                dst, bdst = (o_qf, b_QF) if side == 0 else (o_kf, b_KF)
                pb_, bpb_ = (pA, bpA) if side == 0 else (pB, bpB)
                for hp in range(nhp):
                    fns = []
                    for c in range(ncnk):
                        fns.append((lambda c=c, hp=hp, wt=wt, t0=t0, pb_=pb_: (lambda e: e.matmul(
                            pb_[:, 0:TT], wt[:, c, hp * 128:(hp + 1) * 128], o_xnf[:, c, t0:t0 + TT],
                            start=(c == 0), stop=(c == ncnk - 1))))())
                    S.emit_group("pe", fns, reads=O_XNF2 + [bwt], writes=[bpb_])
                    S.emit("act", (lambda dst=dst, hp=hp, t0=t0, pb_=pb_: (lambda e: e.activation(
                        dst[:, hp, t0:t0 + TT], pb_[:, 0:TT], AF.Copy)))(),
                           reads=[bpb_], writes=[bdst])
            for g in range(4):
                gg = tau * 4 + g
                vv_, bvv_ = vslot(gg)
                for nh in range(2):
                    fns = []
                    for c in range(ncnk):
                        fns.append((lambda c=c, g=g, nh=nh, t0=t0: (lambda e: e.matmul(
                            pC[:, 0:512], o_xnf[:, c, t0 + g * 128:t0 + (g + 1) * 128],
                            wv[:, c, nh * 512:(nh + 1) * 512], start=(c == 0), stop=(c == ncnk - 1))))())
                    S.emit_group("pe", fns, reads=O_XNF2 + [b_wv], writes=[bpC])
                    S.emit("act", (lambda vv_=vv_, nh=nh: (lambda e: e.activation(
                        vv_[:, nh * 512:(nh + 1) * 512], pC[:, 0:512], AF.Copy)))(),
                           reads=[bpC], writes=bvv_)
        S.emit("gp", lambda e: e.dma_start(out=wo[:, :, :], in_=dwpj.rearrange("(c p) n -> p c n", p=128)),
               writes=[b_wo], dma=True)
        pj_x = view(AR3, 0, 1536, [128, 3, TT])
        pj_v = view(AR3, 1536, 512, [128, TT])
        PJB = [b_XS]
        for sub in range(cfg.nsub):
            kend = (sub + 1) * 128
            for h in range(nheads):
                hp, par = h // 2, h % 2
                pb = par * 64
                fns = []
                nblk = (kend + 511) // 512
                for bI in range(nblk):
                    c0 = bI * 512
                    w = min(512, kend - c0)
                    last = (bI == nblk - 1)
                    fns.append((lambda hp=hp, pb=pb, sub=sub, c0=c0, w=w, last=last: (lambda e: e.matmul(
                        pS[:, c0:c0 + w], o_qf[pb:pb + 64, hp, sub * 128:(sub + 1) * 128],
                        o_kf[pb:pb + 64, hp, c0:c0 + w], start=True, stop=not last)))())
                fns.append((lambda kend=kend: (lambda e: e.matmul(
                    pS[:, kend - 128:kend], t_I128[:], t_mTup[:, 1, :], start=False, stop=True)))())
                S.emit_group("pe", fns, reads=[b_QF, b_KF, b_const], writes=[bpS])
                S.emit("dve", (lambda kend=kend: (lambda e: e.reduce_max(t_mx[:, 0:1], pS[:, 0:kend], axis=AX.X)))(),
                       reads=[bpS], writes=[b_mx])
                S.emit("dve", lambda e: e.tensor_scalar_mul(t_bias[:], t_mx[:, 0:1], -1.0 / 32.0),
                       reads=[b_mx], writes=[b_bias])
                S.emit("act", (lambda h=h, kend=kend: (lambda e: e.activation(
                    o_ptf[:, 0:kend], pS[:, 0:kend], AF.Exp, bias=t_bias[:, 0:1], scale=1.0 / 32.0,
                    accum_out=t_den[:, h:h + 1])))(),
                       reads=[bpS, b_bias], writes=[b_PTF, b_den])
                S.emit("dve", (lambda h=h: (lambda e: e.reciprocal(t_den[:, h:h + 1], t_den[:, h:h + 1])))(),
                       reads=[b_den], writes=[b_den])
                S.emit("dve", (lambda h=h, kend=kend: (lambda e: e.tensor_single_scalar(
                    o_ptf[:, 0:kend], o_ptf[:, 0:kend], t_den[:, h:h + 1], ALU.mult)))(),
                       reads=[b_PTF, b_den], writes=[b_PTF])
                nkc = sub + 1
                for c4 in range(0, nkc, 4):
                    cnt = min(4, nkc - c4)
                    fns = []
                    for c in range(cnt):
                        fns.append((lambda c4=c4, c=c: (lambda e: e.transpose(
                            pT[:, c * 128:(c + 1) * 128], o_ptf[:, (c4 + c) * 128:(c4 + c + 1) * 128],
                            t_I128[:])))())
                    S.emit_group("pe", fns, reads=[b_PTF, b_const], writes=[bpT])
                    S.emit("dve", (lambda cnt=cnt: (lambda e: e.tensor_copy(
                        o_ptT[:, :, :].rearrange("p a b -> p (a b)")[:, 0:cnt * 128], pT[:, 0:cnt * 128])))(),
                           reads=[bpT], writes=[b_QB])
                    fns = []
                    rds = [b_QB]
                    for c in range(cnt):
                        vv_, bvv_ = vslot(c4 + c)
                        rds = rds + [bb for bb in bvv_ if bb not in rds]
                        fns.append((lambda c4=c4, c=c, h=h, par=par, sub=sub, vv_=vv_: (lambda e: e.matmul(
                            pC[par * 64:par * 64 + 64, 0:128], vv_[:, h * 64:(h + 1) * 64],
                            o_ptT[:, c, :], start=(c4 + c == 0), stop=(c4 + c == sub),
                            tile_position=(0, par * 64))))())
                    S.emit_group("pe", fns, reads=rds, writes=[bpC])
                if par == 1:
                    S.emit("act", (lambda hp=hp, sub=sub: (lambda e: e.activation(
                        o_aot[:, hp, :], pC[:, 0:128], AF.Copy)))(),
                           reads=[bpC], writes=[b_AOT])
            t0 = sub * 128
            for mg in range(3):
                n = min(3, ncnk - mg * 3)
                S.emit("sp", (lambda t0=t0, mg=mg, n=n: (lambda e: e.dma_start(
                    out=pj_x[:, 0:n, 0:128],
                    in_=dx1[seq, mg * 3:mg * 3 + n, :, t0:t0 + 128].rearrange("c p n -> p c n"))))(),
                       reads=[b_dx1], writes=PJB, dma=True)
                for j in range(n):
                    m = mg * 3 + j
                    fns = []
                    for c in range(ncnk):
                        fns.append((lambda m=m, c=c: (lambda e: e.matmul(
                            pA[:, 0:128], wo[:, c, m * 128:(m + 1) * 128], o_aot[:, c, :],
                            start=(c == 0), stop=(c == ncnk - 1))))())
                    S.emit_group("pe", fns, reads=[b_AOT, b_wo], writes=[bpA])
                    S.emit("dve", (lambda m=m, j=j: (lambda e: e.scalar_tensor_tensor(
                        pj_v[:, 0:128], pA[:, 0:128], t_bpj[:, m:m + 1], pj_x[:, j, 0:128],
                        ALU.add, ALU.add)))(),
                           reads=[bpA, b_bpj] + PJB, writes=PJB)
                    S.emit("sp", (lambda t0=t0, m=m: (lambda e: e.dma_start(
                        out=dx2[seq, m, :, t0:t0 + 128], in_=pj_v[:, 0:128])))(),
                           reads=PJB, writes=[b_dx2], dma=True)
        pes.close()

    for seq in range(NSEQ):
        ph = cfg.phases
        if ph == "faf":
            phase_F(seq, dx, dx1)
            phase_outer(seq)
            phase_F(seq, dx2, dout)
        elif ph == "f":
            phase_F(seq, dx, dout)
        elif ph == "fa":
            phase_F(seq, dx, dx1)
            phase_outer(seq)
            for tau in range(ntt):
                t0 = tau * TT
                for cg in range(3):
                    n = min(3, ncnk - cg * 3)
                    S.emit("sp", (lambda t0=t0, cg=cg, n=n: (lambda e: e.dma_start(
                        out=xf[:, 0:n, :],
                        in_=dx2[seq, cg * 3:cg * 3 + n, :, t0:t0 + TT].rearrange("c p n -> p c n"))))(),
                           reads=[b_dx2], writes=[b_KR], dma=True)
                    S.emit("sp", (lambda t0=t0, cg=cg, n=n: (lambda e: e.dma_start(
                        out=dout[seq, cg * 3:cg * 3 + n, :, t0:t0 + TT].rearrange("c p n -> p c n"),
                        in_=xf[:, 0:n, :])))(),
                           reads=[b_KR], dma=True)

    S.wait_all_done()
    with nc.Block() as block:
        S.flush(block)
    return S



def prep_inputs(inputs, n_cores=8):
    x = np.asarray(inputs["L0_attn_x"])
    B, T, D = x.shape
    wq = np.asarray(inputs["L0_attn_wq"]); wk = np.asarray(inputs["L0_attn_wk"])
    wv = np.asarray(inputs["L0_attn_wv"])
    wpj = np.asarray(inputs["L0_attn_w_proj"]); bpj = np.asarray(inputs["L0_attn_b_proj"])
    Wqkv = np.asarray(inputs["L0_ffn_Wqkv"])
    qs = np.asarray(inputs["L0_ffn_q_scale"]); ks = np.asarray(inputs["L0_ffn_k_scale"])
    Wo = np.asarray(inputs["L0_ffn_Wo"])
    g = np.asarray(inputs["L0_ffn_ln_g"]); bb = np.asarray(inputs["L0_ffn_ln_b"])
    W1 = np.asarray(inputs["L0_ffn_W1"]); b1 = np.asarray(inputs["L0_ffn_b1"])
    W2 = np.asarray(inputs["L0_ffn_W2"]); b2 = np.asarray(inputs["L0_ffn_b2"])
    DEP = Wqkv.shape[0]
    DH, PAD, H2 = 2730, 86, 1408

    a, gate = W1[:, :, :DH], W1[:, :, DH:]
    z = np.zeros((DEP, D, PAD), W1.dtype)
    w1p = np.concatenate([a[:, :, :H2], gate[:, :, :H2], a[:, :, H2:], z,
                          gate[:, :, H2:], z], axis=2)
    b1a, b1g = b1[:, :DH], b1[:, DH:]
    zb = np.zeros((DEP, PAD), b1.dtype)
    b1ap = np.concatenate([b1a[:, :H2], b1a[:, H2:], zb], axis=1)
    b1gp = np.concatenate([b1g[:, :H2], b1g[:, H2:], zb], axis=1)
    zw = np.zeros((DEP, PAD, D), W2.dtype)
    w2p = np.concatenate([W2[:, :H2, :], W2[:, H2:, :], zw], axis=1)

    def cols_h(v):
        n = v.shape[1] // 128
        return np.ascontiguousarray(v.reshape(DEP, n, 128).transpose(2, 0, 1))

    p = np.arange(128)
    shared = dict(
        wqkv=np.ascontiguousarray(Wqkv), wo=np.ascontiguousarray(Wo),
        w1p=np.ascontiguousarray(w1p), w2p=np.ascontiguousarray(w2p),
        g_h=cols_h(g), b_h=cols_h(bb), b1a_h=cols_h(b1ap), b1g_h=cols_h(b1gp),
        b2_h=cols_h(b2),
        qsm=np.ascontiguousarray(qs[:, p % 64].T), qss=np.ascontiguousarray(qs[:, (p + 32) % 64].T),
        ksm=np.ascontiguousarray(ks[:, p % 64].T), kss=np.ascontiguousarray(ks[:, (p + 32) % 64].T),
        wq2=np.ascontiguousarray(wq.transpose(1, 0, 2).reshape(D, D)),
        wk2=np.ascontiguousarray(wk.transpose(1, 0, 2).reshape(D, D)),
        wv2=np.ascontiguousarray(wv.transpose(1, 0, 2).reshape(D, D)),
        wpj=np.ascontiguousarray(wpj),
        bpj_h=np.ascontiguousarray(bpj.reshape(8, 128).T))
    per = B // n_cores
    in_maps = []
    for c in range(n_cores):
        xs_ = x[per * c:per * (c + 1)]
        m = dict(shared)
        m["x"] = np.ascontiguousarray(xs_.transpose(0, 2, 1).reshape(per, 8, 128, T))
        in_maps.append(m)
    return in_maps, (B, T, D)


def kernel(**inputs):
    in_maps, (B, T, D) = prep_inputs(inputs)
    nc = bass.Bass("TRN2", target_bir_lowering=False)
    cfg = Cfg(T=T)
    build(nc, cfg)
    res = run_bass_kernel_spmd(nc, in_maps, core_ids=list(range(8)))
    per = B // 8
    outs = []
    for c in range(8):
        o = np.asarray(res.results[c]["out"]).reshape(per, D, T).transpose(0, 2, 1)
        outs.append(o)
    return np.concatenate(outs, axis=0).astype(np.float32)


# version 08
# speedup vs baseline: 1.1523x; 1.1523x over previous
import numpy as np
from contextlib import ExitStack

import concourse.bass as bass
import concourse.mybir as mybir
from concourse.bass_utils import run_bass_kernel_spmd

F32 = mybir.dt.float32
BF16 = mybir.dt.bfloat16
I32 = mybir.dt.int32
AF = mybir.ActivationFunctionType
ALU = mybir.AluOpType
AX = mybir.AxisListType

TWO_PI = float(2 * np.pi)
NEG_BIG = -1.0e30
LN8 = float(np.log(8.0))

STREAMS = ("pe", "act", "dve", "gp", "sp")


class Buf:
    __slots__ = ("name", "w", "readers")

    def __init__(self, name):
        self.name = name
        self.w = None
        self.readers = {}

    def __repr__(self):
        return f"Buf({self.name})"


NDMASEM = 12


class Sched:
    def __init__(self, nc, es):
        self.nc = nc
        self.streams = {s: [] for s in STREAMS}
        self.sems = {}
        for s in ("pe", "act", "dve", "gp", "sp"):
            self.sems[s] = es.enter_context(nc.semaphore(f"sem_{s}"))
        for q in ("gpd", "spd"):
            for j in range(NDMASEM):
                self.sems[f"{q}{j}"] = es.enter_context(nc.semaphore(f"sem_{q}{j}"))
        self.counts = {s: 0 for s in self.sems}
        self.dma_rot = {"gp": 0, "sp": 0}
        self.waited = {s: {} for s in STREAMS}
        self.nwaits = 0
        self.ninstr = 0

    def _collect_waits(self, stream, reads, writes):
        want = {}

        def add(tok):
            if tok is None:
                return
            s, v = tok
            if want.get(s, -1) < v:
                want[s] = v

        for b in reads:
            add(b.w)
        for b in writes:
            add(b.w)
            for s, v in b.readers.items():
                if want.get(s, -1) < v:
                    want[s] = v
        waits = []
        for s, v in sorted(want.items()):
            if s == "pe" and stream == "pe":
                continue
            if self.waited[stream].get(s, 0) >= v:
                continue
            waits.append((s, v))
            self.waited[stream][s] = v
        return waits

    def emit(self, stream, fn, reads=(), writes=(), dma=False):
        return self.emit_group(stream, [fn], reads=reads, writes=writes, dma=dma)

    def emit_group(self, stream, fns, reads=(), writes=(), dma=False):
        waits = self._collect_waits(stream, reads, writes)
        if dma:
            q = {"gp": "gpd", "sp": "spd"}[stream]
            j = self.dma_rot[stream]
            self.dma_rot[stream] = (j + 1) % NDMASEM
            semname = f"{q}{j}"
            prev = self.counts[semname]
            if prev > 0 and self.waited[stream].get(semname, 0) < prev:
                waits.append((semname, prev))
                self.waited[stream][semname] = prev
            inc = 16
        else:
            semname = stream
            inc = 1
        sem = self.sems[semname]
        self.counts[semname] += inc
        tok = (semname, self.counts[semname])
        semobjs = self.sems
        self.ninstr += len(fns) + len(waits)

        def closure(eng, waits=waits, fns=fns, sem=sem, inc=inc):
            for s, v in waits:
                eng.wait_ge(semobjs[s], v)
            for f in fns[:-1]:
                f(eng)
            ins = fns[-1](eng)
            ins.then_inc(sem, inc)
            return ins

        self.streams[stream].append(closure)
        self.nwaits += len(waits)
        for b in writes:
            b.w = tok
            b.readers = {}
        for b in reads:
            if b.readers.get(semname, 0) < tok[1]:
                b.readers[semname] = tok[1]
        return tok

    def link(self, dsts, srcs):
        merged = {}
        for b in srcs:
            if b.w is not None:
                s, v = b.w
                if merged.get(s, 0) < v:
                    merged[s] = v
            for s, v in b.readers.items():
                if merged.get(s, 0) < v:
                    merged[s] = v
        for b in dsts:
            for s, v in merged.items():
                if b.readers.get(s, 0) < v:
                    b.readers[s] = v

    def wait_all_done(self):
        finals = [(s, self.counts[s]) for s in self.counts if self.counts[s] > 0]
        semobjs = self.sems

        def closure(eng):
            for s, v in finals:
                eng.wait_ge(semobjs[s], v)
        self.streams["sp"].append(closure)

    def flush(self, block):
        st = self.streams
        block.tensor(lambda e: [f(e) for f in st["pe"]])
        block.scalar(lambda e: [f(e) for f in st["act"]])
        block.vector(lambda e: [f(e) for f in st["dve"]])
        block.gpsimd(lambda e: [f(e) for f in st["gp"]])
        block.sync(lambda e: [f(e) for f in st["sp"]])


class Cfg:
    def __init__(self, T=2048, depth=6, n_seq=2, phases="faf", no_ffn=False):
        self.no_ffn = no_ffn
        self.T = T
        self.depth = depth
        self.n_seq = n_seq
        self.phases = phases
        self.d = 1024
        self.ncnk = 8
        self.nheads = 16
        self.nhp = 8
        self.TT = 512
        self.ntt = T // self.TT
        self.nsub = T // 128


def build(nc, cfg: Cfg):
    es = ExitStack()
    S = Sched(nc, es)
    T, TT, ntt = cfg.T, cfg.TT, cfg.ntt
    d, ncnk, nhp, nheads = cfg.d, cfg.ncnk, cfg.nhp, cfg.nheads
    DEP = 6
    NSEQ = cfg.n_seq

    dx = nc.dram_tensor("x", [NSEQ, ncnk, 128, T], F32, kind="ExternalInput").ap()
    dout = nc.dram_tensor("out", [NSEQ, ncnk, 128, T], F32, kind="ExternalOutput").ap()
    dwqkv = nc.dram_tensor("wqkv", [DEP, d, 3 * d], F32, kind="ExternalInput").ap()
    dwo = nc.dram_tensor("wo", [DEP, d, d], F32, kind="ExternalInput").ap()
    dw1 = nc.dram_tensor("w1p", [DEP, d, 5632], F32, kind="ExternalInput").ap()
    dw2 = nc.dram_tensor("w2p", [DEP, 2816, d], F32, kind="ExternalInput").ap()
    dgh = nc.dram_tensor("g_h", [128, DEP, ncnk], F32, kind="ExternalInput").ap()
    dbh = nc.dram_tensor("b_h", [128, DEP, ncnk], F32, kind="ExternalInput").ap()
    db1a = nc.dram_tensor("b1a_h", [128, DEP, 22], F32, kind="ExternalInput").ap()
    db1g = nc.dram_tensor("b1g_h", [128, DEP, 22], F32, kind="ExternalInput").ap()
    db2 = nc.dram_tensor("b2_h", [128, DEP, ncnk], F32, kind="ExternalInput").ap()
    dqsm = nc.dram_tensor("qsm", [128, DEP], F32, kind="ExternalInput").ap()
    dqss = nc.dram_tensor("qss", [128, DEP], F32, kind="ExternalInput").ap()
    dksm = nc.dram_tensor("ksm", [128, DEP], F32, kind="ExternalInput").ap()
    dkss = nc.dram_tensor("kss", [128, DEP], F32, kind="ExternalInput").ap()
    dwq2 = nc.dram_tensor("wq2", [d, d], F32, kind="ExternalInput").ap()
    dwk2 = nc.dram_tensor("wk2", [d, d], F32, kind="ExternalInput").ap()
    dwv2 = nc.dram_tensor("wv2", [d, d], F32, kind="ExternalInput").ap()
    dwpj = nc.dram_tensor("wpj", [d, d], F32, kind="ExternalInput").ap()
    dbpj = nc.dram_tensor("bpj_h", [128, ncnk], F32, kind="ExternalInput").ap()
    dx1 = nc.dram_tensor("x1s", [NSEQ, ncnk, 128, T], F32, kind="Internal").ap()
    dx2 = nc.dram_tensor("x2s", [NSEQ, ncnk, 128, T], F32, kind="Internal").ap()
    b_dx = Buf("dx"); b_dx1 = Buf("dx1"); b_dx2 = Buf("dx2"); b_dout = Buf("dout")
    DBUF = {id(dx): b_dx, id(dx1): b_dx1, id(dx2): b_dx2, id(dout): b_dout}

    sbuf = lambda name, shape, dt=F32: es.enter_context(nc.sbuf_tensor(name, shape, dt))

    t_gh = sbuf("t_gh", [128, DEP, ncnk]); b_gh = Buf("gh")
    t_bh = sbuf("t_bh", [128, DEP, ncnk]); b_bh = Buf("bh")
    t_b1a = sbuf("t_b1a", [128, DEP, 22]); b_b1a = Buf("b1a")
    t_b1g = sbuf("t_b1g", [128, DEP, 22]); b_b1g = Buf("b1g")
    t_b2 = sbuf("t_b2", [128, DEP, ncnk]); b_b2 = Buf("b2")
    t_qsm = sbuf("t_qsm", [128, DEP]); b_qsm = Buf("qsm")
    t_qss = sbuf("t_qss", [128, DEP]); _ = None
    t_ksm = sbuf("t_ksm", [128, DEP])
    t_kss = sbuf("t_kss", [128, DEP])
    t_bpj = sbuf("t_bpj", [128, ncnk]); b_bpj = Buf("bpj")
    for tens, dram, bf in ((t_gh, dgh, b_gh), (t_bh, dbh, b_bh), (t_b1a, db1a, b_b1a),
                           (t_b1g, db1g, b_b1g), (t_b2, db2, b_b2), (t_qsm, dqsm, b_qsm),
                           (t_qss, dqss, b_qsm), (t_ksm, dksm, b_qsm), (t_kss, dkss, b_qsm),
                           (t_bpj, dbpj, b_bpj)):
        S.emit("sp", (lambda tens=tens, dram=dram: (lambda e: e.dma_start(out=tens[:], in_=dram[:])))(),
               writes=[bf], dma=True)

    t_pidx = sbuf("t_pidx", [128, 1])
    t_cst = sbuf("t_cst", [128, 2])
    b_cst = Buf("cst")
    nc.const_aps.register(F32, 1e-5, t_cst[:, 0:1])
    nc.const_aps.register(F32, LN8, t_cst[:, 1:2])
    t_I128 = sbuf("t_I128", [128, 128], BF16)
    t_Psig = sbuf("t_Psig", [128, 128], BF16)
    t_mTup = sbuf("t_mTup", [128, 2, 128], BF16)
    t_ones1 = sbuf("t_ones1", [1, 128], BF16)
    t_onesc = sbuf("t_onesc", [128, 1], BF16)
    t_Bhd = sbuf("t_Bhd", [16, nhp, 128], BF16)
    t_Ohp = sbuf("t_Ohp", [128, nhp, 16], BF16)
    t_sm1 = sbuf("t_sm1", [128, 1]); t_sm2 = sbuf("t_sm2", [128, 1])
    t_sm3 = sbuf("t_sm3", [128, 1]); t_sm4 = sbuf("t_sm4", [128, 1])
    t_den = sbuf("t_den", [128, 16])
    b_den0 = Buf("den0"); b_den1 = Buf("den1"); b_den = b_den0
    t_mx = sbuf("t_mx", [128, 4])
    b_mx0 = Buf("mx0"); b_mx1 = Buf("mx1"); b_mx = b_mx0
    t_bias = sbuf("t_bias", [128, 1]); b_bias = Buf("bias")
    t_inv = sbuf("t_inv", [16, 2 * TT], BF16); b_inv = Buf("inv")
    b_const = Buf("const")
    b_tab = Buf("tab")

    AR1 = sbuf("AR1", [128, 16392])
    AR2 = sbuf("AR2", [128, 16896])
    AR3 = sbuf("AR3", [128, 13824])
    AR4 = sbuf("AR4", [128, 4096])

    def view(arena, off, n, shape, dt=F32):
        ap = arena[:][:, off:off + n]
        if dt == BF16:
            ap = ap.bitcast(BF16)
        elif dt == I32:
            ap = ap.bitcast(I32) if hasattr(ap, "bitcast") else ap
        if len(shape) == 2:
            if shape[0] != 128:
                ap = ap[0:shape[0], :]
        elif len(shape) == 3:
            ap = ap.rearrange("p (a b) -> p a b", a=shape[1], b=shape[2])
            if shape[0] != 128:
                ap = ap[0:shape[0], :, :]
        return ap

    y = view(AR1, 0, ncnk * (T + 1), [128, ncnk, T + 1]); b_y = Buf("y")
    o_qf = view(AR1, 0, 4 * T, [128, nhp, T], BF16)
    o_kf = view(AR1, 4 * T, 4 * T, [128, nhp, T], BF16)
    b_QF = Buf("R_qf"); b_KF = Buf("R_kf")
    YBUFS = [b_y, b_QF, b_KF]
    bqb = [Buf(f"qb{h}") for h in range(nhp)]
    ybufs = [Buf(f"y{t}") for t in range(ntt)]
    YALL = YBUFS + ybufs

    def YR(tau):
        r = [ybufs[tau]]
        if tau > 0:
            r.append(ybufs[tau - 1])
        return r

    wq = view(AR2, 0, 4096, [128, ncnk, d], BF16); b_wq = Buf("R_wq")
    wk = view(AR2, 4096, 4096, [128, ncnk, d], BF16); b_wk = Buf("R_wk")
    wv = view(AR2, 8192, 4096, [128, ncnk, d], BF16); b_wv = Buf("R_wv")
    wo = view(AR2, 12288, 4096, [128, ncnk, d], BF16); b_wo = Buf("R_wo")
    w1v = view(AR2, 0, 11264, [128, ncnk, 2816], BF16)
    w2v = view(AR2, 11264, 5632, [128, 11, 1024], BF16)
    W1BUFS = [b_wq, b_wk, b_wv]
    W2BUFS = [b_wv, b_wo]

    xs = view(AR3, 0, 2048, [128, ncnk, TT], BF16); b_XS = Buf("R_xs")
    qp = view(AR3, 0, 2048, [128, ncnk, TT], BF16)
    hv_lo = view(AR3, 0, 2048, [128, ncnk, TT], BF16)
    qb = view(AR3, 2048, 2048, [128, ncnk, TT], BF16); b_QB = Buf("R_qb")
    v1 = view(AR3, 2048, 512, [128, TT])
    stv = view(AR3, 2560, 1024, [1, 2, TT])
    hv_hi = view(AR3, 2048, 768, [128, 3, TT], BF16)
    gel = view(AR3, 2816, 256, [128, TT], BF16)
    kr = view(AR3, 4096, 3072, [128, ncnk, 768], BF16); b_KR = Buf("R_kr")
    xf = view(AR3, 4096, 1536, [128, 3, TT])
    vr = view(AR3, 7168, 3072, [128, 6, d], BF16); b_VR = Buf("R_vr")
    ao = view(AR3, 10240, 2048, [128, ncnk, TT], BF16); b_AO = Buf("R_ao")
    xnf = view(AR3, 4096, ncnk * T // 2, [128, ncnk, T], BF16)
    ptA = view(AR3, 12288, 192, [128, 384], BF16); b_ptA = Buf("R_ptA")
    ptB = view(AR3, 12480, 192, [128, 384], BF16); b_ptB = Buf("R_ptB")
    ptTA = view(AR3, 12672, 192, [128, 3, 128], BF16); b_ptTA = Buf("R_ptTA")
    ptTB = view(AR3, 12864, 192, [128, 3, 128], BF16); b_ptTB = Buf("R_ptTB")
    tt1 = view(AR3, 13056, 256, [128, TT], BF16); b_T1 = Buf("R_t1")
    tu2 = view(AR3, 13312, 256, [128, TT], BF16); b_U2 = Buf("R_u2")
    sq = view(AR3, 13568, 256, [128, TT], BF16); b_SQ = Buf("R_sq")
    stb = view(AR3, 3584, 512, [1, 2, TT], BF16); b_STB = b_QB
    o_xnf = view(AR3, 0, ncnk * T // 2, [128, ncnk, T], BF16)
    o_vfa = view(AR3, 8192, 5632, [128, 11, d], BF16)
    O_XNF = [b_XS, b_QB, b_KR]
    O_XNF2 = [b_XS, b_QB, b_KR, b_VR]
    O_VFA = [b_VR, b_AO, b_ptA, b_ptB, b_ptTA, b_ptTB, b_T1, b_U2, b_SQ]
    o_xf = view(AR3, 8192, 1536, [128, 3, TT])
    o_v1 = view(AR3, 11776, 512, [128, TT])
    o_stv = view(AR3, 10240, 1024, [1, 2, TT])
    o_stb = view(AR3, 12288, 512, [1, 2, TT], BF16)
    o_sq = view(AR3, 13568, 256, [128, TT], BF16)
    o_ptT = view(AR3, 2048, 256, [128, 4, 128], BF16)

    tb_csS = view(AR4, 0, 1024, [128, 2048], BF16)
    tb_snS = view(AR4, 1024, 1024, [128, 2048], BF16)
    tb_csI = view(AR4, 2048, 1024, [128, 2048], BF16)
    tb_snI = view(AR4, 3072, 1024, [128, 2048], BF16)
    b_A4 = Buf("R_a4")
    o_vfb = view(AR4, 0, 2560, [128, 5, d], BF16)
    o_aot = view(AR4, 2560, 512, [128, ncnk, 128], BF16)
    o_ptf = view(AR4, 3072, T // 2, [128, T], BF16)
    b_VFB = b_A4; b_AOT = b_A4; b_PTF = b_A4

    def build_constants():
        g = "gp"
        t_colid = view(AR3, 512, 128, [128, 128])
        S.emit(g, lambda e: e.memset(t_cst[:, 0:1], 1e-5), writes=[b_const])
        S.emit(g, lambda e: e.memset(t_cst[:, 1:2], LN8), writes=[b_const])
        S.emit(g, lambda e: e.iota(t_pidx[:], pattern=[[0, 1]], base=0,
                                   channel_multiplier=1, allow_small_or_imprecise_dtypes=True),
               writes=[b_const])
        S.emit(g, lambda e: e.iota(t_colid[:], pattern=[[1, 128]], base=0,
                                   channel_multiplier=0, allow_small_or_imprecise_dtypes=True),
               writes=[b_const])
        S.emit("dve", lambda e: e.tensor_single_scalar(t_I128[:], t_colid[:], t_pidx[:, 0:1], ALU.is_equal),
               reads=[b_const], writes=[b_const])
        S.emit("dve", lambda e: e.tensor_single_scalar(t_sm1[:], t_pidx[:], 63.5, ALU.is_gt),
               reads=[b_const], writes=[b_const])
        S.emit("dve", lambda e: e.scalar_tensor_tensor(t_sm2[:], t_sm1[:], -64.0, t_pidx[:], ALU.mult, ALU.add),
               reads=[b_const], writes=[b_const])
        S.emit("dve", lambda e: e.tensor_single_scalar(t_sm3[:], t_sm2[:], 31.5, ALU.is_gt),
               reads=[b_const], writes=[b_const])
        S.emit("dve", lambda e: e.scalar_tensor_tensor(t_sm4[:], t_sm3[:], -64.0, t_pidx[:], ALU.mult, ALU.add),
               reads=[b_const], writes=[b_const])
        S.emit("dve", lambda e: e.tensor_scalar_add(t_sm4[:], t_sm4[:], 32.0),
               reads=[b_const], writes=[b_const])
        S.emit("dve", lambda e: e.tensor_single_scalar(t_Psig[:], t_colid[:], t_sm4[:, 0:1], ALU.is_equal),
               reads=[b_const], writes=[b_const])
        scr = view(AR3, 0, 128, [128, 128])
        S.emit("dve", lambda e: e.tensor_single_scalar(scr[:], t_colid[:], t_pidx[:, 0:1], ALU.is_lt),
               reads=[b_const], writes=[b_XS])
        S.emit("dve", lambda e: e.tensor_scalar_mul(t_mTup[:, 0, :], scr[:], NEG_BIG),
               reads=[b_XS], writes=[b_const])
        S.emit("dve", lambda e: e.tensor_single_scalar(scr[:], t_colid[:], t_pidx[:, 0:1], ALU.is_gt),
               reads=[b_const], writes=[b_XS])
        S.emit("dve", lambda e: e.tensor_scalar_mul(t_mTup[:, 1, :], scr[:], NEG_BIG),
               reads=[b_XS], writes=[b_const])
        S.emit(g, lambda e: e.memset(t_ones1[:], 1.0), writes=[b_const])
        S.emit(g, lambda e: e.memset(t_onesc[:], 1.0), writes=[b_const])
        scr2 = view(AR3, 128, 128, [128, 128])
        S.emit("dve", lambda e: e.tensor_single_scalar(scr[:], t_colid[:], 63.5, ALU.is_gt),
               reads=[b_const], writes=[b_XS])
        for hp in range(nhp):
            S.emit("dve", (lambda hp=hp: (lambda e: e.tensor_scalar_add(scr2[:], scr[:], float(2 * hp))))(),
                   reads=[b_XS], writes=[b_XS])
            S.emit("dve", (lambda hp=hp: (lambda e: e.tensor_single_scalar(
                t_Bhd[:, hp, :], scr2[0:16, :], t_pidx[0:16, 0:1], ALU.is_equal)))(),
                   reads=[b_XS, b_const], writes=[b_const])
        scr3 = view(AR3, 256, 16, [128, 16])
        S.emit(g, lambda e: e.iota(scr3[:], pattern=[[1, 16]], base=0,
                                   channel_multiplier=0, allow_small_or_imprecise_dtypes=True),
               writes=[b_XS])
        S.emit("dve", lambda e: e.tensor_single_scalar(
            scr2[:, 0:16], scr3[:], t_sm1[:, 0:1], ALU.subtract),
               reads=[b_XS, b_const], writes=[b_XS])
        for hp in range(nhp):
            S.emit("dve", (lambda hp=hp: (lambda e: e.tensor_single_scalar(
                t_Ohp[:, hp, :], scr2[:, 0:16], float(2 * hp), ALU.is_equal)))(),
                   reads=[b_XS], writes=[b_const])
        S.emit("dve", lambda e: e.tensor_scalar_mul(t_sm3[:], t_sm3[:], 2.0), reads=[b_const], writes=[b_const])
        S.emit("dve", lambda e: e.tensor_scalar_add(t_sm3[:], t_sm3[:], -1.0), reads=[b_const], writes=[b_const])
        S.emit("dve", lambda e: e.tensor_single_scalar(t_sm1[:], t_sm2[:], 31.5, ALU.is_gt),
               reads=[b_const], writes=[b_const])
        S.emit("dve", lambda e: e.scalar_tensor_tensor(t_sm4[:], t_sm1[:], -32.0, t_sm2[:], ALU.mult, ALU.add),
               reads=[b_const], writes=[b_const])
        S.emit("act", lambda e: e.activation(t_sm1[:], t_sm4[:], AF.Exp, scale=float(-np.log(10000.0) / 32.0)),
               reads=[b_const], writes=[b_const])
        S.emit("dve", lambda e: e.tensor_scalar_mul(t_sm2[:], t_sm4[:], 2.0 / 89.6), reads=[b_const], writes=[b_const])
        S.emit("dve", lambda e: e.tensor_scalar_add(t_sm2[:], t_sm2[:], 25.6 / 89.6), reads=[b_const], writes=[b_const])
        S.emit("act", lambda e: e.activation(t_sm2[:], t_sm2[:], AF.Ln), reads=[b_const], writes=[b_const])

    def build_tables():
        s0 = view(AR3, 0, 2048, [128, 2048])
        s1 = view(AR3, 2048, 2048, [128, 2048])
        si32 = view(AR3, 4096, 2048, [128, 2048], I32)
        sifl = view(AR3, 4096, 2048, [128, 2048])
        scl = view(AR3, 6144, 2048, [128, 2048])
        SCR = [b_XS, b_QB, b_KR, b_VR]
        g = "gp"
        S.emit(g, lambda e: e.iota(s0, pattern=[[1, 2048]], base=0,
                                   channel_multiplier=0, allow_small_or_imprecise_dtypes=True),
               writes=SCR)
        S.emit("dve", lambda e: e.tensor_single_scalar(s0, s0, t_sm1[:, 0:1], ALU.mult),
               reads=SCR + [b_const], writes=SCR)

        def sincos(dst, shift):
            S.emit("dve", lambda e: e.tensor_scalar_mul(s1, s0, 1.0 / TWO_PI), reads=SCR, writes=SCR)
            if shift:
                S.emit("dve", lambda e: e.tensor_scalar_add(s1, s1, 0.25), reads=SCR, writes=SCR)
            S.emit("dve", lambda e: e.tensor_copy(si32, s1), reads=SCR, writes=SCR)
            S.emit("dve", lambda e: e.tensor_copy(sifl, si32), reads=SCR, writes=SCR)
            S.emit("dve", lambda e: e.tensor_sub(s1, s1, sifl), reads=SCR, writes=SCR)
            S.emit("dve", lambda e: e.tensor_single_scalar(sifl, s1, 0.5, ALU.is_gt), reads=SCR, writes=SCR)
            S.emit("dve", lambda e: e.tensor_sub(s1, s1, sifl), reads=SCR, writes=SCR)
            S.emit("act", (lambda dst=dst: (lambda e: e.activation(dst, s1, AF.Sin, scale=TWO_PI)))(),
                   reads=SCR, writes=[b_A4])

        sincos(tb_snS, False)
        sincos(tb_csS, True)
        S.emit(g, lambda e: e.iota(sifl, pattern=[[1, 2048]], base=-1024,
                                   channel_multiplier=0, allow_small_or_imprecise_dtypes=True),
               writes=SCR)
        S.emit("dve", lambda e: e.tensor_scalar_mul(s0, sifl, 1.0 / 128.0), reads=SCR, writes=SCR)
        S.emit("dve", lambda e: e.tensor_single_scalar(s0, s0, t_sm2[:, 0:1], ALU.mult),
               reads=SCR + [b_const], writes=SCR)
        S.emit("act", lambda e: e.activation(scl, s0, AF.Exp), reads=SCR, writes=SCR)
        S.emit("act", lambda e: e.activation(s1, s0, AF.Exp, scale=-1.0), reads=SCR, writes=SCR)
        S.emit("dve", lambda e: e.tensor_mul(tb_snI, tb_snS, s1), reads=SCR + [b_A4], writes=[b_A4])
        S.emit("dve", lambda e: e.tensor_single_scalar(tb_snI, tb_snI, t_sm3[:, 0:1], ALU.mult),
               reads=[b_A4, b_const], writes=[b_A4])
        S.emit("dve", lambda e: e.tensor_mul(tb_csI, tb_csS, s1), reads=SCR + [b_A4], writes=[b_A4])
        S.emit("dve", lambda e: e.tensor_mul(tb_snS, tb_snS, scl), reads=SCR + [b_A4], writes=[b_A4])
        S.emit("dve", lambda e: e.tensor_single_scalar(tb_snS, tb_snS, t_sm3[:, 0:1], ALU.mult),
               reads=[b_A4, b_const], writes=[b_A4])
        S.emit("dve", lambda e: e.tensor_mul(tb_csS, tb_csS, scl), reads=SCR + [b_A4], writes=[b_A4])

    build_constants()

    def ln_stats(xap, RD, sqv, b_sqv, stvv, b_stvv, stbv, b_stbv,
                 p_s, bp_s, p_r, bp_r, p_m, bp_m):
        fns = []
        for c in range(ncnk):
            fns.append((lambda c=c: (lambda e: e.matmul(p_s[0:1, 0:TT], t_onesc[:], xap[:, c, :],
                                                        start=(c == 0), stop=(c == ncnk - 1))))())
        S.emit_group("pe", fns, reads=RD + [b_const], writes=[bp_s])
        for c in range(ncnk):
            S.emit("act", (lambda c=c: (lambda e: e.activation(sqv[:], xap[:, c, :], AF.Square)))(),
                   reads=RD, writes=[b_sqv])
            S.emit("pe", (lambda c=c: (lambda e: e.matmul(p_s[32:33, 0:TT], t_onesc[:], sqv[:],
                                                          start=(c == 0), stop=(c == ncnk - 1))))(),
                   reads=[b_sqv, b_const], writes=[bp_s])
        S.emit("dve", lambda e: e.tensor_scalar_mul(stvv[:, 0, :], p_s[0:1, 0:TT], -1.0 / d),
               reads=[bp_s], writes=[b_stvv])
        S.emit("dve", lambda e: e.tensor_mul(stvv[:, 1, :], stvv[:, 0, :], stvv[:, 0, :]),
               reads=[b_stvv], writes=[b_stvv])
        S.emit("dve", lambda e: e.scalar_tensor_tensor(stvv[:, 1, :], p_s[32:33, 0:TT], 1.0 / d,
                                                       stvv[:, 1, :], ALU.mult, ALU.subtract),
               reads=[bp_s, b_stvv], writes=[b_stvv])
        S.emit("act", lambda e: e.activation(stvv[:, 1, :], stvv[:, 1, :], AF.Ln, bias=1e-5),
               reads=[b_stvv, b_const], writes=[b_stvv])
        S.emit("act", lambda e: e.activation(stbv[:, 0, :], stvv[:, 1, :], AF.Exp, scale=-0.5),
               reads=[b_stvv], writes=[b_stbv])
        S.emit("dve", lambda e: e.tensor_mul(stbv[:, 1, :], stvv[:, 0, :], stbv[:, 0, :]),
               reads=[b_stvv, b_stbv], writes=[b_stbv])
        S.emit("pe", lambda e: e.matmul(p_r[:, 0:TT], t_ones1[:], stbv[:, 0, :], start=True, stop=True),
               reads=[b_stbv, b_const], writes=[bp_r])
        S.emit("pe", lambda e: e.matmul(p_m[:, 0:TT], t_ones1[:], stbv[:, 1, :], start=True, stop=True),
               reads=[b_stbv, b_const], writes=[bp_m])

    def phase_F(seq, xin_dram, xout_dram):
        b_xin = DBUF[id(xin_dram)]
        b_xout = DBUF[id(xout_dram)]
        pes = ExitStack()
        build._uid = getattr(build, "_uid", 0) + 1
        uid = f"u{build._uid}"
        psum = lambda name, shape, dt=F32: pes.enter_context(nc.psum_tensor(name + uid, shape, dt))
        p0 = psum("p0", [128, 512]); bp0 = Buf("p0")
        p1 = psum("p1", [128, 512]); bp1 = Buf("p1")
        p2 = psum("p2", [128, 512]); bp2 = Buf("p2")
        p3 = psum("p3", [128, 512]); bp3 = Buf("p3")
        pS2A = psum("pS2A", [128, 1024]); bpS2A = Buf("pS2A")
        pS2B = psum("pS2B", [128, 1024]); bpS2B = Buf("pS2B")
        p2bf = p2[:].bitcast(BF16)
        p3bf = p3[:].bitcast(BF16)
        pT = p2bf; bpT = Buf("pT") if False else None
        rstdv = pS2A[:][:, 0:512]
        murstdv = pS2B[:][:, 0:512]

        build_tables()
        S.emit("gp", lambda e: e.memset(y[:, :, 0:1], 0.0), writes=YALL)

        def ln_apply_to(dst_writer, src_dram):
            for tau in range(ntt):
                t0 = tau * TT
                for cg in range(3):
                    n = min(3, ncnk - cg * 3)
                    S.emit("sp", (lambda t0=t0, cg=cg, n=n: (lambda e: e.dma_start(
                        out=xf[:, 0:n, :],
                        in_=src_dram[seq, cg * 3:cg * 3 + n, :, t0:t0 + TT].rearrange("c p n -> p c n"))))(),
                           reads=[b_xin], writes=[b_KR], dma=True)
                    for j in range(n):
                        c = cg * 3 + j
                        S.emit("dve", (lambda c=c, j=j: (lambda e: e.tensor_copy(xs[:, c, :], xf[:, j, :])))(),
                               reads=[b_KR], writes=[b_XS])
                ln_stats(xs, [b_XS], sq, b_SQ, stv, b_QB, stb, b_STB, p3, bp3, rstdv, bpS2A, murstdv, bpS2B)
                for cg in range(3):
                    n = min(3, ncnk - cg * 3)
                    S.emit("sp", (lambda t0=t0, cg=cg, n=n: (lambda e: e.dma_start(
                        out=xf[:, 0:n, :],
                        in_=src_dram[seq, cg * 3:cg * 3 + n, :, t0:t0 + TT].rearrange("c p n -> p c n"))))(),
                           reads=[b_xin], writes=[b_KR], dma=True)
                    for j in range(n):
                        c = cg * 3 + j
                        S.emit("dve", (lambda c=c, j=j: (lambda e: e.tensor_mul(v1[:], xf[:, j, :], rstdv[:, 0:TT])))(),
                               reads=[b_KR, bpS2A], writes=[b_QB])
                        dst_writer(c, t0)

        def ywr(c, t0):
            S.emit("dve", (lambda c=c, t0=t0: (lambda e: e.tensor_add(
                y[:, c, 1 + t0:1 + t0 + TT], v1[:], murstdv[:, 0:TT])))(),
                   reads=[b_QB, bpS2B], writes=[ybufs[t0 // TT]])
        ln_apply_to(ywr, xin_dram)

        P = dict(p0=p0, bp0=bp0, p1=p1, bp1=bp1, p2=p2, bp2=bp2, p3=p3, bp3=bp3,
                 pS2A=pS2A, bpS2A=bpS2A, pS2B=pS2B, bpS2B=bpS2B,
                 p2bf=p2bf, p3bf=p3bf, rstdv=rstdv, murstdv=murstdv)
        for li in range(cfg.depth):
            attn_sublayer(seq, li, P)
            if not cfg.no_ffn:
                ffn_sublayer(seq, li, P)

        for tau in range(ntt):
            t0 = tau * TT
            for cg in range(3):
                n = min(3, ncnk - cg * 3)
                S.emit("sp", (lambda t0=t0, cg=cg, n=n: (lambda e: e.dma_start(
                    out=xf[:, 0:n, :],
                    in_=xin_dram[seq, cg * 3:cg * 3 + n, :, t0:t0 + TT].rearrange("c p n -> p c n"))))(),
                       reads=[b_xin], writes=[b_KR], dma=True)
                for j in range(n):
                    c = cg * 3 + j
                    S.emit("dve", (lambda c=c, j=j, t0=t0: (lambda e: e.scalar_tensor_tensor(
                        xf[:, j, :], y[:, c, 1 + t0:1 + t0 + TT], 0.5, xf[:, j, :], ALU.mult, ALU.add)))(),
                           reads=[ybufs[t0 // TT]], writes=[b_KR])
                S.emit("sp", (lambda t0=t0, cg=cg, n=n: (lambda e: e.dma_start(
                    out=xout_dram[seq, cg * 3:cg * 3 + n, :, t0:t0 + TT].rearrange("c p n -> p c n"),
                    in_=xf[:, 0:n, :])))(),
                       reads=[b_KR], writes=[b_xout], dma=True)
        pes.close()

    def attn_sublayer(seq, li, P):
        p0, bp0, p1, bp1, p2, bp2, p3, bp3 = (
            P["p0"], P["bp0"], P["p1"], P["bp1"], P["p2"], P["bp2"], P["p3"], P["bp3"])
        pS2A, bpS2A, pS2B, bpS2B = P["pS2A"], P["bpS2A"], P["pS2B"], P["bpS2B"]
        pS2Bbf = pS2B[:].bitcast(BF16)
        tpA = pS2Bbf[:, 0:384]
        tpB = pS2Bbf[:, 1024:1408]
        S.link(bqb, [b_QB])
        for tens, bf, base in ((wq, b_wq, 0), (wk, b_wk, d), (wv, b_wv, 2 * d)):
            S.emit("gp", (lambda tens=tens, base=base: (lambda e: e.dma_start(
                out=tens[:, :, :], in_=dwqkv[li, :, base:base + d].rearrange("(c p) n -> p c n", p=128))))(),
                   writes=[bf], dma=True)
        S.emit("gp", lambda e: e.dma_start(out=wo[:, :, :],
                                           in_=dwo[li].rearrange("(c p) n -> p c n", p=128)),
               writes=[b_wo], dma=True)

        def prep_feed(tau):
            t0 = tau * TT
            F = []
            F.append((0, lambda t0=t0, tau=tau: S.emit("dve", (lambda t0=t0: (lambda e: e.tensor_copy(
                xs[:, 0:4, :], y[:, 0:4, 1 + t0:1 + t0 + TT])))(), reads=YR(tau), writes=[b_XS])))
            F.append((0, lambda t0=t0, tau=tau: S.emit("dve", (lambda t0=t0: (lambda e: e.tensor_copy(
                xs[:, 4:8, :], y[:, 4:8, t0:t0 + TT])))(), reads=YR(tau), writes=[b_XS])))
            for hp in range(nhp):
                pb_, bpb_ = (p2, bp2) if hp % 2 == 0 else (p3, bp3)
                def grp(hp=hp, pb_=pb_, bpb_=bpb_):
                    fns = []
                    for c in range(ncnk):
                        fns.append((lambda c=c, hp=hp, pb_=pb_: (lambda e: e.matmul(
                            pb_[:, 0:TT], wk[:, c, hp * 128:(hp + 1) * 128], xs[:, c, :],
                            start=(c == 0), stop=(c == ncnk - 1))))())
                    S.emit_group("pe", fns, reads=[b_XS, b_wk], writes=[bpb_])
                F.append((max(0, 4 * (hp - 2) + 4), grp))
                F.append((4 * hp + 4, (lambda hp=hp, pb_=pb_, bpb_=bpb_: (lambda: S.emit(
                    "act", (lambda hp=hp, pb_=pb_: (lambda e: e.activation(qb[:, hp, :], pb_[:, 0:TT], AF.Copy)))(),
                    reads=[bpb_], writes=[bqb[hp]])))()))
            for g in range(4):
                slot = (4 * tau + g) % 6
                for nh in range(2):
                    gate = 0 if (g == 0 and nh == 0) else 31 + g
                    def vgrp(g=g, nh=nh):
                        fns = []
                        for c in range(ncnk):
                            fns.append((lambda c=c, g=g, nh=nh: (lambda e: e.matmul(
                                p1[:, 0:512], xs[:, c, g * 128:(g + 1) * 128],
                                wv[:, c, nh * 512:(nh + 1) * 512],
                                start=(c == 0), stop=(c == ncnk - 1))))())
                        S.emit_group("pe", fns, reads=[b_XS, b_wv], writes=[bp1])
                    F.append((gate, vgrp))
                    F.append((31 + g, (lambda slot=slot, nh=nh: (lambda: S.emit(
                        "act", (lambda slot=slot, nh=nh: (lambda e: e.activation(
                            vr[:, slot, nh * 512:(nh + 1) * 512], p1[:, 0:512], AF.Copy)))(),
                        reads=[bp1], writes=[b_VR])))()))
            if tau > 0:
                F.append((30, lambda: S.emit("gp", lambda e: e.tensor_copy(
                    kr[:, :, 0:256], kr[:, :, 512:768]), reads=[b_KR], writes=[b_KR])))
            return F

        def side_tail(tau, side):
            t0 = tau * TT
            smap = t_qsm[:, li:li + 1] if side == 0 else t_ksm[:, li:li + 1]
            ssap = t_qss[:, li:li + 1] if side == 0 else t_kss[:, li:li + 1]
            cst = tb_csS if side == 0 else tb_csI
            snt = tb_snS if side == 0 else tb_snI
            for hp in range(nhp):
                S.emit("act", (lambda hp=hp: (lambda e: e.activation(sq[:], qb[:, hp, :], AF.Square)))(),
                       reads=[bqb[hp]], writes=[b_SQ])
                S.emit("pe", (lambda hp=hp: (lambda e: e.matmul(
                    p1[0:16, 0:TT], t_Ohp[:, hp, :], sq[:],
                    start=(hp == 0), stop=(hp == nhp - 1))))(),
                       reads=[b_SQ, b_const], writes=[bp1])
            S.emit("dve", lambda e: e.tensor_scalar_max(p1[0:16, 0:TT], p1[0:16, 0:TT], 1e-18),
                   reads=[bp1], writes=[bp1])
            S.emit("act", lambda e: e.activation(p2[0:16, 0:TT], p1[0:16, 0:TT], AF.Ln),
                   reads=[bp1], writes=[bp2])
            if side == 0:
                S.emit("act", lambda e: e.activation(t_inv[:, 0:TT], p2[0:16, 0:TT], AF.Exp, scale=-0.5, bias=LN8),
                       reads=[bp2, b_const], writes=[b_inv])
            else:
                S.emit("act", lambda e: e.activation(t_inv[:, TT:2 * TT], p2[0:16, 0:TT], AF.Exp, scale=-0.5),
                       reads=[bp2], writes=[b_inv])
            colb = 0 if side == 0 else TT
            for hp in range(nhp):
                S.emit("pe", (lambda hp=hp: (lambda e: e.matmul(p2[:, 0:TT], t_Psig[:], qb[:, hp, :],
                                                                start=True, stop=True)))(),
                       reads=[bqb[hp], b_const], writes=[bp2])
                S.emit("dve", (lambda hp=hp, smap=smap, cst=cst, t0=t0: (lambda e: e.scalar_tensor_tensor(
                    tt1[:], qb[:, hp, :], smap, cst[:, t0:t0 + TT], ALU.mult, ALU.mult)))(),
                       reads=[bqb[hp], b_A4, b_qsm], writes=[b_T1])
                S.emit("dve", (lambda ssap=ssap, snt=snt, t0=t0: (lambda e: e.scalar_tensor_tensor(
                    tu2[:], p2[:, 0:TT], ssap, snt[:, t0:t0 + TT], ALU.mult, ALU.mult)))(),
                       reads=[bp2, b_A4, b_qsm], writes=[b_U2])
                S.emit("dve", lambda e: e.tensor_add(tt1[:], tt1[:], tu2[:]),
                       reads=[b_T1, b_U2], writes=[b_T1])
                S.emit("pe", (lambda hp=hp, colb=colb: (lambda e: e.matmul(
                    p3[:, 0:TT], t_Bhd[:, hp, :], t_inv[:, colb:colb + TT], start=True, stop=True)))(),
                       reads=[b_inv, b_const], writes=[bp3])
                if side == 0:
                    S.emit("dve", (lambda hp=hp: (lambda e: e.tensor_mul(qb[:, hp, :], tt1[:], p3[:, 0:TT])))(),
                           reads=[b_T1, bp3], writes=[bqb[hp]])
                else:
                    S.emit("dve", (lambda hp=hp: (lambda e: e.tensor_mul(kr[:, hp, 256:768], tt1[:], p3[:, 0:TT])))(),
                           reads=[b_T1, bp3], writes=[b_KR])

        def q_loop1(tau):
            for hp in range(nhp):
                pb_, bpb_ = (p2, bp2) if hp % 2 == 0 else (p3, bp3)
                fns = []
                for c in range(ncnk):
                    fns.append((lambda c=c, hp=hp, pb_=pb_: (lambda e: e.matmul(
                        pb_[:, 0:TT], wq[:, c, hp * 128:(hp + 1) * 128], xs[:, c, :],
                        start=(c == 0), stop=(c == ncnk - 1))))())
                S.emit_group("pe", fns, reads=[b_XS, b_wq], writes=[bpb_])
                S.emit("act", (lambda hp=hp, pb_=pb_: (lambda e: e.activation(qb[:, hp, :], pb_[:, 0:TT], AF.Copy)))(),
                       reads=[bpb_], writes=[bqb[hp]])

        def sav_phase(tau, feed):
            feed = sorted(feed, key=lambda x: x[0])
            geo = []
            for gl in range(4):
                sub = 4 * tau + gl
                nchunk = min(sub + 1, 3)
                geo.append((gl, sub, nchunk, 128 * nchunk, (gl + 2 - (nchunk - 1)) * 128))

            fpos = [0]

            def pump(j):
                while fpos[0] < len(feed) and feed[fpos[0]][0] <= j:
                    feed[fpos[0]][1]()
                    fpos[0] += 1

            def pctx(j):
                hp = j // 4
                gl, sub, nchunk, W, ring0 = geo[j % 4]
                m = j % 2
                return dict(gl=gl, sub=sub, nchunk=nchunk, W=W, ring0=ring0, hp=hp, m=m,
                            mxs=t_mx[:, 2 * m:2 * m + 2],
                            b_mxs=b_mx0 if m == 0 else b_mx1,
                            b_dens=b_den0 if m == 0 else b_den1)

            def ps0(j):
                c = pctx(j)
                hp, gl, ring0, W, nchunk = c["hp"], c["gl"], c["ring0"], c["W"], c["nchunk"]
                fns = []
                for par in (0, 1):
                    off = par * 512
                    pb = par * 64
                    fns.append((lambda hp=hp, pb=pb, gl=gl, ring0=ring0, W=W, off=off:
                                (lambda e: e.matmul(
                                    pS2A[:, off:off + W], qb[pb:pb + 64, hp, gl * 128:(gl + 1) * 128],
                                    kr[pb:pb + 64, hp, ring0:ring0 + W], start=True, stop=False)))())
                    if nchunk == 3:
                        fns.append((lambda off=off: (lambda e: e.matmul(
                            pS2A[:, off:off + 128], t_I128[:], t_mTup[:, 0, :], start=False, stop=False)))())
                        fns.append((lambda off=off: (lambda e: e.matmul(
                            pS2A[:, off + 256:off + 384], t_I128[:], t_mTup[:, 1, :], start=False, stop=True)))())
                    else:
                        fns.append((lambda off=off, W=W: (lambda e: e.matmul(
                            pS2A[:, off + W - 128:off + W], t_I128[:], t_mTup[:, 1, :], start=False, stop=True)))())
                S.emit_group("pe", fns, reads=[bqb[hp], b_KR, b_const], writes=[bpS2A])

            def ps1(j):
                c = pctx(j)
                W = c["W"]
                pairview = pS2A[:][:, 0:1024].rearrange("p (a b) -> p a b", a=2, b=512)[:, :, 0:W]
                S.emit("dve", (lambda pv=pairview, mxs=c["mxs"]: (lambda e: e.reduce_max(
                    mxs, pv, axis=AX.X, negate=True)))(),
                       reads=[bpS2A], writes=[c["b_mxs"]])

            def ps23(j):
                c = pctx(j)
                W, hp, m = c["W"], c["hp"], c["m"]
                for par in (0, 1):
                    dc = 2 * m + par
                    pt_, bpt_ = (ptA, b_ptA) if par == 0 else (ptB, b_ptB)
                    S.emit("act", (lambda W=W, off=par * 512, pt_=pt_, dc=dc,
                                   mx_=c["mxs"][:, par:par + 1]: (lambda e: e.activation(
                        pt_[:, 0:W], pS2A[:, off:off + W], AF.Exp, bias=mx_,
                        accum_out=t_den[:, dc:dc + 1])))(),
                           reads=[bpS2A, c["b_mxs"]], writes=[bpt_, c["b_dens"]])
                S.emit("dve", (lambda m=m: (lambda e: e.reciprocal(
                    t_den[:, 2 * m:2 * m + 2], t_den[:, 2 * m:2 * m + 2])))(),
                       reads=[c["b_dens"]], writes=[c["b_dens"]])
                for par in (0, 1):
                    dc = 2 * m + par
                    pt_, bpt_ = (ptA, b_ptA) if par == 0 else (ptB, b_ptB)
                    S.emit("dve", (lambda W=W, pt_=pt_, dc=dc: (lambda e: e.tensor_single_scalar(
                        pt_[:, 0:W], pt_[:, 0:W], t_den[:, dc:dc + 1], ALU.mult)))(),
                           reads=[bpt_, c["b_dens"]], writes=[bpt_])

            def ps47(j):
                c = pctx(j)
                W, nchunk, sub, hp, gl = c["W"], c["nchunk"], c["sub"], c["hp"], c["gl"]
                for par in (0, 1):
                    pt_, bpt_ = (ptA, b_ptA) if par == 0 else (ptB, b_ptB)
                    pTb_ = tpA if par == 0 else tpB
                    fns = []
                    for cc in range(nchunk):
                        fns.append((lambda cc=cc, pt_=pt_, pTb_=pTb_: (lambda e: e.transpose(
                            pTb_[:, cc * 128:(cc + 1) * 128], pt_[:, cc * 128:(cc + 1) * 128], t_I128[:])))())
                    S.emit_group("pe", fns, reads=[bpt_, b_const], writes=[bpS2B])
                    ptT_, bptT_ = (ptTA, b_ptTA) if par == 0 else (ptTB, b_ptTB)
                    if par == 0:
                        S.emit("act", (lambda W=W, ptT_=ptT_, pTb_=pTb_: (lambda e: e.activation(
                            ptT_[:, :, :].rearrange("p a b -> p (a b)")[:, 0:W], pTb_[:, 0:W], AF.Copy)))(),
                               reads=[bpS2B], writes=[bptT_])
                    else:
                        S.emit("dve", (lambda W=W, ptT_=ptT_, pTb_=pTb_: (lambda e: e.tensor_copy(
                            ptT_[:, :, :].rearrange("p a b -> p (a b)")[:, 0:W], pTb_[:, 0:W])))(),
                               reads=[bpS2B], writes=[bptT_])
                for par in (0, 1):
                    h = 2 * hp + par
                    ptT_, bptT_ = (ptTA, b_ptTA) if par == 0 else (ptTB, b_ptTB)
                    fns = []
                    for cc in range(nchunk):
                        gslot = (sub - (nchunk - 1) + cc) % 6
                        fns.append((lambda h=h, cc=cc, gslot=gslot, par=par, nchunk=nchunk, ptT_=ptT_:
                                    (lambda e: e.matmul(
                                        p0[par * 64:par * 64 + 64, 0:128], vr[:, gslot, h * 64:(h + 1) * 64],
                                        ptT_[:, cc, :], start=(cc == 0), stop=(cc == nchunk - 1),
                                        tile_position=(0, par * 64))))())
                    S.emit_group("pe", fns, reads=[b_VR, bptT_], writes=[bp0])
                S.emit("act", (lambda hp=hp, gl=gl: (lambda e: e.activation(
                    ao[:, hp, gl * 128:(gl + 1) * 128], p0[:, 0:128], AF.Copy)))(),
                       reads=[bp0], writes=[b_AO])

            NJ = 4 * nhp
            for j in range(NJ + 2):
                if 0 <= j - 2 < NJ:
                    ps47(j - 2)
                if 0 <= j - 1 < NJ:
                    ps23(j - 1)
                if j < NJ:
                    ps0(j)
                    ps1(j)
                pump(j)
            while fpos[0] < len(feed):
                feed[fpos[0]][1]()
                fpos[0] += 1

        def wo_resid(tau):
            t0 = tau * TT
            for m in range(ncnk):
                pb_, bpb_ = (p2, bp2) if m % 2 == 0 else (p3, bp3)
                fns = []
                for c in range(ncnk):
                    fns.append((lambda m=m, c=c, pb_=pb_: (lambda e: e.matmul(
                        pb_[:, 0:TT], wo[:, c, m * 128:(m + 1) * 128], ao[:, c, :],
                        start=(c == 0), stop=(c == ncnk - 1))))())
                S.emit_group("pe", fns, reads=[b_AO, b_wo], writes=[bpb_])
                S.emit("dve", (lambda m=m, t0=t0, pb_=pb_: (lambda e: e.tensor_add(
                    y[:, m, 1 + t0:1 + t0 + TT], pb_[:, 0:TT], y[:, m, 1 + t0:1 + t0 + TT])))(),
                       reads=[bpb_, ybufs[tau]], writes=[ybufs[tau]])

        for g, th in prep_feed(0):
            th()
        side_tail(0, 1)
        q_loop1(0)
        side_tail(0, 0)
        for tau in range(ntt):
            feed = prep_feed(tau + 1) if tau + 1 < ntt else []
            sav_phase(tau, feed)
            wo_resid(tau)
            if tau + 1 < ntt:
                side_tail(tau + 1, 1)
                q_loop1(tau + 1)
                side_tail(tau + 1, 0)
        S.link([b_QB], bqb)

    def ffn_sublayer(seq, li, P):
        p0, bp0, p1, bp1, p2, bp2, p3, bp3 = (
            P["p0"], P["bp0"], P["p1"], P["bp1"], P["p2"], P["bp2"], P["p3"], P["bp3"])
        rstdv, murstdv = P["rstdv"], P["murstdv"]
        bpS2A, bpS2B = P["bpS2A"], P["bpS2B"]
        g_ap = lambda c: t_gh[:, li, c:c + 1]
        bias_ap = lambda c: t_bh[:, li, c:c + 1]
        b2_ap = lambda c: t_b2[:, li, c:c + 1]
        XNB = [b_KR, b_VR, b_AO]

        def hv(kk):
            return hv_lo[:, kk, :] if kk < 8 else hv_hi[:, kk - 8, :]
        HVB = [b_XS, b_QB]

        for tau in range(ntt):
            t0 = tau * TT
            S.emit("dve", (lambda t0=t0: (lambda e: e.tensor_copy(xs[:, 0:4, :], y[:, 0:4, 1 + t0:1 + t0 + TT])))(),
                   reads=YR(tau), writes=[b_XS])
            S.emit("dve", (lambda t0=t0: (lambda e: e.tensor_copy(xs[:, 4:8, :], y[:, 4:8, t0:t0 + TT])))(),
                   reads=YR(tau), writes=[b_XS])
            ln_stats(xs, [b_XS], sq, b_SQ, stv, b_QB, stb, b_STB, p3, bp3, rstdv, bpS2A, murstdv, bpS2B)
            for c in range(ncnk):
                off = 1 + t0 if c < 4 else t0
                S.emit("dve", (lambda c=c, off=off: (lambda e: e.scalar_tensor_tensor(
                    v1[:], y[:, c, off:off + TT], g_ap(c), rstdv[:, 0:TT], ALU.mult, ALU.mult)))(),
                       reads=YR(tau) + [bpS2A, b_gh], writes=[b_QB])
                S.emit("dve", (lambda c=c: (lambda e: e.scalar_tensor_tensor(
                    v1[:], murstdv[:, 0:TT], g_ap(c), v1[:], ALU.mult, ALU.add)))(),
                       reads=[bpS2B, b_QB, b_gh], writes=[b_QB])
                S.emit("act", (lambda c=c, t0=t0: (lambda e: e.activation(
                    xnf[:, c, t0:t0 + TT], v1[:], AF.Identity, bias=bias_ap(c))))(),
                       reads=[b_QB, b_bh], writes=XNB)
        for half in range(2):
            S.emit("gp", (lambda half=half: (lambda e: e.dma_start(
                out=w1v[:, :, :], in_=dw1[li, :, half * 2816:(half + 1) * 2816].rearrange("(c p) n -> p c n", p=128))))(),
                   writes=W1BUFS, dma=True)
            S.emit("gp", (lambda half=half: (lambda e: e.dma_start(
                out=w2v[:, :, :], in_=dw2[li, half * 1408:(half + 1) * 1408, :].rearrange("(c p) n -> p c n", p=128))))(),
                   writes=W2BUFS, dma=True)
            for tau in range(ntt):
                t0 = tau * TT
                for mm in range(11):
                    fns = []
                    for c in range(ncnk):
                        fns.append((lambda mm=mm, c=c, t0=t0: (lambda e: e.matmul(
                            p1[:, 0:TT], w1v[:, c, 1408 + mm * 128:1408 + (mm + 1) * 128],
                            xnf[:, c, t0:t0 + TT], start=(c == 0), stop=(c == ncnk - 1))))())
                    S.emit_group("pe", fns, reads=XNB + W1BUFS, writes=[bp1])
                    S.emit("act", (lambda mm=mm, half=half: (lambda e: e.activation(
                        gel[:, :], p1[:, 0:TT], AF.Gelu, bias=t_b1g[:, li, half * 11 + mm:half * 11 + mm + 1])))(),
                           reads=[bp1, b_b1g], writes=[b_QB])
                    fns = []
                    for c in range(ncnk):
                        fns.append((lambda mm=mm, c=c, t0=t0: (lambda e: e.matmul(
                            p2[:, 0:TT], w1v[:, c, mm * 128:(mm + 1) * 128],
                            xnf[:, c, t0:t0 + TT], start=(c == 0), stop=(c == ncnk - 1))))())
                    S.emit_group("pe", fns, reads=XNB + W1BUFS, writes=[bp2])
                    S.emit("dve", (lambda mm=mm, half=half: (lambda e: e.scalar_tensor_tensor(
                        hv(mm), p2[:, 0:TT], t_b1a[:, li, half * 11 + mm:half * 11 + mm + 1],
                        gel[:, :], ALU.add, ALU.mult)))(),
                           reads=[bp2, b_QB, b_b1a], writes=HVB)
                for m in range(ncnk):
                    fns = []
                    for kk in range(11):
                        fns.append((lambda m=m, kk=kk: (lambda e: e.matmul(
                            p0[:, 0:TT], w2v[:, kk, m * 128:(m + 1) * 128], hv(kk),
                            start=(kk == 0), stop=(kk == 10))))())
                    S.emit_group("pe", fns, reads=HVB + W2BUFS, writes=[bp0])
                    if half == 0:
                        S.emit("dve", (lambda m=m, t0=t0: (lambda e: e.tensor_add(
                            y[:, m, 1 + t0:1 + t0 + TT], p0[:, 0:TT], y[:, m, 1 + t0:1 + t0 + TT])))(),
                               reads=[bp0, ybufs[tau]], writes=[ybufs[tau]])
                    else:
                        S.emit("dve", (lambda m=m, t0=t0: (lambda e: e.scalar_tensor_tensor(
                            y[:, m, 1 + t0:1 + t0 + TT], p0[:, 0:TT], b2_ap(m),
                            y[:, m, 1 + t0:1 + t0 + TT], ALU.add, ALU.add)))(),
                               reads=[bp0, b_b2, ybufs[tau]], writes=[ybufs[tau]])

    def phase_outer(seq):
        S.link([b_QF, b_KF], ybufs)
        S.link(ybufs, [b_QF, b_KF])
        pes = ExitStack()
        psum = lambda name, shape, dt=F32: pes.enter_context(nc.psum_tensor(name + f"o{seq}", shape, dt))
        pS = psum("pS", [128, 2048]); bpS = Buf("pS")
        pA = psum("pA", [128, 512]); bpA = Buf("pA")
        pB = psum("pB", [128, 512]); bpB = Buf("pB")
        pC = psum("pC", [128, 512]); bpC = Buf("pC")
        pT = psum("pTo", [128, 1024], BF16); bpT = Buf("pTo")

        for tens, bf, dram in ((wq, b_wq, dwq2), (wk, b_wk, dwk2), (wv, b_wv, dwv2)):
            S.emit("gp", (lambda tens=tens, dram=dram: (lambda e: e.dma_start(
                out=tens[:, :, :], in_=dram.rearrange("(c p) n -> p c n", p=128))))(),
                   writes=[bf], dma=True)

        def vslot(s):
            return (o_vfa[:, s, :], [b_VR, b_AO, b_ptA, b_ptB, b_ptTA, b_ptTB, b_T1, b_U2, b_SQ]) if s < 11 \
                else (o_vfb[:, s - 11, :], [b_A4])

        OSCR = [b_VR, b_AO]

        for tau in range(ntt):
            t0 = tau * TT
            for cg in range(3):
                n = min(3, ncnk - cg * 3)
                S.emit("sp", (lambda t0=t0, cg=cg, n=n: (lambda e: e.dma_start(
                    out=o_xf[:, 0:n, :],
                    in_=dx1[seq, cg * 3:cg * 3 + n, :, t0:t0 + TT].rearrange("c p n -> p c n"))))(),
                       reads=[b_dx1], writes=OSCR, dma=True)
                for j in range(n):
                    c = cg * 3 + j
                    S.emit("dve", (lambda c=c, j=j, t0=t0: (lambda e: e.tensor_copy(
                        o_xnf[:, c, t0:t0 + TT], o_xf[:, j, :])))(),
                           reads=OSCR, writes=O_XNF2)
            ln_stats(o_xnf[:, :, t0:t0 + TT], O_XNF2, o_sq, b_SQ, o_stv, b_AO, o_stb, b_ptA,
                     pA, bpA, pB, bpB, pC, bpC)
            for cg in range(3):
                n = min(3, ncnk - cg * 3)
                S.emit("sp", (lambda t0=t0, cg=cg, n=n: (lambda e: e.dma_start(
                    out=o_xf[:, 0:n, :],
                    in_=dx1[seq, cg * 3:cg * 3 + n, :, t0:t0 + TT].rearrange("c p n -> p c n"))))(),
                       reads=[b_dx1], writes=OSCR, dma=True)
                for j in range(n):
                    c = cg * 3 + j
                    S.emit("dve", (lambda c=c, j=j: (lambda e: e.tensor_mul(o_v1[:], o_xf[:, j, :], pB[:, 0:TT])))(),
                           reads=OSCR + [bpB], writes=OSCR)
                    S.emit("dve", (lambda c=c, t0=t0: (lambda e: e.tensor_add(
                        o_xnf[:, c, t0:t0 + TT], o_v1[:], pC[:, 0:TT])))(),
                           reads=OSCR + [bpC], writes=O_XNF2)
        for tau in range(ntt):
            t0 = tau * TT
            for side in range(2):
                wt, bwt = (wq, b_wq) if side == 0 else (wk, b_wk)
                dst, bdst = (o_qf, b_QF) if side == 0 else (o_kf, b_KF)
                pb_, bpb_ = (pA, bpA) if side == 0 else (pB, bpB)
                for hp in range(nhp):
                    fns = []
                    for c in range(ncnk):
                        fns.append((lambda c=c, hp=hp, wt=wt, t0=t0, pb_=pb_: (lambda e: e.matmul(
                            pb_[:, 0:TT], wt[:, c, hp * 128:(hp + 1) * 128], o_xnf[:, c, t0:t0 + TT],
                            start=(c == 0), stop=(c == ncnk - 1))))())
                    S.emit_group("pe", fns, reads=O_XNF2 + [bwt], writes=[bpb_])
                    S.emit("act", (lambda dst=dst, hp=hp, t0=t0, pb_=pb_: (lambda e: e.activation(
                        dst[:, hp, t0:t0 + TT], pb_[:, 0:TT], AF.Copy)))(),
                           reads=[bpb_], writes=[bdst])
            for g in range(4):
                gg = tau * 4 + g
                vv_, bvv_ = vslot(gg)
                for nh in range(2):
                    fns = []
                    for c in range(ncnk):
                        fns.append((lambda c=c, g=g, nh=nh, t0=t0: (lambda e: e.matmul(
                            pC[:, 0:512], o_xnf[:, c, t0 + g * 128:t0 + (g + 1) * 128],
                            wv[:, c, nh * 512:(nh + 1) * 512], start=(c == 0), stop=(c == ncnk - 1))))())
                    S.emit_group("pe", fns, reads=O_XNF2 + [b_wv], writes=[bpC])
                    S.emit("act", (lambda vv_=vv_, nh=nh: (lambda e: e.activation(
                        vv_[:, nh * 512:(nh + 1) * 512], pC[:, 0:512], AF.Copy)))(),
                           reads=[bpC], writes=bvv_)
        S.emit("gp", lambda e: e.dma_start(out=wo[:, :, :], in_=dwpj.rearrange("(c p) n -> p c n", p=128)),
               writes=[b_wo], dma=True)
        pj_x = view(AR3, 0, 1536, [128, 3, TT])
        pj_v = view(AR3, 1536, 512, [128, TT])
        PJB = [b_XS]
        for sub in range(cfg.nsub):
            kend = (sub + 1) * 128
            for h in range(nheads):
                hp, par = h // 2, h % 2
                pb = par * 64
                fns = []
                nblk = (kend + 511) // 512
                for bI in range(nblk):
                    c0 = bI * 512
                    w = min(512, kend - c0)
                    last = (bI == nblk - 1)
                    fns.append((lambda hp=hp, pb=pb, sub=sub, c0=c0, w=w, last=last: (lambda e: e.matmul(
                        pS[:, c0:c0 + w], o_qf[pb:pb + 64, hp, sub * 128:(sub + 1) * 128],
                        o_kf[pb:pb + 64, hp, c0:c0 + w], start=True, stop=not last)))())
                fns.append((lambda kend=kend: (lambda e: e.matmul(
                    pS[:, kend - 128:kend], t_I128[:], t_mTup[:, 1, :], start=False, stop=True)))())
                S.emit_group("pe", fns, reads=[b_QF, b_KF, b_const], writes=[bpS])
                S.emit("dve", (lambda kend=kend: (lambda e: e.reduce_max(t_mx[:, 0:1], pS[:, 0:kend], axis=AX.X)))(),
                       reads=[bpS], writes=[b_mx])
                S.emit("dve", lambda e: e.tensor_scalar_mul(t_bias[:], t_mx[:, 0:1], -1.0 / 32.0),
                       reads=[b_mx], writes=[b_bias])
                S.emit("act", (lambda h=h, kend=kend: (lambda e: e.activation(
                    o_ptf[:, 0:kend], pS[:, 0:kend], AF.Exp, bias=t_bias[:, 0:1], scale=1.0 / 32.0,
                    accum_out=t_den[:, h:h + 1])))(),
                       reads=[bpS, b_bias], writes=[b_PTF, b_den])
                S.emit("dve", (lambda h=h: (lambda e: e.reciprocal(t_den[:, h:h + 1], t_den[:, h:h + 1])))(),
                       reads=[b_den], writes=[b_den])
                S.emit("dve", (lambda h=h, kend=kend: (lambda e: e.tensor_single_scalar(
                    o_ptf[:, 0:kend], o_ptf[:, 0:kend], t_den[:, h:h + 1], ALU.mult)))(),
                       reads=[b_PTF, b_den], writes=[b_PTF])
                nkc = sub + 1
                for c4 in range(0, nkc, 4):
                    cnt = min(4, nkc - c4)
                    fns = []
                    for c in range(cnt):
                        fns.append((lambda c4=c4, c=c: (lambda e: e.transpose(
                            pT[:, c * 128:(c + 1) * 128], o_ptf[:, (c4 + c) * 128:(c4 + c + 1) * 128],
                            t_I128[:])))())
                    S.emit_group("pe", fns, reads=[b_PTF, b_const], writes=[bpT])
                    S.emit("dve", (lambda cnt=cnt: (lambda e: e.tensor_copy(
                        o_ptT[:, :, :].rearrange("p a b -> p (a b)")[:, 0:cnt * 128], pT[:, 0:cnt * 128])))(),
                           reads=[bpT], writes=[b_QB])
                    fns = []
                    rds = [b_QB]
                    for c in range(cnt):
                        vv_, bvv_ = vslot(c4 + c)
                        rds = rds + [bb for bb in bvv_ if bb not in rds]
                        fns.append((lambda c4=c4, c=c, h=h, par=par, sub=sub, vv_=vv_: (lambda e: e.matmul(
                            pC[par * 64:par * 64 + 64, 0:128], vv_[:, h * 64:(h + 1) * 64],
                            o_ptT[:, c, :], start=(c4 + c == 0), stop=(c4 + c == sub),
                            tile_position=(0, par * 64))))())
                    S.emit_group("pe", fns, reads=rds, writes=[bpC])
                if par == 1:
                    S.emit("act", (lambda hp=hp, sub=sub: (lambda e: e.activation(
                        o_aot[:, hp, :], pC[:, 0:128], AF.Copy)))(),
                           reads=[bpC], writes=[b_AOT])
            t0 = sub * 128
            for mg in range(3):
                n = min(3, ncnk - mg * 3)
                S.emit("sp", (lambda t0=t0, mg=mg, n=n: (lambda e: e.dma_start(
                    out=pj_x[:, 0:n, 0:128],
                    in_=dx1[seq, mg * 3:mg * 3 + n, :, t0:t0 + 128].rearrange("c p n -> p c n"))))(),
                       reads=[b_dx1], writes=PJB, dma=True)
                for j in range(n):
                    m = mg * 3 + j
                    fns = []
                    for c in range(ncnk):
                        fns.append((lambda m=m, c=c: (lambda e: e.matmul(
                            pA[:, 0:128], wo[:, c, m * 128:(m + 1) * 128], o_aot[:, c, :],
                            start=(c == 0), stop=(c == ncnk - 1))))())
                    S.emit_group("pe", fns, reads=[b_AOT, b_wo], writes=[bpA])
                    S.emit("dve", (lambda m=m, j=j: (lambda e: e.scalar_tensor_tensor(
                        pj_v[:, 0:128], pA[:, 0:128], t_bpj[:, m:m + 1], pj_x[:, j, 0:128],
                        ALU.add, ALU.add)))(),
                           reads=[bpA, b_bpj] + PJB, writes=PJB)
                    S.emit("sp", (lambda t0=t0, m=m: (lambda e: e.dma_start(
                        out=dx2[seq, m, :, t0:t0 + 128], in_=pj_v[:, 0:128])))(),
                           reads=PJB, writes=[b_dx2], dma=True)
        pes.close()

    for seq in range(NSEQ):
        ph = cfg.phases
        if ph == "faf":
            phase_F(seq, dx, dx1)
            phase_outer(seq)
            phase_F(seq, dx2, dout)
        elif ph == "f":
            phase_F(seq, dx, dout)
        elif ph == "fa":
            phase_F(seq, dx, dx1)
            phase_outer(seq)
            for tau in range(ntt):
                t0 = tau * TT
                for cg in range(3):
                    n = min(3, ncnk - cg * 3)
                    S.emit("sp", (lambda t0=t0, cg=cg, n=n: (lambda e: e.dma_start(
                        out=xf[:, 0:n, :],
                        in_=dx2[seq, cg * 3:cg * 3 + n, :, t0:t0 + TT].rearrange("c p n -> p c n"))))(),
                           reads=[b_dx2], writes=[b_KR], dma=True)
                    S.emit("sp", (lambda t0=t0, cg=cg, n=n: (lambda e: e.dma_start(
                        out=dout[seq, cg * 3:cg * 3 + n, :, t0:t0 + TT].rearrange("c p n -> p c n"),
                        in_=xf[:, 0:n, :])))(),
                           reads=[b_KR], dma=True)

    S.wait_all_done()
    with nc.Block() as block:
        S.flush(block)
    return S



def prep_inputs(inputs, n_cores=8):
    x = np.asarray(inputs["L0_attn_x"])
    B, T, D = x.shape
    wq = np.asarray(inputs["L0_attn_wq"]); wk = np.asarray(inputs["L0_attn_wk"])
    wv = np.asarray(inputs["L0_attn_wv"])
    wpj = np.asarray(inputs["L0_attn_w_proj"]); bpj = np.asarray(inputs["L0_attn_b_proj"])
    Wqkv = np.asarray(inputs["L0_ffn_Wqkv"])
    qs = np.asarray(inputs["L0_ffn_q_scale"]); ks = np.asarray(inputs["L0_ffn_k_scale"])
    Wo = np.asarray(inputs["L0_ffn_Wo"])
    g = np.asarray(inputs["L0_ffn_ln_g"]); bb = np.asarray(inputs["L0_ffn_ln_b"])
    W1 = np.asarray(inputs["L0_ffn_W1"]); b1 = np.asarray(inputs["L0_ffn_b1"])
    W2 = np.asarray(inputs["L0_ffn_W2"]); b2 = np.asarray(inputs["L0_ffn_b2"])
    DEP = Wqkv.shape[0]
    DH, PAD, H2 = 2730, 86, 1408

    a, gate = W1[:, :, :DH], W1[:, :, DH:]
    z = np.zeros((DEP, D, PAD), W1.dtype)
    w1p = np.concatenate([a[:, :, :H2], gate[:, :, :H2], a[:, :, H2:], z,
                          gate[:, :, H2:], z], axis=2)
    b1a, b1g = b1[:, :DH], b1[:, DH:]
    zb = np.zeros((DEP, PAD), b1.dtype)
    b1ap = np.concatenate([b1a[:, :H2], b1a[:, H2:], zb], axis=1)
    b1gp = np.concatenate([b1g[:, :H2], b1g[:, H2:], zb], axis=1)
    zw = np.zeros((DEP, PAD, D), W2.dtype)
    w2p = np.concatenate([W2[:, :H2, :], W2[:, H2:, :], zw], axis=1)

    def cols_h(v):
        n = v.shape[1] // 128
        return np.ascontiguousarray(v.reshape(DEP, n, 128).transpose(2, 0, 1))

    p = np.arange(128)
    shared = dict(
        wqkv=np.ascontiguousarray(Wqkv), wo=np.ascontiguousarray(Wo),
        w1p=np.ascontiguousarray(w1p), w2p=np.ascontiguousarray(w2p),
        g_h=cols_h(g), b_h=cols_h(bb), b1a_h=cols_h(b1ap), b1g_h=cols_h(b1gp),
        b2_h=cols_h(b2),
        qsm=np.ascontiguousarray(qs[:, p % 64].T), qss=np.ascontiguousarray(qs[:, (p + 32) % 64].T),
        ksm=np.ascontiguousarray(ks[:, p % 64].T), kss=np.ascontiguousarray(ks[:, (p + 32) % 64].T),
        wq2=np.ascontiguousarray(wq.transpose(1, 0, 2).reshape(D, D)),
        wk2=np.ascontiguousarray(wk.transpose(1, 0, 2).reshape(D, D)),
        wv2=np.ascontiguousarray(wv.transpose(1, 0, 2).reshape(D, D)),
        wpj=np.ascontiguousarray(wpj),
        bpj_h=np.ascontiguousarray(bpj.reshape(8, 128).T))
    per = B // n_cores
    in_maps = []
    for c in range(n_cores):
        xs_ = x[per * c:per * (c + 1)]
        m = dict(shared)
        m["x"] = np.ascontiguousarray(xs_.transpose(0, 2, 1).reshape(per, 8, 128, T))
        in_maps.append(m)
    return in_maps, (B, T, D)


def kernel(**inputs):
    in_maps, (B, T, D) = prep_inputs(inputs)
    nc = bass.Bass("TRN2", target_bir_lowering=False)
    cfg = Cfg(T=T)
    build(nc, cfg)
    res = run_bass_kernel_spmd(nc, in_maps, core_ids=list(range(8)))
    per = B // 8
    outs = []
    for c in range(8):
        o = np.asarray(res.results[c]["out"]).reshape(per, D, T).transpose(0, 2, 1)
        outs.append(o)
    return np.concatenate(outs, axis=0).astype(np.float32)


# version 09 (final)
# speedup vs baseline: 1.1932x; 1.0355x over previous
import numpy as np
from contextlib import ExitStack

import concourse.bass as bass
import concourse.mybir as mybir
from concourse.bass_utils import run_bass_kernel_spmd

F32 = mybir.dt.float32
BF16 = mybir.dt.bfloat16
I32 = mybir.dt.int32
AF = mybir.ActivationFunctionType
ALU = mybir.AluOpType
AX = mybir.AxisListType

TWO_PI = float(2 * np.pi)
NEG_BIG = -1.0e30
LN8 = float(np.log(8.0))

STREAMS = ("pe", "act", "dve", "gp", "sp")


class Buf:
    __slots__ = ("name", "w", "readers")

    def __init__(self, name):
        self.name = name
        self.w = None
        self.readers = {}

    def __repr__(self):
        return f"Buf({self.name})"


NDMASEM = 12


class Sched:
    def __init__(self, nc, es):
        self.nc = nc
        self.streams = {s: [] for s in STREAMS}
        self.sems = {}
        for s in ("pe", "act", "dve", "gp", "sp"):
            self.sems[s] = es.enter_context(nc.semaphore(f"sem_{s}"))
        for q in ("gpd", "spd"):
            for j in range(NDMASEM):
                self.sems[f"{q}{j}"] = es.enter_context(nc.semaphore(f"sem_{q}{j}"))
        self.counts = {s: 0 for s in self.sems}
        self.dma_rot = {"gp": 0, "sp": 0}
        self.waited = {s: {} for s in STREAMS}
        self.nwaits = 0
        self.ninstr = 0

    def _collect_waits(self, stream, reads, writes):
        want = {}

        def add(tok):
            if tok is None:
                return
            s, v = tok
            if want.get(s, -1) < v:
                want[s] = v

        for b in reads:
            add(b.w)
        for b in writes:
            add(b.w)
            for s, v in b.readers.items():
                if want.get(s, -1) < v:
                    want[s] = v
        waits = []
        for s, v in sorted(want.items()):
            if s == "pe" and stream == "pe":
                continue
            if self.waited[stream].get(s, 0) >= v:
                continue
            waits.append((s, v))
            self.waited[stream][s] = v
        return waits

    def emit(self, stream, fn, reads=(), writes=(), dma=False):
        return self.emit_group(stream, [fn], reads=reads, writes=writes, dma=dma)

    def emit_group(self, stream, fns, reads=(), writes=(), dma=False):
        waits = self._collect_waits(stream, reads, writes)
        if dma:
            q = {"gp": "gpd", "sp": "spd"}[stream]
            j = self.dma_rot[stream]
            self.dma_rot[stream] = (j + 1) % NDMASEM
            semname = f"{q}{j}"
            prev = self.counts[semname]
            if prev > 0 and self.waited[stream].get(semname, 0) < prev:
                waits.append((semname, prev))
                self.waited[stream][semname] = prev
            inc = 16
        else:
            semname = stream
            inc = 1
        sem = self.sems[semname]
        self.counts[semname] += inc
        tok = (semname, self.counts[semname])
        semobjs = self.sems
        self.ninstr += len(fns) + len(waits)

        def closure(eng, waits=waits, fns=fns, sem=sem, inc=inc):
            for s, v in waits:
                eng.wait_ge(semobjs[s], v)
            for f in fns[:-1]:
                f(eng)
            ins = fns[-1](eng)
            ins.then_inc(sem, inc)
            return ins

        self.streams[stream].append(closure)
        self.nwaits += len(waits)
        for b in writes:
            b.w = tok
            b.readers = {}
        for b in reads:
            if b.readers.get(semname, 0) < tok[1]:
                b.readers[semname] = tok[1]
        return tok

    def link(self, dsts, srcs):
        merged = {}
        for b in srcs:
            if b.w is not None:
                s, v = b.w
                if merged.get(s, 0) < v:
                    merged[s] = v
            for s, v in b.readers.items():
                if merged.get(s, 0) < v:
                    merged[s] = v
        for b in dsts:
            for s, v in merged.items():
                if b.readers.get(s, 0) < v:
                    b.readers[s] = v

    def wait_all_done(self):
        finals = [(s, self.counts[s]) for s in self.counts if self.counts[s] > 0]
        semobjs = self.sems

        def closure(eng):
            for s, v in finals:
                eng.wait_ge(semobjs[s], v)
        self.streams["sp"].append(closure)

    def flush(self, block):
        st = self.streams
        block.tensor(lambda e: [f(e) for f in st["pe"]])
        block.scalar(lambda e: [f(e) for f in st["act"]])
        block.vector(lambda e: [f(e) for f in st["dve"]])
        block.gpsimd(lambda e: [f(e) for f in st["gp"]])
        block.sync(lambda e: [f(e) for f in st["sp"]])


class Cfg:
    def __init__(self, T=2048, depth=6, n_seq=2, phases="faf"):
        self.T = T
        self.depth = depth
        self.n_seq = n_seq
        self.phases = phases
        self.d = 1024
        self.ncnk = 8
        self.nheads = 16
        self.nhp = 8
        self.TT = 512
        self.ntt = T // self.TT
        self.nsub = T // 128


def build(nc, cfg: Cfg):
    es = ExitStack()
    S = Sched(nc, es)
    T, TT, ntt = cfg.T, cfg.TT, cfg.ntt
    d, ncnk, nhp, nheads = cfg.d, cfg.ncnk, cfg.nhp, cfg.nheads
    DEP = 6
    NSEQ = cfg.n_seq

    dx = nc.dram_tensor("x", [NSEQ, ncnk, 128, T], F32, kind="ExternalInput").ap()
    dout = nc.dram_tensor("out", [NSEQ, ncnk, 128, T], F32, kind="ExternalOutput").ap()
    dwqkv = nc.dram_tensor("wqkv", [DEP, d, 3 * d], F32, kind="ExternalInput").ap()
    dwo = nc.dram_tensor("wo", [DEP, d, d], F32, kind="ExternalInput").ap()
    dw1 = nc.dram_tensor("w1p", [DEP, d, 5632], F32, kind="ExternalInput").ap()
    dw2 = nc.dram_tensor("w2p", [DEP, 2816, d], F32, kind="ExternalInput").ap()
    dgh = nc.dram_tensor("g_h", [128, DEP, ncnk], F32, kind="ExternalInput").ap()
    dbh = nc.dram_tensor("b_h", [128, DEP, ncnk], F32, kind="ExternalInput").ap()
    db1a = nc.dram_tensor("b1a_h", [128, DEP, 22], F32, kind="ExternalInput").ap()
    db1g = nc.dram_tensor("b1g_h", [128, DEP, 22], F32, kind="ExternalInput").ap()
    db2 = nc.dram_tensor("b2_h", [128, DEP, ncnk], F32, kind="ExternalInput").ap()
    dqsm = nc.dram_tensor("qsm", [128, DEP], F32, kind="ExternalInput").ap()
    dqss = nc.dram_tensor("qss", [128, DEP], F32, kind="ExternalInput").ap()
    dksm = nc.dram_tensor("ksm", [128, DEP], F32, kind="ExternalInput").ap()
    dkss = nc.dram_tensor("kss", [128, DEP], F32, kind="ExternalInput").ap()
    dwq2 = nc.dram_tensor("wq2", [d, d], F32, kind="ExternalInput").ap()
    dwk2 = nc.dram_tensor("wk2", [d, d], F32, kind="ExternalInput").ap()
    dwv2 = nc.dram_tensor("wv2", [d, d], F32, kind="ExternalInput").ap()
    dwpj = nc.dram_tensor("wpj", [d, d], F32, kind="ExternalInput").ap()
    dbpj = nc.dram_tensor("bpj_h", [128, ncnk], F32, kind="ExternalInput").ap()
    dx1 = nc.dram_tensor("x1s", [NSEQ, ncnk, 128, T], F32, kind="Internal").ap()
    dx2 = nc.dram_tensor("x2s", [NSEQ, ncnk, 128, T], F32, kind="Internal").ap()
    b_dx = Buf("dx"); b_dx1 = Buf("dx1"); b_dx2 = Buf("dx2"); b_dout = Buf("dout")
    DBUF = {id(dx): b_dx, id(dx1): b_dx1, id(dx2): b_dx2, id(dout): b_dout}

    sbuf = lambda name, shape, dt=F32: es.enter_context(nc.sbuf_tensor(name, shape, dt))

    t_gh = sbuf("t_gh", [128, DEP, ncnk]); b_gh = Buf("gh")
    t_bh = sbuf("t_bh", [128, DEP, ncnk]); b_bh = Buf("bh")
    t_b1a = sbuf("t_b1a", [128, DEP, 22]); b_b1a = Buf("b1a")
    t_b1g = sbuf("t_b1g", [128, DEP, 22]); b_b1g = Buf("b1g")
    t_b2 = sbuf("t_b2", [128, DEP, ncnk]); b_b2 = Buf("b2")
    t_qsm = sbuf("t_qsm", [128, DEP]); b_qsm = Buf("qsm")
    t_qss = sbuf("t_qss", [128, DEP]); _ = None
    t_ksm = sbuf("t_ksm", [128, DEP])
    t_kss = sbuf("t_kss", [128, DEP])
    t_bpj = sbuf("t_bpj", [128, ncnk]); b_bpj = Buf("bpj")
    for tens, dram, bf in ((t_gh, dgh, b_gh), (t_bh, dbh, b_bh), (t_b1a, db1a, b_b1a),
                           (t_b1g, db1g, b_b1g), (t_b2, db2, b_b2), (t_qsm, dqsm, b_qsm),
                           (t_qss, dqss, b_qsm), (t_ksm, dksm, b_qsm), (t_kss, dkss, b_qsm),
                           (t_bpj, dbpj, b_bpj)):
        S.emit("sp", (lambda tens=tens, dram=dram: (lambda e: e.dma_start(out=tens[:], in_=dram[:])))(),
               writes=[bf], dma=True)

    t_pidx = sbuf("t_pidx", [128, 1])
    t_cst = sbuf("t_cst", [128, 2])
    b_cst = Buf("cst")
    nc.const_aps.register(F32, 1e-5, t_cst[:, 0:1])
    nc.const_aps.register(F32, LN8, t_cst[:, 1:2])
    t_I128 = sbuf("t_I128", [128, 128], BF16)
    t_Psig = sbuf("t_Psig", [128, 128], BF16)
    t_mTup = sbuf("t_mTup", [128, 2, 128], BF16)
    t_ones1 = sbuf("t_ones1", [1, 128], BF16)
    t_onesc = sbuf("t_onesc", [128, 1], BF16)
    t_Bhd = sbuf("t_Bhd", [16, nhp, 128], BF16)
    t_Ohp = sbuf("t_Ohp", [128, nhp, 16], BF16)
    t_sm1 = sbuf("t_sm1", [128, 1]); t_sm2 = sbuf("t_sm2", [128, 1])
    t_sm3 = sbuf("t_sm3", [128, 1]); t_sm4 = sbuf("t_sm4", [128, 1])
    t_den = sbuf("t_den", [128, 16])
    b_den0 = Buf("den0"); b_den1 = Buf("den1"); b_den = b_den0
    t_mx = sbuf("t_mx", [128, 4])
    b_mx0 = Buf("mx0"); b_mx1 = Buf("mx1"); b_mx = b_mx0
    t_bias = sbuf("t_bias", [128, 1]); b_bias = Buf("bias")
    t_inv = sbuf("t_inv", [16, 2 * TT], BF16); b_inv = Buf("inv")
    b_const = Buf("const")
    b_tab = Buf("tab")

    AR1 = sbuf("AR1", [128, 16392])
    AR2 = sbuf("AR2", [128, 16896])
    AR3 = sbuf("AR3", [128, 13824])
    AR4 = sbuf("AR4", [128, 4096])

    def view(arena, off, n, shape, dt=F32):
        ap = arena[:][:, off:off + n]
        if dt == BF16:
            ap = ap.bitcast(BF16)
        elif dt == I32:
            ap = ap.bitcast(I32) if hasattr(ap, "bitcast") else ap
        if len(shape) == 2:
            if shape[0] != 128:
                ap = ap[0:shape[0], :]
        elif len(shape) == 3:
            ap = ap.rearrange("p (a b) -> p a b", a=shape[1], b=shape[2])
            if shape[0] != 128:
                ap = ap[0:shape[0], :, :]
        return ap

    y = view(AR1, 0, ncnk * (T + 1), [128, ncnk, T + 1]); b_y = Buf("y")
    o_qf = view(AR1, 0, 4 * T, [128, nhp, T], BF16)
    o_kf = view(AR1, 4 * T, 4 * T, [128, nhp, T], BF16)
    b_QF = Buf("R_qf"); b_KF = Buf("R_kf")
    YBUFS = [b_y, b_QF, b_KF]

    wq = view(AR2, 0, 4096, [128, ncnk, d], BF16); b_wq = Buf("R_wq")
    wk = view(AR2, 4096, 4096, [128, ncnk, d], BF16); b_wk = Buf("R_wk")
    wv = view(AR2, 8192, 4096, [128, ncnk, d], BF16); b_wv = Buf("R_wv")
    wo = view(AR2, 12288, 4096, [128, ncnk, d], BF16); b_wo = Buf("R_wo")
    w1v = view(AR2, 0, 11264, [128, ncnk, 2816], BF16)
    w2v = view(AR2, 11264, 5632, [128, 11, 1024], BF16)
    W1BUFS = [b_wq, b_wk, b_wv]
    W2BUFS = [b_wv, b_wo]

    xs = view(AR3, 0, 2048, [128, ncnk, TT], BF16); b_XS = Buf("R_xs")
    qp = view(AR3, 0, 2048, [128, ncnk, TT], BF16)
    hv_lo = view(AR3, 0, 2048, [128, ncnk, TT], BF16)
    qb = view(AR3, 2048, 2048, [128, ncnk, TT], BF16); b_QB = Buf("R_qb")
    v1 = view(AR3, 2048, 512, [128, TT])
    stv = view(AR3, 2560, 1024, [1, 2, TT])
    hv_hi = view(AR3, 2048, 768, [128, 3, TT], BF16)
    gel = view(AR3, 2816, 256, [128, TT], BF16)
    kr = view(AR3, 4096, 3072, [128, ncnk, 768], BF16); b_KR = Buf("R_kr")
    xf = view(AR3, 4096, 1536, [128, 3, TT])
    vr = view(AR3, 7168, 3072, [128, 6, d], BF16); b_VR = Buf("R_vr")
    ao = view(AR3, 10240, 2048, [128, ncnk, TT], BF16); b_AO = Buf("R_ao")
    B_XNF = [Buf(f"xnfT{t}") for t in range(ntt)]
    _TRIO = (b_KR, b_VR, b_AO)

    def wx(lst):
        return lst + B_XNF if any(b in _TRIO for b in lst) else lst
    xnf = view(AR3, 4096, ncnk * T // 2, [128, ncnk, T], BF16)
    ptA = view(AR3, 12288, 192, [128, 384], BF16); b_ptA = Buf("R_ptA")
    ptB = view(AR3, 12480, 192, [128, 384], BF16); b_ptB = Buf("R_ptB")
    ptTA = view(AR3, 12672, 192, [128, 3, 128], BF16); b_ptTA = Buf("R_ptTA")
    ptTB = view(AR3, 12864, 192, [128, 3, 128], BF16); b_ptTB = Buf("R_ptTB")
    tt1 = view(AR3, 13056, 256, [128, TT], BF16); b_T1 = Buf("R_t1")
    tu2 = view(AR3, 13312, 256, [128, TT], BF16); b_U2 = Buf("R_u2")
    sq = view(AR3, 13568, 256, [128, TT], BF16); b_SQ = Buf("R_sq")
    stb = view(AR3, 3584, 512, [1, 2, TT], BF16); b_STB = b_QB
    o_xnf = view(AR3, 0, ncnk * T // 2, [128, ncnk, T], BF16)
    o_vfa = view(AR3, 8192, 5632, [128, 11, d], BF16)
    O_XNF = [b_XS, b_QB, b_KR]
    O_XNF2 = [b_XS, b_QB, b_KR, b_VR]
    O_VFA = [b_VR, b_AO, b_ptA, b_ptB, b_ptTA, b_ptTB, b_T1, b_U2, b_SQ]
    o_xf = view(AR3, 8192, 1536, [128, 3, TT])
    o_v1 = view(AR3, 11776, 512, [128, TT])
    o_stv = view(AR3, 10240, 1024, [1, 2, TT])
    o_stb = view(AR3, 12288, 512, [1, 2, TT], BF16)
    o_sq = view(AR3, 13568, 256, [128, TT], BF16)
    o_ptT = view(AR3, 2048, 256, [128, 4, 128], BF16)

    tb_csS = view(AR4, 0, 1024, [128, 2048], BF16)
    tb_snS = view(AR4, 1024, 1024, [128, 2048], BF16)
    tb_csI = view(AR4, 2048, 1024, [128, 2048], BF16)
    tb_snI = view(AR4, 3072, 1024, [128, 2048], BF16)
    b_A4 = Buf("R_a4")
    o_vfb = view(AR4, 0, 2560, [128, 5, d], BF16)
    o_aot = view(AR4, 2560, 512, [128, ncnk, 128], BF16)
    o_ptf = view(AR4, 3072, T // 2, [128, T], BF16)
    b_VFB = b_A4; b_AOT = b_A4; b_PTF = b_A4

    def build_constants():
        g = "gp"
        t_colid = view(AR3, 512, 128, [128, 128])
        S.emit(g, lambda e: e.memset(t_cst[:, 0:1], 1e-5), writes=[b_const])
        S.emit(g, lambda e: e.memset(t_cst[:, 1:2], LN8), writes=[b_const])
        S.emit(g, lambda e: e.iota(t_pidx[:], pattern=[[0, 1]], base=0,
                                   channel_multiplier=1, allow_small_or_imprecise_dtypes=True),
               writes=[b_const])
        S.emit(g, lambda e: e.iota(t_colid[:], pattern=[[1, 128]], base=0,
                                   channel_multiplier=0, allow_small_or_imprecise_dtypes=True),
               writes=[b_const])
        S.emit("dve", lambda e: e.tensor_single_scalar(t_I128[:], t_colid[:], t_pidx[:, 0:1], ALU.is_equal),
               reads=[b_const], writes=[b_const])
        S.emit("dve", lambda e: e.tensor_single_scalar(t_sm1[:], t_pidx[:], 63.5, ALU.is_gt),
               reads=[b_const], writes=[b_const])
        S.emit("dve", lambda e: e.scalar_tensor_tensor(t_sm2[:], t_sm1[:], -64.0, t_pidx[:], ALU.mult, ALU.add),
               reads=[b_const], writes=[b_const])
        S.emit("dve", lambda e: e.tensor_single_scalar(t_sm3[:], t_sm2[:], 31.5, ALU.is_gt),
               reads=[b_const], writes=[b_const])
        S.emit("dve", lambda e: e.scalar_tensor_tensor(t_sm4[:], t_sm3[:], -64.0, t_pidx[:], ALU.mult, ALU.add),
               reads=[b_const], writes=[b_const])
        S.emit("dve", lambda e: e.tensor_scalar_add(t_sm4[:], t_sm4[:], 32.0),
               reads=[b_const], writes=[b_const])
        S.emit("dve", lambda e: e.tensor_single_scalar(t_Psig[:], t_colid[:], t_sm4[:, 0:1], ALU.is_equal),
               reads=[b_const], writes=[b_const])
        scr = view(AR3, 0, 128, [128, 128])
        S.emit("dve", lambda e: e.tensor_single_scalar(scr[:], t_colid[:], t_pidx[:, 0:1], ALU.is_lt),
               reads=[b_const], writes=[b_XS])
        S.emit("dve", lambda e: e.tensor_scalar_mul(t_mTup[:, 0, :], scr[:], NEG_BIG),
               reads=[b_XS], writes=[b_const])
        S.emit("dve", lambda e: e.tensor_single_scalar(scr[:], t_colid[:], t_pidx[:, 0:1], ALU.is_gt),
               reads=[b_const], writes=[b_XS])
        S.emit("dve", lambda e: e.tensor_scalar_mul(t_mTup[:, 1, :], scr[:], NEG_BIG),
               reads=[b_XS], writes=[b_const])
        S.emit(g, lambda e: e.memset(t_ones1[:], 1.0), writes=[b_const])
        S.emit(g, lambda e: e.memset(t_onesc[:], 1.0), writes=[b_const])
        scr2 = view(AR3, 128, 128, [128, 128])
        S.emit("dve", lambda e: e.tensor_single_scalar(scr[:], t_colid[:], 63.5, ALU.is_gt),
               reads=[b_const], writes=[b_XS])
        for hp in range(nhp):
            S.emit("dve", (lambda hp=hp: (lambda e: e.tensor_scalar_add(scr2[:], scr[:], float(2 * hp))))(),
                   reads=[b_XS], writes=[b_XS])
            S.emit("dve", (lambda hp=hp: (lambda e: e.tensor_single_scalar(
                t_Bhd[:, hp, :], scr2[0:16, :], t_pidx[0:16, 0:1], ALU.is_equal)))(),
                   reads=[b_XS, b_const], writes=[b_const])
        scr3 = view(AR3, 256, 16, [128, 16])
        S.emit(g, lambda e: e.iota(scr3[:], pattern=[[1, 16]], base=0,
                                   channel_multiplier=0, allow_small_or_imprecise_dtypes=True),
               writes=[b_XS])
        S.emit("dve", lambda e: e.tensor_single_scalar(
            scr2[:, 0:16], scr3[:], t_sm1[:, 0:1], ALU.subtract),
               reads=[b_XS, b_const], writes=[b_XS])
        for hp in range(nhp):
            S.emit("dve", (lambda hp=hp: (lambda e: e.tensor_single_scalar(
                t_Ohp[:, hp, :], scr2[:, 0:16], float(2 * hp), ALU.is_equal)))(),
                   reads=[b_XS], writes=[b_const])
        S.emit("dve", lambda e: e.tensor_scalar_mul(t_sm3[:], t_sm3[:], 2.0), reads=[b_const], writes=[b_const])
        S.emit("dve", lambda e: e.tensor_scalar_add(t_sm3[:], t_sm3[:], -1.0), reads=[b_const], writes=[b_const])
        S.emit("dve", lambda e: e.tensor_single_scalar(t_sm1[:], t_sm2[:], 31.5, ALU.is_gt),
               reads=[b_const], writes=[b_const])
        S.emit("dve", lambda e: e.scalar_tensor_tensor(t_sm4[:], t_sm1[:], -32.0, t_sm2[:], ALU.mult, ALU.add),
               reads=[b_const], writes=[b_const])
        S.emit("act", lambda e: e.activation(t_sm1[:], t_sm4[:], AF.Exp, scale=float(-np.log(10000.0) / 32.0)),
               reads=[b_const], writes=[b_const])
        S.emit("dve", lambda e: e.tensor_scalar_mul(t_sm2[:], t_sm4[:], 2.0 / 89.6), reads=[b_const], writes=[b_const])
        S.emit("dve", lambda e: e.tensor_scalar_add(t_sm2[:], t_sm2[:], 25.6 / 89.6), reads=[b_const], writes=[b_const])
        S.emit("act", lambda e: e.activation(t_sm2[:], t_sm2[:], AF.Ln), reads=[b_const], writes=[b_const])

    def build_tables():
        s0 = view(AR3, 0, 2048, [128, 2048])
        s1 = view(AR3, 2048, 2048, [128, 2048])
        si32 = view(AR3, 4096, 2048, [128, 2048], I32)
        sifl = view(AR3, 4096, 2048, [128, 2048])
        scl = view(AR3, 6144, 2048, [128, 2048])
        SCR = [b_XS, b_QB, b_KR, b_VR]
        SCRW = SCR + B_XNF
        g = "gp"
        S.emit(g, lambda e: e.iota(s0, pattern=[[1, 2048]], base=0,
                                   channel_multiplier=0, allow_small_or_imprecise_dtypes=True),
               writes=SCRW)
        S.emit("dve", lambda e: e.tensor_single_scalar(s0, s0, t_sm1[:, 0:1], ALU.mult),
               reads=SCR + [b_const], writes=SCRW)

        def sincos(dst, shift):
            S.emit("dve", lambda e: e.tensor_scalar_mul(s1, s0, 1.0 / TWO_PI), reads=SCR, writes=SCRW)
            if shift:
                S.emit("dve", lambda e: e.tensor_scalar_add(s1, s1, 0.25), reads=SCR, writes=SCRW)
            S.emit("dve", lambda e: e.tensor_copy(si32, s1), reads=SCR, writes=SCRW)
            S.emit("dve", lambda e: e.tensor_copy(sifl, si32), reads=SCR, writes=SCRW)
            S.emit("dve", lambda e: e.tensor_sub(s1, s1, sifl), reads=SCR, writes=SCRW)
            S.emit("dve", lambda e: e.tensor_single_scalar(sifl, s1, 0.5, ALU.is_gt), reads=SCR, writes=SCRW)
            S.emit("dve", lambda e: e.tensor_sub(s1, s1, sifl), reads=SCR, writes=SCRW)
            S.emit("act", (lambda dst=dst: (lambda e: e.activation(dst, s1, AF.Sin, scale=TWO_PI)))(),
                   reads=SCR, writes=[b_A4])

        sincos(tb_snS, False)
        sincos(tb_csS, True)
        S.emit(g, lambda e: e.iota(sifl, pattern=[[1, 2048]], base=-1024,
                                   channel_multiplier=0, allow_small_or_imprecise_dtypes=True),
               writes=SCRW)
        S.emit("dve", lambda e: e.tensor_scalar_mul(s0, sifl, 1.0 / 128.0), reads=SCR, writes=SCRW)
        S.emit("dve", lambda e: e.tensor_single_scalar(s0, s0, t_sm2[:, 0:1], ALU.mult),
               reads=SCR + [b_const], writes=SCRW)
        S.emit("act", lambda e: e.activation(scl, s0, AF.Exp), reads=SCR, writes=SCRW)
        S.emit("act", lambda e: e.activation(s1, s0, AF.Exp, scale=-1.0), reads=SCR, writes=SCRW)
        S.emit("dve", lambda e: e.tensor_mul(tb_snI, tb_snS, s1), reads=SCR + [b_A4], writes=[b_A4])
        S.emit("dve", lambda e: e.tensor_single_scalar(tb_snI, tb_snI, t_sm3[:, 0:1], ALU.mult),
               reads=[b_A4, b_const], writes=[b_A4])
        S.emit("dve", lambda e: e.tensor_mul(tb_csI, tb_csS, s1), reads=SCR + [b_A4], writes=[b_A4])
        S.emit("dve", lambda e: e.tensor_mul(tb_snS, tb_snS, scl), reads=SCR + [b_A4], writes=[b_A4])
        S.emit("dve", lambda e: e.tensor_single_scalar(tb_snS, tb_snS, t_sm3[:, 0:1], ALU.mult),
               reads=[b_A4, b_const], writes=[b_A4])
        S.emit("dve", lambda e: e.tensor_mul(tb_csS, tb_csS, scl), reads=SCR + [b_A4], writes=[b_A4])

    build_constants()

    def ln_stats(xap, RD, sqv, b_sqv, stvv, b_stvv, stbv, b_stbv,
                 p_s, bp_s, p_r, bp_r, p_m, bp_m):
        fns = []
        for c in range(ncnk):
            fns.append((lambda c=c: (lambda e: e.matmul(p_s[0:1, 0:TT], t_onesc[:], xap[:, c, :],
                                                        start=(c == 0), stop=(c == ncnk - 1))))())
        S.emit_group("pe", fns, reads=RD + [b_const], writes=[bp_s])
        for c in range(ncnk):
            S.emit("act", (lambda c=c: (lambda e: e.activation(sqv[:], xap[:, c, :], AF.Square)))(),
                   reads=RD, writes=[b_sqv])
            S.emit("pe", (lambda c=c: (lambda e: e.matmul(p_s[32:33, 0:TT], t_onesc[:], sqv[:],
                                                          start=(c == 0), stop=(c == ncnk - 1))))(),
                   reads=[b_sqv, b_const], writes=[bp_s])
        S.emit("dve", lambda e: e.tensor_scalar_mul(stvv[:, 0, :], p_s[0:1, 0:TT], -1.0 / d),
               reads=[bp_s], writes=wx([b_stvv]))
        S.emit("dve", lambda e: e.tensor_mul(stvv[:, 1, :], stvv[:, 0, :], stvv[:, 0, :]),
               reads=[b_stvv], writes=wx([b_stvv]))
        S.emit("dve", lambda e: e.scalar_tensor_tensor(stvv[:, 1, :], p_s[32:33, 0:TT], 1.0 / d,
                                                       stvv[:, 1, :], ALU.mult, ALU.subtract),
               reads=[bp_s, b_stvv], writes=wx([b_stvv]))
        S.emit("act", lambda e: e.activation(stvv[:, 1, :], stvv[:, 1, :], AF.Ln, bias=1e-5),
               reads=[b_stvv, b_const], writes=wx([b_stvv]))
        S.emit("act", lambda e: e.activation(stbv[:, 0, :], stvv[:, 1, :], AF.Exp, scale=-0.5),
               reads=[b_stvv], writes=[b_stbv])
        S.emit("dve", lambda e: e.tensor_mul(stbv[:, 1, :], stvv[:, 0, :], stbv[:, 0, :]),
               reads=[b_stvv, b_stbv], writes=[b_stbv])
        S.emit("pe", lambda e: e.matmul(p_r[:, 0:TT], t_ones1[:], stbv[:, 0, :], start=True, stop=True),
               reads=[b_stbv, b_const], writes=[bp_r])
        S.emit("pe", lambda e: e.matmul(p_m[:, 0:TT], t_ones1[:], stbv[:, 1, :], start=True, stop=True),
               reads=[b_stbv, b_const], writes=[bp_m])

    def phase_F(seq, xin_dram, xout_dram):
        b_xin = DBUF[id(xin_dram)]
        b_xout = DBUF[id(xout_dram)]
        pes = ExitStack()
        build._uid = getattr(build, "_uid", 0) + 1
        uid = f"u{build._uid}"
        psum = lambda name, shape, dt=F32: pes.enter_context(nc.psum_tensor(name + uid, shape, dt))
        p0 = psum("p0", [128, 512]); bp0 = Buf("p0")
        p1 = psum("p1", [128, 512]); bp1 = Buf("p1")
        p2 = psum("p2", [128, 512]); bp2 = Buf("p2")
        p3 = psum("p3", [128, 512]); bp3 = Buf("p3")
        pS2A = psum("pS2A", [128, 1024]); bpS2A = Buf("pS2A")
        pS2B = psum("pS2B", [128, 1024]); bpS2B = Buf("pS2B")
        p2bf = p2[:].bitcast(BF16)
        p3bf = p3[:].bitcast(BF16)
        pT = p2bf; bpT = Buf("pT") if False else None
        rstdv = pS2A[:][:, 0:512]
        murstdv = pS2B[:][:, 0:512]

        build_tables()
        S.emit("gp", lambda e: e.memset(y[:, :, 0:1], 0.0), writes=YBUFS)

        def ln_apply_to(dst_writer, src_dram):
            for tau in range(ntt):
                t0 = tau * TT
                for cg in range(3):
                    n = min(3, ncnk - cg * 3)
                    S.emit("sp", (lambda t0=t0, cg=cg, n=n: (lambda e: e.dma_start(
                        out=xf[:, 0:n, :],
                        in_=src_dram[seq, cg * 3:cg * 3 + n, :, t0:t0 + TT].rearrange("c p n -> p c n"))))(),
                           reads=[b_xin], writes=wx([b_KR]), dma=True)
                    for j in range(n):
                        c = cg * 3 + j
                        S.emit("dve", (lambda c=c, j=j: (lambda e: e.tensor_copy(xs[:, c, :], xf[:, j, :])))(),
                               reads=[b_KR], writes=[b_XS])
                ln_stats(xs, [b_XS], sq, b_SQ, stv, b_QB, stb, b_STB, p3, bp3, rstdv, bpS2A, murstdv, bpS2B)
                for cg in range(3):
                    n = min(3, ncnk - cg * 3)
                    S.emit("sp", (lambda t0=t0, cg=cg, n=n: (lambda e: e.dma_start(
                        out=xf[:, 0:n, :],
                        in_=src_dram[seq, cg * 3:cg * 3 + n, :, t0:t0 + TT].rearrange("c p n -> p c n"))))(),
                           reads=[b_xin], writes=wx([b_KR]), dma=True)
                    for j in range(n):
                        c = cg * 3 + j
                        S.emit("dve", (lambda c=c, j=j: (lambda e: e.tensor_mul(v1[:], xf[:, j, :], rstdv[:, 0:TT])))(),
                               reads=[b_KR, bpS2A], writes=[b_QB])
                        dst_writer(c, t0)

        def ywr(c, t0):
            S.emit("dve", (lambda c=c, t0=t0: (lambda e: e.tensor_add(
                y[:, c, 1 + t0:1 + t0 + TT], v1[:], murstdv[:, 0:TT])))(),
                   reads=[b_QB, bpS2B], writes=YBUFS)
        ln_apply_to(ywr, xin_dram)

        P = dict(p0=p0, bp0=bp0, p1=p1, bp1=bp1, p2=p2, bp2=bp2, p3=p3, bp3=bp3,
                 pS2A=pS2A, bpS2A=bpS2A, pS2B=pS2B, bpS2B=bpS2B,
                 p2bf=p2bf, p3bf=p3bf, rstdv=rstdv, murstdv=murstdv)
        for li in range(cfg.depth):
            attn_sublayer(seq, li, P)
            ffn_sublayer(seq, li, P)

        for tau in range(ntt):
            t0 = tau * TT
            for cg in range(3):
                n = min(3, ncnk - cg * 3)
                S.emit("sp", (lambda t0=t0, cg=cg, n=n: (lambda e: e.dma_start(
                    out=xf[:, 0:n, :],
                    in_=xin_dram[seq, cg * 3:cg * 3 + n, :, t0:t0 + TT].rearrange("c p n -> p c n"))))(),
                       reads=[b_xin], writes=wx([b_KR]), dma=True)
                for j in range(n):
                    c = cg * 3 + j
                    S.emit("dve", (lambda c=c, j=j, t0=t0: (lambda e: e.scalar_tensor_tensor(
                        xf[:, j, :], y[:, c, 1 + t0:1 + t0 + TT], 0.5, xf[:, j, :], ALU.mult, ALU.add)))(),
                           reads=YBUFS, writes=wx([b_KR]))
                S.emit("sp", (lambda t0=t0, cg=cg, n=n: (lambda e: e.dma_start(
                    out=xout_dram[seq, cg * 3:cg * 3 + n, :, t0:t0 + TT].rearrange("c p n -> p c n"),
                    in_=xf[:, 0:n, :])))(),
                       reads=[b_KR], writes=[b_xout], dma=True)
        pes.close()

    def attn_sublayer(seq, li, P):
        p0, bp0, p1, bp1, p2, bp2, p3, bp3 = (
            P["p0"], P["bp0"], P["p1"], P["bp1"], P["p2"], P["bp2"], P["p3"], P["bp3"])
        p2bf = P["p2bf"]
        p3bf = P["p3bf"]
        pS2A, bpS2A, pS2B, bpS2B = P["pS2A"], P["bpS2A"], P["pS2B"], P["bpS2B"]
        for tens, bf, base in ((wq, b_wq, 0), (wk, b_wk, d), (wv, b_wv, 2 * d)):
            S.emit("gp", (lambda tens=tens, base=base: (lambda e: e.dma_start(
                out=tens[:, :, :], in_=dwqkv[li, :, base:base + d].rearrange("(c p) n -> p c n", p=128))))(),
                   writes=[bf], dma=True)
        S.emit("gp", lambda e: e.dma_start(out=wo[:, :, :],
                                           in_=dwo[li].rearrange("(c p) n -> p c n", p=128)),
               writes=[b_wo], dma=True)

        bqp = [Buf(f"qp{li}_{hp}") for hp in range(nhp)]
        for tau in range(ntt):
            t0 = tau * TT
            if tau > 0:
                S.emit("gp", lambda e: e.tensor_copy(kr[:, :, 0:256], kr[:, :, 512:768]),
                       reads=[b_KR], writes=wx([b_KR]))
            S.emit("dve", (lambda t0=t0: (lambda e: e.tensor_copy(xs[:, 0:4, :], y[:, 0:4, 1 + t0:1 + t0 + TT])))(),
                   reads=YBUFS, writes=[b_XS] + bqp)
            S.emit("dve", (lambda t0=t0: (lambda e: e.tensor_copy(xs[:, 4:8, :], y[:, 4:8, t0:t0 + TT])))(),
                   reads=YBUFS, writes=[b_XS])
            for g in range(4):
                slot = (4 * tau + g) % 6
                for nh in range(2):
                    fns = []
                    for c in range(ncnk):
                        fns.append((lambda c=c, g=g, nh=nh: (lambda e: e.matmul(
                            p2[:, 0:512], xs[:, c, g * 128:(g + 1) * 128], wv[:, c, nh * 512:(nh + 1) * 512],
                            start=(c == 0), stop=(c == ncnk - 1))))())
                    S.emit_group("pe", fns, reads=[b_XS, b_wv], writes=[bp2])
                    S.emit("act", (lambda slot=slot, nh=nh: (lambda e: e.activation(
                        vr[:, slot, nh * 512:(nh + 1) * 512], p2[:, 0:512], AF.Copy)))(),
                           reads=[bp2], writes=wx([b_VR]))
            for side in (1, 0):
                wt, bwt = (wq, b_wq) if side == 0 else (wk, b_wk)
                smap = t_qsm[:, li:li + 1] if side == 0 else t_ksm[:, li:li + 1]
                ssap = t_qss[:, li:li + 1] if side == 0 else t_kss[:, li:li + 1]
                cst = tb_csS if side == 0 else tb_csI
                snt = tb_snS if side == 0 else tb_snI
                for hp in range(nhp):
                    pb_, bpb_ = (p0, bp0) if hp % 2 == 0 else (p1, bp1)
                    fns = []
                    for c in range(ncnk):
                        fns.append((lambda c=c, hp=hp, wt=wt, pb_=pb_: (lambda e: e.matmul(
                            pb_[:, 0:TT], wt[:, c, hp * 128:(hp + 1) * 128], xs[:, c, :],
                            start=(c == 0), stop=(c == ncnk - 1))))())
                    S.emit_group("pe", fns, reads=[b_XS, bwt], writes=[bpb_])
                    S.emit("act", (lambda hp=hp, pb_=pb_: (lambda e: e.activation(qb[:, hp, :], pb_[:, 0:TT], AF.Copy)))(),
                           reads=[bpb_], writes=[b_QB])
                    S.emit("act", (lambda hp=hp: (lambda e: e.activation(sq[:], qb[:, hp, :], AF.Square)))(),
                           reads=[b_QB], writes=[b_SQ])
                    S.emit("pe", (lambda hp=hp: (lambda e: e.matmul(
                        pS2A[0:16, 0:TT], t_Ohp[:, hp, :], sq[:],
                        start=(hp == 0), stop=(hp == nhp - 1))))(),
                           reads=[b_SQ, b_const], writes=[bpS2A])
                S.emit("dve", lambda e: e.tensor_scalar_max(pS2A[0:16, 0:TT], pS2A[0:16, 0:TT], 1e-18),
                       reads=[bpS2A], writes=[bpS2A])
                S.emit("act", lambda e: e.activation(p2[0:16, 0:TT], pS2A[0:16, 0:TT], AF.Ln),
                       reads=[bpS2A], writes=[bp2])
                if side == 0:
                    S.emit("act", lambda e: e.activation(t_inv[:, 0:TT], p2[0:16, 0:TT], AF.Exp, scale=-0.5, bias=LN8),
                           reads=[bp2, b_const], writes=[b_inv])
                else:
                    S.emit("act", lambda e: e.activation(t_inv[:, TT:2 * TT], p2[0:16, 0:TT], AF.Exp, scale=-0.5),
                           reads=[bp2], writes=[b_inv])
                colb = 0 if side == 0 else TT
                for hp in range(nhp):
                    pg_ = pS2B[:][:, 0:512] if hp % 2 == 0 else pS2B[:][:, 512:1024]
                    bpg_ = bpS2B
                    pv_, bpv_ = (p2, bp2) if hp % 2 == 0 else (p3, bp3)
                    S.emit("pe", (lambda hp=hp, pg_=pg_: (lambda e: e.matmul(pg_[:, 0:TT], t_Psig[:], qb[:, hp, :],
                                                                             start=True, stop=True)))(),
                           reads=[b_QB, b_const], writes=[bpg_])
                    S.emit("dve", (lambda hp=hp, smap=smap, cst=cst, t0=t0: (lambda e: e.scalar_tensor_tensor(
                        tt1[:], qb[:, hp, :], smap, cst[:, t0:t0 + TT], ALU.mult, ALU.mult)))(),
                           reads=[b_QB, b_A4, b_qsm], writes=[b_T1])
                    S.emit("dve", (lambda ssap=ssap, snt=snt, t0=t0, pg_=pg_: (lambda e: e.scalar_tensor_tensor(
                        tu2[:], pg_[:, 0:TT], ssap, snt[:, t0:t0 + TT], ALU.mult, ALU.mult)))(),
                           reads=[bpg_, b_A4, b_qsm], writes=[b_U2])
                    S.emit("dve", lambda e: e.tensor_add(tt1[:], tt1[:], tu2[:]),
                           reads=[b_T1, b_U2], writes=[b_T1])
                    S.emit("pe", (lambda hp=hp, colb=colb, pv_=pv_: (lambda e: e.matmul(
                        pv_[:, 0:TT], t_Bhd[:, hp, :], t_inv[:, colb:colb + TT], start=True, stop=True)))(),
                           reads=[b_inv, b_const], writes=[bpv_])
                    if side == 0:
                        S.emit("dve", (lambda hp=hp, pv_=pv_: (lambda e: e.tensor_mul(qp[:, hp, :], tt1[:], pv_[:, 0:TT])))(),
                               reads=[b_T1, bpv_], writes=[b_XS, bqp[hp]])
                    else:
                        S.emit("dve", (lambda hp=hp, pv_=pv_: (lambda e: e.tensor_mul(kr[:, hp, 256:768], tt1[:], pv_[:, 0:TT])))(),
                               reads=[b_T1, bpv_], writes=wx([b_KR]))
            geo = []
            for gl in range(4):
                sub = 4 * tau + gl
                nchunk = min(sub + 1, 3)
                geo.append((gl, sub, nchunk, 128 * nchunk, (gl + 2 - (nchunk - 1)) * 128))

            def pctx(j):
                gl, sub, nchunk, W, ring0 = geo[j // nhp]
                hp = j % nhp
                m = j % 2
                pSx, bpSx = (pS2A, bpS2A) if m == 0 else (pS2B, bpS2B)
                return dict(gl=gl, sub=sub, nchunk=nchunk, W=W, ring0=ring0, hp=hp,
                            m=m, pSx=pSx, bpSx=bpSx,
                            mxs=t_mx[:, 2 * m:2 * m + 2],
                            b_mxs=b_mx0 if m == 0 else b_mx1,
                            b_dens=b_den0 if m == 0 else b_den1)

            def ps0(j):
                c = pctx(j)
                pSx, bpSx = c["pSx"], c["bpSx"]
                hp, gl, ring0, W, nchunk = c["hp"], c["gl"], c["ring0"], c["W"], c["nchunk"]
                fns = []
                for par in (0, 1):
                    off = par * 512
                    pb = par * 64
                    fns.append((lambda hp=hp, pb=pb, gl=gl, ring0=ring0, W=W, off=off, pSx=pSx:
                                (lambda e: e.matmul(
                                    pSx[:, off:off + W], qp[pb:pb + 64, hp, gl * 128:(gl + 1) * 128],
                                    kr[pb:pb + 64, hp, ring0:ring0 + W], start=True, stop=False)))())
                    if nchunk == 3:
                        fns.append((lambda off=off, pSx=pSx: (lambda e: e.matmul(
                            pSx[:, off:off + 128], t_I128[:], t_mTup[:, 0, :], start=False, stop=False)))())
                        fns.append((lambda off=off, pSx=pSx: (lambda e: e.matmul(
                            pSx[:, off + 256:off + 384], t_I128[:], t_mTup[:, 1, :], start=False, stop=True)))())
                    else:
                        fns.append((lambda off=off, W=W, pSx=pSx: (lambda e: e.matmul(
                            pSx[:, off + W - 128:off + W], t_I128[:], t_mTup[:, 1, :], start=False, stop=True)))())
                S.emit_group("pe", fns, reads=[bqp[hp], b_KR, b_const], writes=[bpSx])

            def ps1(j):
                c = pctx(j)
                pSx, W = c["pSx"], c["W"]
                pairview = pSx[:][:, 0:1024].rearrange("p (a b) -> p a b", a=2, b=512)[:, :, 0:W]
                S.emit("dve", (lambda pv=pairview, mxs=c["mxs"]: (lambda e: e.reduce_max(
                    mxs, pv, axis=AX.X, negate=True)))(),
                       reads=[c["bpSx"]], writes=[c["b_mxs"]])

            def ps23(j):
                c = pctx(j)
                pSx, bpSx, W, hp = c["pSx"], c["bpSx"], c["W"], c["hp"]
                for par in (0, 1):
                    h = 2 * hp + par
                    pt_, bpt_ = (ptA, b_ptA) if par == 0 else (ptB, b_ptB)
                    S.emit("act", (lambda h=h, W=W, off=par * 512, pSx=pSx, pt_=pt_,
                                   mx_=c["mxs"][:, par:par + 1]: (lambda e: e.activation(
                        pt_[:, 0:W], pSx[:, off:off + W], AF.Exp, bias=mx_,
                        accum_out=t_den[:, h:h + 1])))(),
                           reads=[bpSx, c["b_mxs"]], writes=[bpt_, c["b_dens"]])
                S.emit("dve", (lambda hp=hp: (lambda e: e.reciprocal(
                    t_den[:, 2 * hp:2 * hp + 2], t_den[:, 2 * hp:2 * hp + 2])))(),
                       reads=[c["b_dens"]], writes=[c["b_dens"]])
                for par in (0, 1):
                    h = 2 * hp + par
                    pt_, bpt_ = (ptA, b_ptA) if par == 0 else (ptB, b_ptB)
                    S.emit("dve", (lambda h=h, W=W, pt_=pt_: (lambda e: e.tensor_single_scalar(
                        pt_[:, 0:W], pt_[:, 0:W], t_den[:, h:h + 1], ALU.mult)))(),
                           reads=[bpt_, c["b_dens"]], writes=[bpt_])

            def ps47(j):
                c = pctx(j)
                W, nchunk, sub, hp, gl = c["W"], c["nchunk"], c["sub"], c["hp"], c["gl"]
                for par in (0, 1):
                    pt_, bpt_ = (ptA, b_ptA) if par == 0 else (ptB, b_ptB)
                    pTb_, bpTb_ = (p2bf, bp2) if par == 0 else (p3bf, bp3)
                    fns = []
                    for cc in range(nchunk):
                        fns.append((lambda cc=cc, pt_=pt_, pTb_=pTb_: (lambda e: e.transpose(
                            pTb_[:, cc * 128:(cc + 1) * 128], pt_[:, cc * 128:(cc + 1) * 128], t_I128[:])))())
                    S.emit_group("pe", fns, reads=[bpt_, b_const], writes=[bpTb_])
                    ptT_, bptT_ = (ptTA, b_ptTA) if par == 0 else (ptTB, b_ptTB)
                    eng = "act" if par == 0 else "dve"
                    if eng == "act":
                        S.emit("act", (lambda W=W, ptT_=ptT_, pTb_=pTb_: (lambda e: e.activation(
                            ptT_[:, :, :].rearrange("p a b -> p (a b)")[:, 0:W], pTb_[:, 0:W], AF.Copy)))(),
                               reads=[bpTb_], writes=[bptT_])
                    else:
                        S.emit("dve", (lambda W=W, ptT_=ptT_, pTb_=pTb_: (lambda e: e.tensor_copy(
                            ptT_[:, :, :].rearrange("p a b -> p (a b)")[:, 0:W], pTb_[:, 0:W])))(),
                               reads=[bpTb_], writes=[bptT_])
                for par in (0, 1):
                    h = 2 * hp + par
                    ptT_, bptT_ = (ptTA, b_ptTA) if par == 0 else (ptTB, b_ptTB)
                    fns = []
                    for cc in range(nchunk):
                        gslot = (sub - (nchunk - 1) + cc) % 6
                        fns.append((lambda h=h, cc=cc, gslot=gslot, par=par, nchunk=nchunk, ptT_=ptT_:
                                    (lambda e: e.matmul(
                                        p0[par * 64:par * 64 + 64, 0:128], vr[:, gslot, h * 64:(h + 1) * 64],
                                        ptT_[:, cc, :], start=(cc == 0), stop=(cc == nchunk - 1),
                                        tile_position=(0, par * 64))))())
                    S.emit_group("pe", fns, reads=[b_VR, bptT_], writes=[bp0])
                S.emit("act", (lambda hp=hp, gl=gl: (lambda e: e.activation(
                    ao[:, hp, gl * 128:(gl + 1) * 128], p0[:, 0:128], AF.Copy)))(),
                       reads=[bp0], writes=wx([b_AO]))

            NJ = 4 * nhp
            for j in range(NJ + 2):
                if j < NJ:
                    ps0(j)
                if 0 <= j - 2 < NJ:
                    ps47(j - 2)
                if 0 <= j - 1 < NJ:
                    ps23(j - 1)
                if j < NJ:
                    ps1(j)
            for m in range(ncnk):
                fns = []
                for c in range(ncnk):
                    fns.append((lambda m=m, c=c: (lambda e: e.matmul(
                        p1[:, 0:TT], wo[:, c, m * 128:(m + 1) * 128], ao[:, c, :],
                        start=(c == 0), stop=(c == ncnk - 1))))())
                S.emit_group("pe", fns, reads=[b_AO, b_wo], writes=[bp1])
                S.emit("dve", (lambda m=m, t0=t0: (lambda e: e.tensor_add(
                    y[:, m, 1 + t0:1 + t0 + TT], p1[:, 0:TT], y[:, m, 1 + t0:1 + t0 + TT])))(),
                       reads=[bp1] + YBUFS, writes=YBUFS)

    def ffn_sublayer(seq, li, P):
        p0, bp0, p1, bp1, p2, bp2, p3, bp3 = (
            P["p0"], P["bp0"], P["p1"], P["bp1"], P["p2"], P["bp2"], P["p3"], P["bp3"])
        rstdv, murstdv = P["rstdv"], P["murstdv"]
        bpS2A, bpS2B = P["bpS2A"], P["bpS2B"]
        g_ap = lambda c: t_gh[:, li, c:c + 1]
        bias_ap = lambda c: t_bh[:, li, c:c + 1]
        b2_ap = lambda c: t_b2[:, li, c:c + 1]
        XNB = [b_KR, b_VR, b_AO]

        def hv(kk):
            return hv_lo[:, kk, :] if kk < 8 else hv_hi[:, kk - 8, :]
        HVB = [b_XS, b_QB]

        for tau in range(ntt):
            t0 = tau * TT
            S.emit("dve", (lambda t0=t0: (lambda e: e.tensor_copy(xs[:, 0:4, :], y[:, 0:4, 1 + t0:1 + t0 + TT])))(),
                   reads=YBUFS, writes=[b_XS])
            S.emit("dve", (lambda t0=t0: (lambda e: e.tensor_copy(xs[:, 4:8, :], y[:, 4:8, t0:t0 + TT])))(),
                   reads=YBUFS, writes=[b_XS])
            ln_stats(xs, [b_XS], sq, b_SQ, stv, b_QB, stb, b_STB, p3, bp3, rstdv, bpS2A, murstdv, bpS2B)
            for c in range(ncnk):
                off = 1 + t0 if c < 4 else t0
                S.emit("dve", (lambda c=c, off=off: (lambda e: e.scalar_tensor_tensor(
                    v1[:], y[:, c, off:off + TT], g_ap(c), rstdv[:, 0:TT], ALU.mult, ALU.mult)))(),
                       reads=YBUFS + [bpS2A, b_gh], writes=[b_QB])
                S.emit("dve", (lambda c=c: (lambda e: e.scalar_tensor_tensor(
                    v1[:], murstdv[:, 0:TT], g_ap(c), v1[:], ALU.mult, ALU.add)))(),
                       reads=[bpS2B, b_QB, b_gh], writes=[b_QB])
                S.emit("act", (lambda c=c, t0=t0: (lambda e: e.activation(
                    xnf[:, c, t0:t0 + TT], v1[:], AF.Identity, bias=bias_ap(c))))(),
                       reads=[b_QB, b_bh], writes=XNB + [B_XNF[tau]])
        for half in range(2):
            S.emit("gp", (lambda half=half: (lambda e: e.dma_start(
                out=w1v[:, :, :], in_=dw1[li, :, half * 2816:(half + 1) * 2816].rearrange("(c p) n -> p c n", p=128))))(),
                   writes=W1BUFS, dma=True)
            S.emit("gp", (lambda half=half: (lambda e: e.dma_start(
                out=w2v[:, :, :], in_=dw2[li, half * 1408:(half + 1) * 1408, :].rearrange("(c p) n -> p c n", p=128))))(),
                   writes=W2BUFS, dma=True)
            for tau in range(ntt):
                t0 = tau * TT
                for mm in range(11):
                    fns = []
                    for c in range(ncnk):
                        fns.append((lambda mm=mm, c=c, t0=t0: (lambda e: e.matmul(
                            p1[:, 0:TT], w1v[:, c, 1408 + mm * 128:1408 + (mm + 1) * 128],
                            xnf[:, c, t0:t0 + TT], start=(c == 0), stop=(c == ncnk - 1))))())
                    S.emit_group("pe", fns, reads=[B_XNF[tau]] + W1BUFS, writes=[bp1])
                    S.emit("act", (lambda mm=mm, half=half: (lambda e: e.activation(
                        gel[:, :], p1[:, 0:TT], AF.Gelu, bias=t_b1g[:, li, half * 11 + mm:half * 11 + mm + 1])))(),
                           reads=[bp1, b_b1g], writes=[b_QB])
                    fns = []
                    for c in range(ncnk):
                        fns.append((lambda mm=mm, c=c, t0=t0: (lambda e: e.matmul(
                            p2[:, 0:TT], w1v[:, c, mm * 128:(mm + 1) * 128],
                            xnf[:, c, t0:t0 + TT], start=(c == 0), stop=(c == ncnk - 1))))())
                    S.emit_group("pe", fns, reads=[B_XNF[tau]] + W1BUFS, writes=[bp2])
                    S.emit("dve", (lambda mm=mm, half=half: (lambda e: e.scalar_tensor_tensor(
                        hv(mm), p2[:, 0:TT], t_b1a[:, li, half * 11 + mm:half * 11 + mm + 1],
                        gel[:, :], ALU.add, ALU.mult)))(),
                           reads=[bp2, b_QB, b_b1a], writes=HVB)
                for m in range(ncnk):
                    fns = []
                    for kk in range(11):
                        fns.append((lambda m=m, kk=kk: (lambda e: e.matmul(
                            p0[:, 0:TT], w2v[:, kk, m * 128:(m + 1) * 128], hv(kk),
                            start=(kk == 0), stop=(kk == 10))))())
                    S.emit_group("pe", fns, reads=HVB + W2BUFS, writes=[bp0])
                    if half == 0:
                        S.emit("dve", (lambda m=m, t0=t0: (lambda e: e.tensor_add(
                            y[:, m, 1 + t0:1 + t0 + TT], p0[:, 0:TT], y[:, m, 1 + t0:1 + t0 + TT])))(),
                               reads=[bp0] + YBUFS, writes=YBUFS)
                    else:
                        S.emit("dve", (lambda m=m, t0=t0: (lambda e: e.scalar_tensor_tensor(
                            y[:, m, 1 + t0:1 + t0 + TT], p0[:, 0:TT], b2_ap(m),
                            y[:, m, 1 + t0:1 + t0 + TT], ALU.add, ALU.add)))(),
                               reads=[bp0, b_b2] + YBUFS, writes=YBUFS)

    def phase_outer(seq):
        pes = ExitStack()
        psum = lambda name, shape, dt=F32: pes.enter_context(nc.psum_tensor(name + f"o{seq}", shape, dt))
        pS = psum("pS", [128, 2048]); bpS = Buf("pS")
        pA = psum("pA", [128, 512]); bpA = Buf("pA")
        pB = psum("pB", [128, 512]); bpB = Buf("pB")
        pC = psum("pC", [128, 512]); bpC = Buf("pC")
        pT = psum("pTo", [128, 1024], BF16); bpT = Buf("pTo")

        for tens, bf, dram in ((wq, b_wq, dwq2), (wk, b_wk, dwk2), (wv, b_wv, dwv2)):
            S.emit("gp", (lambda tens=tens, dram=dram: (lambda e: e.dma_start(
                out=tens[:, :, :], in_=dram.rearrange("(c p) n -> p c n", p=128))))(),
                   writes=[bf], dma=True)

        def vslot(s):
            return (o_vfa[:, s, :], [b_VR, b_AO, b_ptA, b_ptB, b_ptTA, b_ptTB, b_T1, b_U2, b_SQ]) if s < 11 \
                else (o_vfb[:, s - 11, :], [b_A4])

        OSCR = [b_VR, b_AO]

        for tau in range(ntt):
            t0 = tau * TT
            for cg in range(3):
                n = min(3, ncnk - cg * 3)
                S.emit("sp", (lambda t0=t0, cg=cg, n=n: (lambda e: e.dma_start(
                    out=o_xf[:, 0:n, :],
                    in_=dx1[seq, cg * 3:cg * 3 + n, :, t0:t0 + TT].rearrange("c p n -> p c n"))))(),
                       reads=[b_dx1], writes=wx(OSCR), dma=True)
                for j in range(n):
                    c = cg * 3 + j
                    S.emit("dve", (lambda c=c, j=j, t0=t0: (lambda e: e.tensor_copy(
                        o_xnf[:, c, t0:t0 + TT], o_xf[:, j, :])))(),
                           reads=OSCR, writes=wx(O_XNF2))
            ln_stats(o_xnf[:, :, t0:t0 + TT], O_XNF2, o_sq, b_SQ, o_stv, b_AO, o_stb, b_ptA,
                     pA, bpA, pB, bpB, pC, bpC)
            for cg in range(3):
                n = min(3, ncnk - cg * 3)
                S.emit("sp", (lambda t0=t0, cg=cg, n=n: (lambda e: e.dma_start(
                    out=o_xf[:, 0:n, :],
                    in_=dx1[seq, cg * 3:cg * 3 + n, :, t0:t0 + TT].rearrange("c p n -> p c n"))))(),
                       reads=[b_dx1], writes=wx(OSCR), dma=True)
                for j in range(n):
                    c = cg * 3 + j
                    S.emit("dve", (lambda c=c, j=j: (lambda e: e.tensor_mul(o_v1[:], o_xf[:, j, :], pB[:, 0:TT])))(),
                           reads=OSCR + [bpB], writes=wx(OSCR))
                    S.emit("dve", (lambda c=c, t0=t0: (lambda e: e.tensor_add(
                        o_xnf[:, c, t0:t0 + TT], o_v1[:], pC[:, 0:TT])))(),
                           reads=OSCR + [bpC], writes=wx(O_XNF2))
        for tau in range(ntt):
            t0 = tau * TT
            for side in range(2):
                wt, bwt = (wq, b_wq) if side == 0 else (wk, b_wk)
                dst, bdst = (o_qf, b_QF) if side == 0 else (o_kf, b_KF)
                pb_, bpb_ = (pA, bpA) if side == 0 else (pB, bpB)
                for hp in range(nhp):
                    fns = []
                    for c in range(ncnk):
                        fns.append((lambda c=c, hp=hp, wt=wt, t0=t0, pb_=pb_: (lambda e: e.matmul(
                            pb_[:, 0:TT], wt[:, c, hp * 128:(hp + 1) * 128], o_xnf[:, c, t0:t0 + TT],
                            start=(c == 0), stop=(c == ncnk - 1))))())
                    S.emit_group("pe", fns, reads=O_XNF2 + [bwt], writes=[bpb_])
                    S.emit("act", (lambda dst=dst, hp=hp, t0=t0, pb_=pb_: (lambda e: e.activation(
                        dst[:, hp, t0:t0 + TT], pb_[:, 0:TT], AF.Copy)))(),
                           reads=[bpb_], writes=[bdst])
            for g in range(4):
                gg = tau * 4 + g
                vv_, bvv_ = vslot(gg)
                for nh in range(2):
                    fns = []
                    for c in range(ncnk):
                        fns.append((lambda c=c, g=g, nh=nh, t0=t0: (lambda e: e.matmul(
                            pC[:, 0:512], o_xnf[:, c, t0 + g * 128:t0 + (g + 1) * 128],
                            wv[:, c, nh * 512:(nh + 1) * 512], start=(c == 0), stop=(c == ncnk - 1))))())
                    S.emit_group("pe", fns, reads=O_XNF2 + [b_wv], writes=[bpC])
                    S.emit("act", (lambda vv_=vv_, nh=nh: (lambda e: e.activation(
                        vv_[:, nh * 512:(nh + 1) * 512], pC[:, 0:512], AF.Copy)))(),
                           reads=[bpC], writes=wx(bvv_))
        S.emit("gp", lambda e: e.dma_start(out=wo[:, :, :], in_=dwpj.rearrange("(c p) n -> p c n", p=128)),
               writes=[b_wo], dma=True)
        pj_x = view(AR3, 0, 1536, [128, 3, TT])
        pj_v = view(AR3, 1536, 512, [128, TT])
        PJB = [b_XS]
        for sub in range(cfg.nsub):
            kend = (sub + 1) * 128
            for h in range(nheads):
                hp, par = h // 2, h % 2
                pb = par * 64
                fns = []
                nblk = (kend + 511) // 512
                for bI in range(nblk):
                    c0 = bI * 512
                    w = min(512, kend - c0)
                    last = (bI == nblk - 1)
                    fns.append((lambda hp=hp, pb=pb, sub=sub, c0=c0, w=w, last=last: (lambda e: e.matmul(
                        pS[:, c0:c0 + w], o_qf[pb:pb + 64, hp, sub * 128:(sub + 1) * 128],
                        o_kf[pb:pb + 64, hp, c0:c0 + w], start=True, stop=not last)))())
                fns.append((lambda kend=kend: (lambda e: e.matmul(
                    pS[:, kend - 128:kend], t_I128[:], t_mTup[:, 1, :], start=False, stop=True)))())
                S.emit_group("pe", fns, reads=[b_QF, b_KF, b_const], writes=[bpS])
                S.emit("dve", (lambda kend=kend: (lambda e: e.reduce_max(t_mx[:, 0:1], pS[:, 0:kend], axis=AX.X)))(),
                       reads=[bpS], writes=[b_mx])
                S.emit("dve", lambda e: e.tensor_scalar_mul(t_bias[:], t_mx[:, 0:1], -1.0 / 32.0),
                       reads=[b_mx], writes=[b_bias])
                S.emit("act", (lambda h=h, kend=kend: (lambda e: e.activation(
                    o_ptf[:, 0:kend], pS[:, 0:kend], AF.Exp, bias=t_bias[:, 0:1], scale=1.0 / 32.0,
                    accum_out=t_den[:, h:h + 1])))(),
                       reads=[bpS, b_bias], writes=[b_PTF, b_den])
                S.emit("dve", (lambda h=h: (lambda e: e.reciprocal(t_den[:, h:h + 1], t_den[:, h:h + 1])))(),
                       reads=[b_den], writes=[b_den])
                S.emit("dve", (lambda h=h, kend=kend: (lambda e: e.tensor_single_scalar(
                    o_ptf[:, 0:kend], o_ptf[:, 0:kend], t_den[:, h:h + 1], ALU.mult)))(),
                       reads=[b_PTF, b_den], writes=[b_PTF])
                nkc = sub + 1
                for c4 in range(0, nkc, 4):
                    cnt = min(4, nkc - c4)
                    fns = []
                    for c in range(cnt):
                        fns.append((lambda c4=c4, c=c: (lambda e: e.transpose(
                            pT[:, c * 128:(c + 1) * 128], o_ptf[:, (c4 + c) * 128:(c4 + c + 1) * 128],
                            t_I128[:])))())
                    S.emit_group("pe", fns, reads=[b_PTF, b_const], writes=[bpT])
                    S.emit("dve", (lambda cnt=cnt: (lambda e: e.tensor_copy(
                        o_ptT[:, :, :].rearrange("p a b -> p (a b)")[:, 0:cnt * 128], pT[:, 0:cnt * 128])))(),
                           reads=[bpT], writes=[b_QB])
                    fns = []
                    rds = [b_QB]
                    for c in range(cnt):
                        vv_, bvv_ = vslot(c4 + c)
                        rds = rds + [bb for bb in bvv_ if bb not in rds]
                        fns.append((lambda c4=c4, c=c, h=h, par=par, sub=sub, vv_=vv_: (lambda e: e.matmul(
                            pC[par * 64:par * 64 + 64, 0:128], vv_[:, h * 64:(h + 1) * 64],
                            o_ptT[:, c, :], start=(c4 + c == 0), stop=(c4 + c == sub),
                            tile_position=(0, par * 64))))())
                    S.emit_group("pe", fns, reads=rds, writes=[bpC])
                if par == 1:
                    S.emit("act", (lambda hp=hp, sub=sub: (lambda e: e.activation(
                        o_aot[:, hp, :], pC[:, 0:128], AF.Copy)))(),
                           reads=[bpC], writes=[b_AOT])
            t0 = sub * 128
            for mg in range(3):
                n = min(3, ncnk - mg * 3)
                S.emit("sp", (lambda t0=t0, mg=mg, n=n: (lambda e: e.dma_start(
                    out=pj_x[:, 0:n, 0:128],
                    in_=dx1[seq, mg * 3:mg * 3 + n, :, t0:t0 + 128].rearrange("c p n -> p c n"))))(),
                       reads=[b_dx1], writes=PJB, dma=True)
                for j in range(n):
                    m = mg * 3 + j
                    fns = []
                    for c in range(ncnk):
                        fns.append((lambda m=m, c=c: (lambda e: e.matmul(
                            pA[:, 0:128], wo[:, c, m * 128:(m + 1) * 128], o_aot[:, c, :],
                            start=(c == 0), stop=(c == ncnk - 1))))())
                    S.emit_group("pe", fns, reads=[b_AOT, b_wo], writes=[bpA])
                    S.emit("dve", (lambda m=m, j=j: (lambda e: e.scalar_tensor_tensor(
                        pj_v[:, 0:128], pA[:, 0:128], t_bpj[:, m:m + 1], pj_x[:, j, 0:128],
                        ALU.add, ALU.add)))(),
                           reads=[bpA, b_bpj] + PJB, writes=PJB)
                    S.emit("sp", (lambda t0=t0, m=m: (lambda e: e.dma_start(
                        out=dx2[seq, m, :, t0:t0 + 128], in_=pj_v[:, 0:128])))(),
                           reads=PJB, writes=[b_dx2], dma=True)
        pes.close()

    for seq in range(NSEQ):
        ph = cfg.phases
        if ph == "faf":
            phase_F(seq, dx, dx1)
            phase_outer(seq)
            phase_F(seq, dx2, dout)
        elif ph == "f":
            phase_F(seq, dx, dout)
        elif ph == "fa":
            phase_F(seq, dx, dx1)
            phase_outer(seq)
            for tau in range(ntt):
                t0 = tau * TT
                for cg in range(3):
                    n = min(3, ncnk - cg * 3)
                    S.emit("sp", (lambda t0=t0, cg=cg, n=n: (lambda e: e.dma_start(
                        out=xf[:, 0:n, :],
                        in_=dx2[seq, cg * 3:cg * 3 + n, :, t0:t0 + TT].rearrange("c p n -> p c n"))))(),
                           reads=[b_dx2], writes=wx([b_KR]), dma=True)
                    S.emit("sp", (lambda t0=t0, cg=cg, n=n: (lambda e: e.dma_start(
                        out=dout[seq, cg * 3:cg * 3 + n, :, t0:t0 + TT].rearrange("c p n -> p c n"),
                        in_=xf[:, 0:n, :])))(),
                           reads=[b_KR], dma=True)

    S.wait_all_done()
    with nc.Block() as block:
        S.flush(block)
    return S



def prep_inputs(inputs, n_cores=8):
    x = np.asarray(inputs["L0_attn_x"])
    B, T, D = x.shape
    wq = np.asarray(inputs["L0_attn_wq"]); wk = np.asarray(inputs["L0_attn_wk"])
    wv = np.asarray(inputs["L0_attn_wv"])
    wpj = np.asarray(inputs["L0_attn_w_proj"]); bpj = np.asarray(inputs["L0_attn_b_proj"])
    Wqkv = np.asarray(inputs["L0_ffn_Wqkv"])
    qs = np.asarray(inputs["L0_ffn_q_scale"]); ks = np.asarray(inputs["L0_ffn_k_scale"])
    Wo = np.asarray(inputs["L0_ffn_Wo"])
    g = np.asarray(inputs["L0_ffn_ln_g"]); bb = np.asarray(inputs["L0_ffn_ln_b"])
    W1 = np.asarray(inputs["L0_ffn_W1"]); b1 = np.asarray(inputs["L0_ffn_b1"])
    W2 = np.asarray(inputs["L0_ffn_W2"]); b2 = np.asarray(inputs["L0_ffn_b2"])
    DEP = Wqkv.shape[0]
    DH, PAD, H2 = 2730, 86, 1408

    a, gate = W1[:, :, :DH], W1[:, :, DH:]
    z = np.zeros((DEP, D, PAD), W1.dtype)
    w1p = np.concatenate([a[:, :, :H2], gate[:, :, :H2], a[:, :, H2:], z,
                          gate[:, :, H2:], z], axis=2)
    b1a, b1g = b1[:, :DH], b1[:, DH:]
    zb = np.zeros((DEP, PAD), b1.dtype)
    b1ap = np.concatenate([b1a[:, :H2], b1a[:, H2:], zb], axis=1)
    b1gp = np.concatenate([b1g[:, :H2], b1g[:, H2:], zb], axis=1)
    zw = np.zeros((DEP, PAD, D), W2.dtype)
    w2p = np.concatenate([W2[:, :H2, :], W2[:, H2:, :], zw], axis=1)

    def cols_h(v):
        n = v.shape[1] // 128
        return np.ascontiguousarray(v.reshape(DEP, n, 128).transpose(2, 0, 1))

    p = np.arange(128)
    shared = dict(
        wqkv=np.ascontiguousarray(Wqkv), wo=np.ascontiguousarray(Wo),
        w1p=np.ascontiguousarray(w1p), w2p=np.ascontiguousarray(w2p),
        g_h=cols_h(g), b_h=cols_h(bb), b1a_h=cols_h(b1ap), b1g_h=cols_h(b1gp),
        b2_h=cols_h(b2),
        qsm=np.ascontiguousarray(qs[:, p % 64].T), qss=np.ascontiguousarray(qs[:, (p + 32) % 64].T),
        ksm=np.ascontiguousarray(ks[:, p % 64].T), kss=np.ascontiguousarray(ks[:, (p + 32) % 64].T),
        wq2=np.ascontiguousarray(wq.transpose(1, 0, 2).reshape(D, D)),
        wk2=np.ascontiguousarray(wk.transpose(1, 0, 2).reshape(D, D)),
        wv2=np.ascontiguousarray(wv.transpose(1, 0, 2).reshape(D, D)),
        wpj=np.ascontiguousarray(wpj),
        bpj_h=np.ascontiguousarray(bpj.reshape(8, 128).T))
    per = B // n_cores
    in_maps = []
    for c in range(n_cores):
        xs_ = x[per * c:per * (c + 1)]
        m = dict(shared)
        m["x"] = np.ascontiguousarray(xs_.transpose(0, 2, 1).reshape(per, 8, 128, T))
        in_maps.append(m)
    return in_maps, (B, T, D)


def kernel(**inputs):
    in_maps, (B, T, D) = prep_inputs(inputs)
    nc = bass.Bass("TRN2", target_bir_lowering=False)
    cfg = Cfg(T=T)
    build(nc, cfg)
    res = run_bass_kernel_spmd(nc, in_maps, core_ids=list(range(8)))
    per = B // 8
    outs = []
    for c in range(8):
        o = np.asarray(res.results[c]["out"]).reshape(per, D, T).transpose(0, 2, 1)
        outs.append(o)
    return np.concatenate(outs, axis=0).astype(np.float32)
